# version 02
# baseline (speedup 1.0000x reference)
import contextlib
import numpy as np
import ml_dtypes
import concourse.bass as bass
import concourse.mybir as mybir
from concourse.bass_utils import run_bass_kernel_spmd

F32 = mybir.dt.float32
BF16 = mybir.dt.bfloat16
AF = mybir.ActivationFunctionType
ALU = mybir.AluOpType
AX = mybir.AxisListType

NCORES = 8
T, D = 2048, 1024
NH, DH = 32, 32
NT = T // 128
NQT = T // 512
KT = T // 128
NPAIR = 16
SCALE = float(DH) ** -0.5
EPS1, EPS2 = 1e-5, 1e-6
BIG = 1e30


def build_nc(ph=6):
    nc = bass.Bass("TRN2", target_bir_lowering=False)
    dram = lambda name, shape, dtype=F32, kind="ExternalInput": nc.dram_tensor(
        name, shape, dtype, kind=kind).ap()
    x_d = dram("x", [T, D])
    wqkv_d = dram("wqkv", [D, 3 * D])
    wlinT_d = dram("wlinT", [D, D])
    wout_d = dram("wout_h", [NPAIR * 128, D])
    gwT_d = dram("gwT", [D, 8])
    leftf_d = dram("leftf", [D, 32])
    rightp_d = dram("rightp", [128, D])
    blin_d = dram("blin_rep", [128, D])
    bout_d = dram("bout_rep", [128, D])
    gam_d = dram("gam_rep", [128, D])
    bet_d = dram("bet_rep", [128, D])
    ident_d = dram("ident", [128, 128])
    e4_d = dram("e4", [128, 4])
    g8_d = dram("g8pad", [128, 128])
    sel_d = dram("sel", [128, NPAIR * 32])
    ep_d = dram("epair", [32, NPAIR * 128])
    ones1_d = dram("ones1", [1, 128])
    eps_d = dram("eps_rep", [128, 2])
    out_d = dram("out", [T, D], F32, kind="ExternalOutput")
    pre_d = dram("pre_scr", [T, D], BF16, kind="Internal")
    g_d = dram("g_scr", [T, D], BF16, kind="Internal")
    bnd_d = dram("bnd_scr", [4, 64], F32, kind="Internal")
    nb_d = dram("negB_scr", [128, 1], F32, kind="Internal")

    ctx = contextlib.ExitStack()
    with ctx:
        sb = lambda name, shape, dtype: ctx.enter_context(nc.sbuf_tensor(name, shape, dtype))
        psum = lambda name, shape, dtype=F32: ctx.enter_context(nc.psum_tensor(name, shape, dtype))

        q_sb = sb("q_sb", [128, 8 * T], BF16)
        k_sb = sb("k_sb", [128, 8 * T], BF16)
        v_sb = sb("v_sb", [128, KT * NH * 33], BF16)
        ident_sb = sb("ident_sb", [128, 128], BF16)
        sel_sb = sb("sel_sb", [128, NPAIR * 32], BF16)
        ep_sb = sb("ep_sb", [32, NPAIR * 128], BF16)
        e4_sb = sb("e4_sb", [128, 4], BF16)
        g8_sb = sb("g8_sb", [128, 128], BF16)
        st1 = sb("st1", [128, 16], F32)
        xst = sb("xst", [128, 2 * D], F32)
        scr = sb("scr", [128, D], F32)
        scr2 = sb("scr2", [128, D], F32)
        xn_tm = sb("xn_tm", [128, 2 * D], BF16)
        prest = xn_tm
        blin_sb = sb("blin_sb", [128, D], F32)
        biasB_sb = sb("biasB_sb", [128, 1], F32)
        eps_sb = sb("eps_sb", [128, 2], F32)
        mqk = scr2[0:1, 256:512]
        ones1_sb = scr2[0:1, 512:640]
        bnd = scr2[0:4, 640:720]

        ps_big = psum("ps_big", [128, 2048])
        ps_two = psum("ps_two", [128, 1024])
        ps_sm = psum("ps_sm", [128, 512])
        ps_tr = psum("ps_tr", [128, 256], BF16)

        sem = lambda name: ctx.enter_context(nc.semaphore(name))
        ld = sem("ld")
        lx = sem("lx")
        sv = sem("sv")
        sa = sem("sa")
        sm = sem("sm")
        s_sc = sem("ssc")
        s_xp = sem("sxp")
        s_av = sem("sav")
        s_ev = sem("sev")

        block = ctx.enter_context(nc.Block())

        G0_DMAS = 8 * 4 + 6
        G0_LD = 16 * G0_DMAS
        LDW = lambda dc: G0_LD + 16 * (dc + 1)
        LD_2A = G0_LD + 16 * 16
        LD_BND1 = LD_2A + 16
        LD_BND = LD_2A + 16 * 3
        LD_NB1 = LD_BND + 16
        LD_NB = LD_BND + 32
        LD_WOUT = lambda i: LD_NB + 16 * (i + 1)
        LD_TOT = LD_NB + 16 * NPAIR

        LX_CONST = 16 * 3
        LX_X1 = lambda t: LX_CONST + 16 * (t + 1)
        LX1 = LX_CONST + 16 * NT
        LX_PST = lambda t: LX1 + 16 * (t + 1)
        LX2 = LX1 + 16 * NT
        _pr = {t: (t if t < 3 else 2 * t - 2) for t in range(NT)}
        _ps = {t: (3 + 2 * t if t <= NT - 4 else NT + t) for t in range(NT)}
        LX_PLD = lambda t: LX2 + 16 * (_pr[t] + 1)
        LX_GST = lambda t: LX2 + 16 * (_ps[t] + 1)
        LX3 = LX2 + 16 * NT
        LX4 = LX2 + 16 * 2 * NT
        LX_EPI = lambda qt: LX4 + 16 * 12 * qt
        LX_EXG = lambda qt, ts: LX_EPI(qt) + 16 * (2 * ts + 1)
        LX_EGG = lambda qt, ts: LX_EPI(qt) + 16 * (2 * ts + 2)
        LX_EOUT = lambda qt, ts: LX_EPI(qt) + 16 * 8 + 16 * (ts + 1)
        LX_TOT = LX4 + 16 * 12 * NQT

        P1_SV, P1_SA, P1_SM = 12 * NT, 3 * NT, 8 * NT
        SV_2A = P1_SV + 12 * 16
        P2A_SVD = lambda dc: P1_SV + 12 * dc
        P2A_SM = lambda dc: P1_SM + 5 * dc
        SM_2A = P1_SM + 5 * 16
        SA_BF = P1_SA + 2
        SV_BF = SV_2A + 4
        P2B_SV = lambda t: SV_BF + 12 * t
        P2B_SM = lambda t: SM_2A + 2 * t
        P2B_SA = lambda t: SA_BF + t
        SV_2B, SM_2B, SA_2B = SV_BF + 12 * NT, SM_2A + 2 * NT, SA_BF + NT
        P2C_SV = lambda qt: SV_2B + 2 * qt
        P2C_SM = lambda qt: SM_2B + 2 * qt
        SV_2C, SM_2C = SV_2B + 2 * NQT, SM_2B + 2 * NQT
        P2D_SV = lambda t: SV_2C + 2 * t
        P2D_SM = lambda t: SM_2C + 2 * t
        SV_2D, SM_2D = SV_2C + 2 * NT, SM_2C + 2 * NT
        SA_SIL = lambda t: SA_2B + t + 1
        SA_3P = SA_2B + NT
        QT_SC = lambda qt: 256 * qt
        QT_AV = lambda qt: 256 * qt
        QT_XP = lambda qt: 256 * qt
        QT_EV = lambda qt: 16 * qt
        P3_SV = lambda qt: SV_2D + 89 * qt
        P3_SM = lambda qt: SM_2D + 25 * qt
        P3_SA = lambda qt: SA_3P + 12 * qt
        SV_TOT = SV_2D + 89 * NQT
        SM_TOT = SM_2D + 25 * NQT
        SA_TOT = SA_3P + 12 * NQT

        @block.gpsimd
        def _(g):
            n = 0
            for dst, src in [(ident_sb[:, :], ident_d), (e4_sb[:, :], e4_d),
                             (g8_sb[:, :], g8_d), (sel_sb[:, :], sel_d),
                             (ep_sb[0:32, :], ep_d)]:
                g.dma_start(out=dst, in_=src).then_inc(ld, 16)
                n += 1
            g.memset(v_sb[:, :], 1.0)
            assert n == 5

        @block.sync
        def _(s):
            for dst, src in [(blin_sb[:, :], blin_d), (ones1_sb[:, :], ones1_d),
                             (eps_sb[:, :], eps_d)]:
                s.dma_start(out=dst, in_=src).then_inc(lx, 16)
            for t in range(NT):
                if t >= 2:
                    s.wait_ge(sv, 12 * (t - 2) + 4)
                s.dma_start(out=xst[:, (t % 2) * D:(t % 2 + 1) * D],
                            in_=x_d[t * 128:(t + 1) * 128, :]).then_inc(lx, 16)

        ctx2 = ctx.enter_context(contextlib.ExitStack())
        sb2 = lambda name, shape, dtype: ctx2.enter_context(nc.sbuf_tensor(name, shape, dtype))
        xn_sb = sb2("xn_sb", [128, 8 * T], BF16)
        wv_sb = sb2("wv_sb", [128, 8 * D], BF16)
        wlin_sb = sb2("wlin_sb", [128, 8 * D], BF16)
        gwT_sb = sb2("gwT_sb", [128, 64], BF16)
        leftf_sb = sb2("leftf_sb", [128, 256], BF16)
        rightp_sb = sb2("rightp_sb", [128, D], BF16)
        wq_st = sb2("wq_st", [128, 2 * 1024], BF16)
        sqs = sb2("sqs", [128, 512], BF16)
        gates_bf = sb2("gates_bf", [128, NT * 8], BF16)
        gpad = sb2("gpad", [128, 128], BF16)
        gfm_sb = sb2("gfm_sb", [128, NT * 128], BF16)
        t_tmp = sb2("t_tmp", [32, 512], BF16)
        t1pad = sb2("t1pad", [128, T], BF16)

        @block.gpsimd
        def _(g):
            for c in range(8):
                g.dma_start(out=wv_sb[:, c * D:(c + 1) * D],
                            in_=wqkv_d[c * 128:(c + 1) * 128, 2 * D:3 * D]).then_inc(ld, 16)
                g.dma_start(out=wlin_sb[:, c * D:(c + 1) * D],
                            in_=wlinT_d[c * 128:(c + 1) * 128, :]).then_inc(ld, 16)
                g.dma_start(out=gwT_sb[:, c * 8:(c + 1) * 8],
                            in_=gwT_d[c * 128:(c + 1) * 128, :]).then_inc(ld, 16)
                g.dma_start(out=leftf_sb[:, c * 32:(c + 1) * 32],
                            in_=leftf_d[c * 128:(c + 1) * 128, :]).then_inc(ld, 16)
            g.dma_start(out=rightp_sb[:, :], in_=rightp_d).then_inc(ld, 16)
            g.memset(t1pad[:, :], 0.0)
            g.memset(gpad[:, :], 0.0)
            for dc in range(16 if ph >= 2 else 0):
                if dc >= 2:
                    g.wait_ge(sm, P2A_SM(dc - 1) + 1)
                g.dma_start(
                    out=wq_st[:, (dc % 2) * 1024:(dc % 2 + 1) * 1024].rearrange(
                        "p (c m) -> p c m", c=8),
                    in_=wqkv_d[:, dc * 128:(dc + 1) * 128].rearrange(
                        "(c p) m -> p c m", p=128)).then_inc(ld, 16)

        @block.vector
        def _(v):
            for t in range(NT):
                b = 12 * t
                xt = xst[:, (t % 2) * D:(t % 2 + 1) * D]
                v.wait_ge(lx, LX_X1(t))
                if t >= 2:
                    v.wait_ge(sm, 8 * (t - 2) + 8)
                v.tensor_scalar(xn_tm[:, (t % 2) * D:(t % 2 + 1) * D], xt, 1.0 / D,
                                0.0, op0=ALU.mult, op1=ALU.add,
                                accum_out=st1[:, 1:2]).then_inc(sv, 1)
                v.wait_ge(sa, 3 * t + 1)
                v.wait_ge(sv, b + 1)
                v.scalar_tensor_tensor(st1[:, 3:4], st1[:, 1:2], -1.0, st1[:, 1:2],
                                       op0=ALU.mult, op1=ALU.mult).then_inc(sv, 1)
                v.wait_ge(sv, b + 2)
                if t >= 1:
                    v.wait_ge(sa, 3 * (t - 1) + 2)
                v.tensor_scalar(st1[:, 5:6], st1[:, 2:3], 1.0 / D, st1[:, 3:4],
                                op0=ALU.mult, op1=ALU.add).then_inc(sv, 1)
                v.wait_ge(sa, 3 * t + 3)
                v.tensor_scalar(xn_tm[:, (t % 2) * D:(t % 2 + 1) * D], xt,
                                st1[:, 1:2], st1[:, 6:7],
                                op0=ALU.subtract, op1=ALU.mult).then_inc(sv, 1)
                for c in range(8):
                    v.wait_ge(sm, 8 * t + c + 1)
                    v.tensor_copy(xn_sb[:, c * T + t * 128:c * T + (t + 1) * 128],
                                  ps_tr[:, 0:128]).then_inc(sv, 1)

        @block.scalar
        def _(sc):
            for t in range(NT):
                xt = xst[:, (t % 2) * D:(t % 2 + 1) * D]
                sc.wait_ge(lx, LX_X1(t))
                if t >= 1:
                    sc.wait_ge(sv, 12 * (t - 1) + 3)
                sc.activation(scr[:, :], xt, AF.Square,
                              accum_out=st1[:, 2:3]).then_inc(sa, 1)
                sc.wait_ge(sv, 12 * t + 3)
                sc.activation(st1[:, 7:8], st1[:, 5:6], AF.Ln,
                              bias=eps_sb[:, 0:1], scale=1.0).then_inc(sa, 1)
                sc.wait_ge(sa, 3 * t + 2)
                if t >= 1:
                    sc.wait_ge(sv, 12 * (t - 1) + 4)
                sc.activation(st1[:, 6:7], st1[:, 7:8], AF.Exp,
                              bias=0.0, scale=-0.5).then_inc(sa, 1)

        @block.tensor
        def _(t_):
            t_.wait_ge(ld, 16 * 5)
            for t in range(NT):
                for c in range(8):
                    t_.wait_ge(sv, 12 * t + 4)
                    prev = 8 * t + c - 1
                    if prev >= 0:
                        pt, pc = prev // 8, prev % 8
                        t_.wait_ge(sv, 12 * pt + 4 + pc + 1)
                    t_.transpose(ps_tr[:, 0:128],
                                 xn_tm[:, (t % 2) * D + c * 128:(t % 2) * D + (c + 1) * 128],
                                 ident_sb[:, :]).then_inc(sm, 1)

        @block.tensor
        def _(t_):
            for dc in range(16 if ph >= 2 else 0):
                t_.wait_ge(ld, LDW(dc))
                if dc >= 1:
                    t_.wait_ge(sv, P2A_SVD(dc - 1) + 4)
                ins = None
                for c in range(8):
                    for qt in range(NQT):
                        ins = t_.matmul(
                            ps_big[:, qt * 512:(qt + 1) * 512],
                            wq_st[:, (dc % 2) * 1024 + c * 128:(dc % 2) * 1024 + (c + 1) * 128],
                            xn_sb[:, c * T + qt * 512:c * T + (qt + 1) * 512],
                            start=(c == 0), stop=(c == 7), skip_group_check=True)
                ins.then_inc(sm, 1)
                for qt in range(NQT):
                    t_.wait_ge(sv, P2A_SVD(dc) + 5 + 2 * qt)
                    t_.matmul(ps_sm[0:4, 0:512], e4_sb[:, :], sqs[:, :],
                              start=True, stop=True,
                              skip_group_check=True).then_inc(sm, 1)

        @block.vector
        def _(v):
            for dc in range(16 if ph >= 2 else 0):
                dst = q_sb if dc < 8 else k_sb
                cc = dc % 8
                base = P2A_SVD(dc)
                v.wait_ge(sm, P2A_SM(dc) + 1)
                for qt in range(NQT):
                    v.tensor_copy(dst[:, cc * T + qt * 512:cc * T + (qt + 1) * 512],
                                  ps_big[:, qt * 512:(qt + 1) * 512]).then_inc(sv, 1)
                for qt in range(NQT):
                    v.wait_ge(sv, base + 4)
                    v.wait_ge(sm, P2A_SM(dc) + 1 + qt)
                    src = dst[:, cc * T + qt * 512:cc * T + (qt + 1) * 512]
                    v.tensor_tensor(sqs[:, :], src, src, op=ALU.mult).then_inc(sv, 1)
                    v.wait_ge(sm, P2A_SM(dc) + 1 + qt + 1)
                    v.reduce_max(bnd[0:4, dc * 4 + qt:dc * 4 + qt + 1],
                                 ps_sm[0:4, 0:512], axis=AX.X).then_inc(sv, 1)

        @block.gpsimd
        def _(g):
            if ph < 2:
                return
            g.wait_ge(sv, SV_2A)
            g.dma_start(out=bnd_d, in_=bnd[0:4, 0:64]).then_inc(ld, 16)
            g.wait_ge(ld, LD_BND1)
            g.dma_start(out=mqk[:, 0:128].rearrange("a (p c) -> a p c", p=4),
                        in_=bnd_d[:, 0:32]).then_inc(ld, 16)
            g.dma_start(out=mqk[:, 128:256].rearrange("a (p c) -> a p c", p=4),
                        in_=bnd_d[:, 32:64]).then_inc(ld, 16)

        @block.vector
        def _(v):
            if ph < 2:
                return
            v.wait_ge(ld, LD_BND)
            v.reduce_max(st1[0:1, 8:9], mqk[:, 0:128], axis=AX.X)
            v.reduce_max(st1[0:1, 9:10], mqk[:, 128:256], axis=AX.X).then_inc(sv, 1)
            v.wait_ge(sv, SV_2A + 1)
            v.tensor_tensor(st1[0:1, 10:11], st1[0:1, 8:9], st1[0:1, 9:10],
                            op=ALU.mult).then_inc(sv, 1)
            v.wait_ge(sa, SA_BF)
            v.tensor_scalar(st1[0:1, 12:13], st1[0:1, 11:12], -SCALE * 1.02, None,
                            op0=ALU.mult).then_inc(sv, 1)
            v.wait_ge(sv, SV_2A + 3)
            v.tensor_scalar(mqk[:, 0:128], ones1_sb[0:1, :], st1[0:1, 12:13], None,
                            op0=ALU.mult).then_inc(sv, 1)

        @block.scalar
        def _(sc):
            if ph < 2:
                return
            sc.wait_ge(sv, SV_2A + 2)
            sc.activation(st1[0:1, 13:14], st1[0:1, 10:11], AF.Ln,
                          bias=0.0, scale=1.0).then_inc(sa, 1)
            sc.wait_ge(sa, P1_SA + 1)
            sc.activation(st1[0:1, 11:12], st1[0:1, 13:14], AF.Exp,
                          bias=0.0, scale=0.5).then_inc(sa, 1)

        @block.gpsimd
        def _(g):
            if ph < 2:
                return
            g.wait_ge(sv, SV_BF)
            g.dma_start(out=nb_d[:, :].rearrange("p a -> (p a)").rearrange(
                "(a q) -> a q", a=1), in_=mqk[:, 0:128]).then_inc(ld, 16)
            g.wait_ge(ld, LD_NB1)
            g.dma_start(out=biasB_sb[:, :], in_=nb_d).then_inc(ld, 16)

        @block.tensor
        def _(t_):
            for t in range(NT if ph >= 3 else 0):
                if t == 0:
                    t_.wait_ge(sv, SV_2A)
                if t >= 1:
                    t_.wait_ge(sv, P2B_SV(t - 1) + 7)
                    t_.wait_ge(sa, P2B_SA(t - 1) + 1)
                ins = None
                for c in range(8):
                    xnc = xn_sb[:, c * T + t * 128:c * T + (t + 1) * 128]
                    t_.matmul(ps_two[:, 0:512], xnc, wv_sb[:, c * D:c * D + 512],
                              start=(c == 0), stop=(c == 7), skip_group_check=True)
                    t_.matmul(ps_two[:, 512:1024], xnc, wv_sb[:, c * D + 512:(c + 1) * D],
                              start=(c == 0), stop=(c == 7), skip_group_check=True)
                    ins = t_.matmul(ps_sm[:, 0:8], xnc, gwT_sb[:, c * 8:(c + 1) * 8],
                                    start=(c == 0), stop=(c == 7), skip_group_check=True)
                ins.then_inc(sm, 1)
                t_.wait_ge(sv, P2B_SV(t) + 11)
                if t >= 1:
                    t_.wait_ge(sv, P2B_SV(t - 1) + 12)
                t_.transpose(ps_tr[:, 128:256], gpad[:, :], ident_sb[:, :]).then_inc(sm, 1)

        @block.scalar
        def _(sc):
            for t in range(NT if ph >= 3 else 0):
                sc.wait_ge(sv, P2B_SV(t) + 2)
                sc.activation(scr2[:, 0:8], ps_sm[:, 0:8], AF.Exp, bias=st1[:, 8:9],
                              accum_out=st1[:, 9:10]).then_inc(sa, 1)

        @block.vector
        def _(v):
            for t in range(NT if ph >= 3 else 0):
                b = P2B_SV(t)
                v.wait_ge(sm, P2B_SM(t) + 1)
                v.tensor_copy(
                    v_sb[:, t * NH * 33:(t + 1) * NH * 33].rearrange(
                        "p (h w) -> p h w", h=NH)[:, :, 0:32],
                    ps_two[:, :].rearrange("p (h w) -> p h w", h=NH)).then_inc(sv, 1)
                v.reduce_max(st1[:, 8:9], ps_sm[:, 0:8], axis=AX.X,
                             negate=True).then_inc(sv, 1)
                v.wait_ge(sv, b + 2)
                v.tensor_scalar(st1[:, 10:11], st1[:, 8:9], -1.0, None,
                                op0=ALU.mult).then_inc(sv, 1)
                v.wait_ge(sv, b + 3)
                v.tensor_scalar(scr2[:, 8:16], ps_sm[:, 0:8], st1[:, 10:11], None,
                                op0=ALU.is_ge).then_inc(sv, 1)
                v.wait_ge(sv, b + 4)
                v.scalar_tensor_tensor(scr2[:, 16:24], scr2[:, 8:16], -BIG,
                                       ps_sm[:, 0:8], op0=ALU.mult,
                                       op1=ALU.add).then_inc(sv, 1)
                v.wait_ge(sv, b + 5)
                v.reduce_max(st1[:, 11:12], scr2[:, 16:24], axis=AX.X).then_inc(sv, 1)
                v.wait_ge(sv, b + 6)
                v.tensor_scalar(scr2[:, 24:32], ps_sm[:, 0:8], st1[:, 11:12], None,
                                op0=ALU.is_ge).then_inc(sv, 1)
                v.wait_ge(sa, P2B_SA(t) + 1)
                with nc.allow_low_precision(reason="tiny gate sums"):
                    v.reciprocal(st1[:, 12:13], st1[:, 9:10]).then_inc(sv, 1)
                v.wait_ge(sv, b + 8)
                v.scalar_tensor_tensor(gates_bf[:, t * 8:(t + 1) * 8], scr2[:, 0:8],
                                       1.0, scr2[:, 24:32], op0=ALU.mult,
                                       op1=ALU.mult).then_inc(sv, 1)
                v.wait_ge(sv, b + 9)
                v.tensor_scalar(gates_bf[:, t * 8:(t + 1) * 8],
                                gates_bf[:, t * 8:(t + 1) * 8], st1[:, 12:13], None,
                                op0=ALU.mult).then_inc(sv, 1)
                v.wait_ge(sv, b + 10)
                if t >= 1:
                    v.wait_ge(sm, P2B_SM(t - 1) + 2)
                v.tensor_copy(gpad[:, 0:8],
                              gates_bf[:, t * 8:(t + 1) * 8]).then_inc(sv, 1)
                v.wait_ge(sm, P2B_SM(t) + 2)
                v.tensor_copy(gfm_sb[:, t * 128:(t + 1) * 128],
                              ps_tr[:, 128:256]).then_inc(sv, 1)

        @block.tensor
        def _(t_):
            for qt in range(NQT if ph >= 4 else 0):
                if qt == 0:
                    t_.wait_ge(sv, SV_2B)
                    t_.wait_ge(sa, SA_2B)
                else:
                    t_.wait_ge(sv, P2C_SV(qt - 1) + 1)
                ins = None
                for c in range(8):
                    ins = t_.matmul(ps_sm[0:32, :], leftf_sb[:, c * 32:(c + 1) * 32],
                                    xn_sb[:, c * T + qt * 512:c * T + (qt + 1) * 512],
                                    start=(c == 0), stop=(c == 7), skip_group_check=True)
                ins.then_inc(sm, 1)
                if qt >= 1:
                    t_.wait_ge(sv, P2C_SV(qt - 1) + 1)
                t_.matmul(ps_two[:, 0:512], g8_sb[:, :],
                          gfm_sb[:, qt * 512:(qt + 1) * 512],
                          start=True, stop=True, skip_group_check=True).then_inc(sm, 1)

        @block.vector
        def _(v):
            for qt in range(NQT if ph >= 4 else 0):
                b = P2C_SV(qt)
                v.wait_ge(sm, P2C_SM(qt) + 1)
                v.tensor_copy(t_tmp[0:32, :], ps_sm[0:32, :]).then_inc(sv, 1)
                v.wait_ge(sv, b + 1)
                v.wait_ge(sm, P2C_SM(qt) + 2)
                v.tensor_tensor(t1pad[0:32, qt * 512:(qt + 1) * 512],
                                ps_two[0:32, 0:512], t_tmp[0:32, :],
                                op=ALU.mult).then_inc(sv, 1)

        @block.tensor
        def _(t_):
            for t in range(NT if ph >= 4 else 0):
                t_.wait_ge(sv, P2C_SV(t // 4) + 2)
                if t >= 2:
                    t_.wait_ge(sv, P2D_SV(t - 2) + 2)
                for half in range(2):
                    hb = (t % 2) * 1024
                    ins = None
                    for c in range(8):
                        ins = t_.matmul(
                            ps_big[:, hb + half * 512:hb + (half + 1) * 512],
                            xn_sb[:, c * T + t * 128:c * T + (t + 1) * 128],
                            wlin_sb[:, c * D + half * 512:c * D + (half + 1) * 512],
                            start=(c == 0), stop=False, skip_group_check=True)
                    ins = t_.matmul(
                        ps_big[:, hb + half * 512:hb + (half + 1) * 512],
                        t1pad[:, t * 128:(t + 1) * 128],
                        rightp_sb[:, half * 512:(half + 1) * 512],
                        start=False, stop=True, skip_group_check=True)
                    ins.then_inc(sm, 1)

        @block.vector
        def _(v):
            for t in range(NT if ph >= 4 else 0):
                b = P2D_SV(t)
                hb = (t % 2) * 1024
                for half in range(2):
                    v.wait_ge(sm, P2D_SM(t) + half + 1)
                    if t >= 2:
                        v.wait_ge(lx, LX_PST(t - 2))
                    v.tensor_tensor(prest[:, (t % 2) * D + half * 512:(t % 2) * D + (half + 1) * 512],
                                    ps_big[:, hb + half * 512:hb + (half + 1) * 512],
                                    blin_sb[:, half * 512:(half + 1) * 512],
                                    op=ALU.add).then_inc(sv, 1)

        @block.sync
        def _(s):
            for t in range(NT if ph >= 4 else 0):
                s.wait_ge(sv, P2D_SV(t) + 2)
                s.dma_start(out=pre_d[t * 128:(t + 1) * 128, :],
                            in_=prest[:, (t % 2) * D:(t % 2 + 1) * D]).then_inc(lx, 16)

        ctx2.close()
        ctx3 = ctx.enter_context(contextlib.ExitStack())
        sb3 = lambda name, shape, dtype: ctx3.enter_context(nc.sbuf_tensor(name, shape, dtype))
        wout_sb = sb3("wout_sb", [128, NPAIR * D], BF16)
        attn = sb3("attn", [128, NPAIR * 512], BF16)
        expS = sb3("expS", [128, 2 * 1024], BF16)
        rpk = sb3("rpk", [32, 512], BF16)
        rb_bf = sb3("rb_bf", [128, 512], BF16)
        gst = sb3("gst", [128, 2 * D], BF16)
        spre = sb3("spre", [128, 2 * D], F32)
        bout_sb = sb3("bout_sb", [128, D], F32)
        gam_sb = sb3("gam_sb", [128, D], F32)
        bet_sb = sb3("bet_sb", [128, D], F32)

        @block.gpsimd
        def _(g):
            if ph >= 5:
                g.wait_ge(sm, SM_2D)
                g.wait_ge(sv, SV_2D)
            for p in range(NPAIR if ph >= 5 else 0):
                g.dma_start(out=wout_sb[:, p * D:(p + 1) * D],
                            in_=wout_d[p * 128:(p + 1) * 128, :]).then_inc(ld, 16)

        @block.sync
        def _(s):
            for dst, src in [(bout_sb[:, :], bout_d), (gam_sb[:, :], gam_d),
                             (bet_sb[:, :], bet_d)]:
                s.dma_start(out=dst, in_=src).then_inc(lx, 16) if False else None
            if ph >= 5.5:
                order = [("r", 0), ("r", 1), ("r", 2)]
                for k in range(NT - 3):
                    order += [("s", k), ("r", k + 3)]
                order += [("s", NT - 3), ("s", NT - 2), ("s", NT - 1)]
                for kind, t in order:
                    if kind == "r":
                        s.wait_ge(lx, LX_PST(NT - 1))
                        if t >= 2:
                            s.wait_ge(sa, SA_SIL(t - 2))
                        s.dma_start(out=prest[:, (t % 2) * D:(t % 2 + 1) * D],
                                    in_=pre_d[t * 128:(t + 1) * 128, :]).then_inc(lx, 16)
                    else:
                        s.wait_ge(sa, SA_SIL(t))
                        s.dma_start(out=g_d[t * 128:(t + 1) * 128, :],
                                    in_=gst[:, (t % 2) * D:(t % 2 + 1) * D]).then_inc(lx, 16)
            elif ph == 5:
                for t in range(NT):
                    s.wait_ge(lx, LX_PST(NT - 1))
                    s.dma_start(out=prest[:, (t % 2) * D:(t % 2 + 1) * D],
                                in_=pre_d[t * 128:(t + 1) * 128, :]).then_inc(lx, 16)

        @block.gpsimd
        def _(g):
            for dst, src in ([(bout_sb[:, :], bout_d), (gam_sb[:, :], gam_d),
                             (bet_sb[:, :], bet_d)] if ph >= 5 else []):
                g.dma_start(out=dst, in_=src).then_inc(ld, 16)
        LD_TOT2 = LD_TOT + 16 * 3

        @block.scalar
        def _(sc):
            for t in range(NT if ph >= 5.5 else 0):
                sc.wait_ge(lx, LX_PLD(t))
                if t >= 2:
                    sc.wait_ge(lx, LX_GST(t - 2))
                sc.activation(gst[:, (t % 2) * D:(t % 2 + 1) * D],
                              prest[:, (t % 2) * D:(t % 2 + 1) * D],
                              AF.Silu).then_inc(sa, 1)

        @block.tensor
        def _(t_):
            if ph >= 6:
                t_.wait_ge(ld, LD_TOT2)
            for qt in range(NQT if ph >= 6 else 0):
                scb, avb, xpb, evb = QT_SC(qt), QT_AV(qt), QT_XP(qt), QT_EV(qt)
                if qt > 0:
                    t_.wait_ge(sv, P3_SV(qt))
                for p in range(NPAIR):
                    cp, jp = p // 2, p % 2
                    sA, sB = 64 * jp, 64 * jp + 32
                    hA, hB = 4 * cp + 2 * jp, 4 * cp + 2 * jp + 1
                    base = 256 * qt + 16 * p
                    for kt in range(KT):
                        par = kt % 2
                        t_.wait_ge(s_xp, max(0, base + kt - 1))
                        i1 = t_.matmul(ps_big[:, par * 1024:par * 1024 + 512],
                                       k_sb[sA:sA + 32, cp * T + kt * 128:cp * T + (kt + 1) * 128],
                                       q_sb[sA:sA + 32, cp * T + qt * 512:cp * T + (qt + 1) * 512],
                                       start=True, stop=True, tile_position=(sA, 0),
                                       skip_group_check=True)
                        i2 = t_.matmul(ps_big[:, par * 1024 + 512:par * 1024 + 1024],
                                       k_sb[sB:sB + 32, cp * T + kt * 128:cp * T + (kt + 1) * 128],
                                       q_sb[sB:sB + 32, cp * T + qt * 512:cp * T + (qt + 1) * 512],
                                       start=True, stop=True, tile_position=(sB, 0),
                                       skip_group_check=True)
                        i2.then_inc(s_sc, 1)
                        t_.wait_ge(s_xp, base + kt + 1)
                        if p >= 2 and kt == 0:
                            t_.wait_ge(s_ev, evb + p - 1)
                        if qt > 0 and p < 2 and kt == 0:
                            t_.wait_ge(s_ev, evb)
                        ub = (p % 2) * 512
                        t_.matmul(ps_two[0:33, ub:ub + 512],
                                  v_sb[:, kt * NH * 33 + hA * 33:kt * NH * 33 + hA * 33 + 33],
                                  expS[:, par * 1024:par * 1024 + 512],
                                  start=(kt == 0), stop=(kt == KT - 1),
                                  tile_position=(0, 0), skip_group_check=True)
                        i4 = t_.matmul(ps_two[64:97, ub:ub + 512],
                                       v_sb[:, kt * NH * 33 + hB * 33:kt * NH * 33 + hB * 33 + 33],
                                       expS[:, par * 1024 + 512:par * 1024 + 1024],
                                       start=(kt == 0), stop=(kt == KT - 1),
                                       tile_position=(0, 64), skip_group_check=True)
                        i4.then_inc(s_av, 1)
                for p in range(NPAIR):
                    t_.wait_ge(s_ev, evb + p + 1)
                    ins = t_.matmul(ps_sm[0:32, :], sel_sb[:, p * 32:(p + 1) * 32],
                                    attn[:, p * 512:(p + 1) * 512],
                                    start=(p == 0), stop=(p == NPAIR - 1),
                                    skip_group_check=True)
                ins.then_inc(sm, 1)
                for p in range(NPAIR):
                    t_.wait_ge(sv, P3_SV(qt) + 1 + 3 * p)
                    t_.matmul(ps_big[:, 0:512], ep_sb[0:32, p * 128:(p + 1) * 128],
                              rpk[0:32, :], start=True, stop=True,
                              skip_group_check=True).then_inc(sm, 1)
                for ts in range(4):
                    for half in range(2):
                        t_.wait_ge(sv, P3_SV(qt) + 49)
                        if ts >= 1 or half >= 1:
                            pidx = 2 * ts + half - 2
                            if pidx >= 0:
                                t_.wait_ge(sv, P3_SV(qt) + 49 + (pidx // 2) * 10 +
                                           (2 if pidx % 2 == 1 else 1))
                        ins = None
                        for p in range(NPAIR):
                            ins = t_.matmul(
                                ps_big[:, 1024 + half * 512:1024 + (half + 1) * 512],
                                attn[:, p * 512 + ts * 128:p * 512 + (ts + 1) * 128],
                                wout_sb[:, p * D + half * 512:p * D + (half + 1) * 512],
                                start=(p == 0), stop=(p == NPAIR - 1),
                                skip_group_check=True)
                        ins.then_inc(sm, 1)

        @block.scalar
        def _(sc):
            if ph >= 6:
                sc.wait_ge(ld, LD_NB)
            for qt in range(NQT if ph >= 6 else 0):
                for p in range(NPAIR):
                    base = 256 * qt + 16 * p
                    for kt in range(KT):
                        par = kt % 2
                        sc.wait_ge(s_sc, base + kt + 1)
                        sc.wait_ge(s_av, max(0, base + kt - 1))
                        sc.activation(expS[:, par * 1024:par * 1024 + 1024],
                                      ps_big[:, par * 1024:par * 1024 + 1024],
                                      AF.Exp, bias=biasB_sb[:, :],
                                      scale=SCALE).then_inc(s_xp, 1)
                for ts in range(4):
                    b3 = P3_SA(qt) + 3 * ts
                    sc.wait_ge(sv, P3_SV(qt) + 49 + 10 * ts + 2)
                    sc.activation(scr[:, :], spre[:, (ts % 2) * D:(ts % 2 + 1) * D],
                                  AF.Square, accum_out=st1[:, 2:3]).then_inc(sa, 1)
                    sc.wait_ge(sv, P3_SV(qt) + 49 + 10 * ts + 5)
                    sc.activation(st1[:, 7:8], st1[:, 5:6], AF.Ln,
                                  bias=eps_sb[:, 1:2], scale=1.0).then_inc(sa, 1)
                    sc.wait_ge(sa, b3 + 2)
                    sc.activation(st1[:, 6:7], st1[:, 7:8], AF.Exp,
                                  bias=0.0, scale=-0.5).then_inc(sa, 1)

        @block.vector
        def _(v):
            for qt in range(NQT if ph >= 6 else 0):
                bq = P3_SV(qt)
                evb = QT_EV(qt)
                for p in range(NPAIR):
                    v.wait_ge(s_av, 256 * qt + 16 * p + 16)
                    v.tensor_copy(attn[:, p * 512:(p + 1) * 512],
                                  ps_two[:, (p % 2) * 512:(p % 2) * 512 + 512]).then_inc(s_ev, 1)
                v.wait_ge(sm, P3_SM(qt) + 1)
                with nc.allow_low_precision(reason="1/denom bf16: uniform per-(h,q) scale"):
                    v.reciprocal(rpk[0:32, :], ps_sm[0:32, :]).then_inc(sv, 1)
                for p in range(NPAIR):
                    v.wait_ge(sm, P3_SM(qt) + 2 + p)
                    v.tensor_copy(rb_bf[:, :], ps_big[:, 0:512]).then_inc(sv, 1)
                    v.wait_ge(sv, bq + 1 + 3 * p + 1)
                    v.tensor_tensor(attn[0:64, p * 512:(p + 1) * 512],
                                    attn[0:64, p * 512:(p + 1) * 512],
                                    rb_bf[0:64, :], op=ALU.mult).then_inc(sv, 1)
                    v.tensor_tensor(attn[64:128, p * 512:(p + 1) * 512],
                                    attn[64:128, p * 512:(p + 1) * 512],
                                    rb_bf[64:128, :], op=ALU.mult).then_inc(sv, 1)
                for ts in range(4):
                    be = bq + 49 + 10 * ts
                    tok0 = qt * 512 + ts * 128
                    sl = (ts % 2)
                    v.wait_ge(sm, P3_SM(qt) + 17 + 2 * ts + 1)
                    if True:
                        pts = 4 * qt + ts - 2
                        if pts >= 0:
                            v.wait_ge(lx, LX_EOUT(pts // 4, pts % 4))
                    gts = 4 * qt + ts - 2
                    for half in range(2):
                        if half == 1:
                            v.wait_ge(sm, P3_SM(qt) + 17 + 2 * ts + 2)
                        if gts >= 0:
                            v.wait_ge(sa, P3_SA(gts // 4) + 3 * (gts % 4) + 1)
                        v.tensor_tensor(spre[:, sl * D + half * 512:sl * D + (half + 1) * 512],
                                        ps_big[:, 1024 + half * 512:1024 + (half + 1) * 512],
                                        bout_sb[:, half * 512:(half + 1) * 512],
                                        op=ALU.add).then_inc(sv, 1)
                    sp = spre[:, sl * D:(sl + 1) * D]
                    v.wait_ge(sv, be + 2)
                    v.tensor_scalar(scr[:, :], sp, 1.0 / D, 0.0, op0=ALU.mult,
                                    op1=ALU.add,
                                    accum_out=st1[:, 1:2]).then_inc(sv, 1)
                    v.wait_ge(sa, P3_SA(qt) + 3 * ts + 1)
                    v.wait_ge(sv, be + 3)
                    v.scalar_tensor_tensor(st1[:, 3:4], st1[:, 1:2], -1.0, st1[:, 1:2],
                                           op0=ALU.mult, op1=ALU.mult).then_inc(sv, 1)
                    v.wait_ge(sv, be + 4)
                    v.tensor_scalar(st1[:, 5:6], st1[:, 2:3], 1.0 / D, st1[:, 3:4],
                                    op0=ALU.mult, op1=ALU.add).then_inc(sv, 1)
                    v.wait_ge(sa, P3_SA(qt) + 3 * ts + 3)
                    v.tensor_scalar(scr2[:, :], sp, st1[:, 1:2], st1[:, 6:7],
                                    op0=ALU.subtract, op1=ALU.mult).then_inc(sv, 1)
                    v.wait_ge(sv, be + 6)
                    v.tensor_tensor(scr2[:, :], scr2[:, :], gam_sb[:, :],
                                    op=ALU.mult).then_inc(sv, 1)
                    v.wait_ge(sv, be + 7)
                    v.tensor_tensor(scr2[:, :], scr2[:, :], bet_sb[:, :],
                                    op=ALU.add).then_inc(sv, 1)
                    v.wait_ge(sv, be + 8)
                    v.wait_ge(lx, LX_EGG(qt, ts))
                    v.tensor_tensor(scr2[:, :], scr2[:, :],
                                    gst[:, sl * D:(sl + 1) * D],
                                    op=ALU.mult).then_inc(sv, 1)
                    v.wait_ge(sv, be + 9)
                    v.wait_ge(lx, LX_EXG(qt, ts))
                    v.tensor_tensor(spre[:, sl * D:(sl + 1) * D], scr2[:, :],
                                    xst[:, sl * D:(sl + 1) * D],
                                    op=ALU.add).then_inc(sv, 1)

        @block.sync
        def _(s):
            for qt in range(NQT if ph >= 6 else 0):
                for ts in range(4):
                    tok0 = qt * 512 + ts * 128
                    sl = ts % 2
                    pts = 4 * qt + ts - 2
                    if pts >= 0:
                        s.wait_ge(sv, P3_SV(pts // 4) + 49 + 10 * (pts % 4) + 10)
                    s.dma_start(out=xst[:, sl * D:(sl + 1) * D],
                                in_=x_d[tok0:tok0 + 128, :]).then_inc(lx, 16)
                    s.dma_start(out=gst[:, sl * D:(sl + 1) * D],
                                in_=g_d[tok0:tok0 + 128, :]).then_inc(lx, 16)
                for ts in range(4):
                    s.wait_ge(sv, P3_SV(qt) + 49 + 10 * ts + 10)
                    s.dma_start(out=out_d[qt * 512 + ts * 128:qt * 512 + (ts + 1) * 128, :],
                                in_=spre[:, (ts % 2) * D:(ts % 2 + 1) * D]).then_inc(lx, 16)
            if ph >= 6:
                s.wait_ge(lx, LX_TOT)
            else:
                tgt = {1: ("sv", P1_SV), 2: ("ld", LD_NB), 3: ("sv", SV_2B),
                       4: ("lx", LX2), 5: ("lx", LX3), 5.5: ("lx", LX4)}[ph]
                s.wait_ge({"sv": sv, "ld": ld, "lx": lx}[tgt[0]], tgt[1])
                s.dma_start(out=out_d[0:128, :], in_=xst[:, 0:D]).then_inc(lx, 16)
                s.wait_ge(lx, (LX1 if ph <= 3 else LX2 if ph == 4 else
                               LX3 if ph == 5 else LX4) + 16)
    return nc


def _host_consts():
    ident = np.eye(128, dtype=np.float32)
    e4 = np.zeros((128, 4), np.float32)
    for j in range(4):
        e4[32 * j:32 * (j + 1), j] = 1.0
    g8 = np.zeros((128, 128), np.float32)
    for e in range(8):
        for r in range(4):
            g8[e, 4 * e + r] = 1.0
    sel = np.zeros((128, NPAIR * 32), np.float32)
    for p in range(NPAIR):
        sel[32, p * 32 + 2 * p] = 1.0
        sel[96, p * 32 + 2 * p + 1] = 1.0
    ep = np.zeros((32, NPAIR * 128), np.float32)
    for p in range(NPAIR):
        ep[2 * p, p * 128 + 0:p * 128 + 32] = 1.0
        ep[2 * p + 1, p * 128 + 64:p * 128 + 96] = 1.0
    ones1 = np.ones((1, 128), np.float32)
    return ident, e4, g8, sel, ep, ones1


def kernel(L0_attn_x, L0_attn_w_qkv, L0_attn_w_out, L0_attn_b_out, L0_attn_ln_gamma,
           L0_attn_ln_beta, L0_ffn_W_lin, L0_ffn_b_lin, L0_ffn_gate_w, L0_ffn_left,
           L0_ffn_right):
    x = np.ascontiguousarray(np.asarray(L0_attn_x, dtype=np.float32))
    wqkv = np.ascontiguousarray(np.asarray(L0_attn_w_qkv, dtype=np.float32))
    wout = np.asarray(L0_attn_w_out, dtype=np.float32)
    bout = np.asarray(L0_attn_b_out, dtype=np.float32)
    gam = np.asarray(L0_attn_ln_gamma, dtype=np.float32)
    bet = np.asarray(L0_attn_ln_beta, dtype=np.float32)
    wlin = np.asarray(L0_ffn_W_lin, dtype=np.float32)
    blin = np.asarray(L0_ffn_b_lin, dtype=np.float32)
    gw = np.asarray(L0_ffn_gate_w, dtype=np.float32)
    left = np.asarray(L0_ffn_left, dtype=np.float32)
    right = np.asarray(L0_ffn_right, dtype=np.float32)

    wlinT = np.ascontiguousarray(wlin.T)
    gwT = np.ascontiguousarray(gw.T)
    leftf = np.ascontiguousarray(left.transpose(1, 0, 2).reshape(D, 32))
    rightp = np.zeros((128, D), np.float32)
    rightp[0:32] = right.reshape(32, D)
    wout_h = np.zeros((NPAIR * 128, D), np.float32)
    for p in range(NPAIR):
        wout_h[128 * p + 0:128 * p + 32] = wout[32 * (2 * p):32 * (2 * p) + 32]
        wout_h[128 * p + 64:128 * p + 96] = wout[32 * (2 * p + 1):32 * (2 * p + 1) + 32]
    rep = lambda a: np.ascontiguousarray(np.broadcast_to(a.reshape(1, D), (128, D)))
    ident, e4, g8, sel, ep, ones1 = _host_consts()

    nc = build_nc()
    eps_rep = np.ascontiguousarray(np.broadcast_to(
        np.array([EPS1, EPS2], np.float32).reshape(1, 2), (128, 2)))
    shared = dict(wqkv=wqkv, wlinT=wlinT, wout_h=wout_h, gwT=gwT, leftf=leftf,
                  rightp=rightp, blin_rep=rep(blin), bout_rep=rep(bout),
                  gam_rep=rep(gam), bet_rep=rep(bet), ident=ident, e4=e4,
                  g8pad=g8, sel=sel, epair=ep, ones1=ones1, eps_rep=eps_rep)
    in_maps = [dict(shared, x=np.ascontiguousarray(x[b])) for b in range(NCORES)]
    res = run_bass_kernel_spmd(nc, in_maps, core_ids=list(range(NCORES)))
    out = np.stack([res.results[b]["out"] for b in range(NCORES)], axis=0)
    return out.astype(np.float32)


# version 04
# speedup vs baseline: 1.6241x; 1.6241x over previous
import contextlib
import numpy as np
import ml_dtypes
import concourse.bass as bass
import concourse.mybir as mybir
from concourse.bass_utils import run_bass_kernel_spmd

F32 = mybir.dt.float32
BF16 = mybir.dt.bfloat16
AF = mybir.ActivationFunctionType
ALU = mybir.AluOpType
AX = mybir.AxisListType

NCORES = 8
T, D = 2048, 1024
NH, DH = 32, 32
NT = T // 128
NQT = T // 512
KT = T // 128
NPAIR = 16
SCALE = float(DH) ** -0.5
EPS1, EPS2 = 1e-5, 1e-6
BIG = 1e30


def build_nc(ph=6):
    nc = bass.Bass("TRN2", target_bir_lowering=False)
    dram = lambda name, shape, dtype=F32, kind="ExternalInput": nc.dram_tensor(
        name, shape, dtype, kind=kind).ap()
    x_d = dram("x", [T, D])
    wqkv_d = dram("wqkv", [D, 3 * D])
    wlinT_d = dram("wlinT", [D, D])
    wout_d = dram("wout_h", [NPAIR * 128, D])
    gwT_d = dram("gwT", [D, 8])
    leftf_d = dram("leftf", [D, 32])
    rightp_d = dram("rightp", [128, D])
    blin_d = dram("blin_rep", [128, D])
    bout_d = dram("bout_rep", [128, D])
    gam_d = dram("gam_rep", [128, D])
    bet_d = dram("bet_rep", [128, D])
    ident_d = dram("ident", [128, 128])
    e4_d = dram("e4", [128, 4])
    g8_d = dram("g8pad", [128, 128])
    sel_d = dram("sel", [128, NPAIR * 32])
    ep_d = dram("epair", [32, NPAIR * 128])
    ones1_d = dram("ones1", [1, 128])
    eps_d = dram("eps_rep", [128, 2])
    out_d = dram("out", [T, D], F32, kind="ExternalOutput")
    pre_d = dram("pre_scr", [T, D], BF16, kind="Internal")
    g_d = dram("g_scr", [T, D], BF16, kind="Internal")
    bnd_d = dram("bnd_scr", [4, 64], F32, kind="Internal")
    nb_d = dram("negB_scr", [128, 1], F32, kind="Internal")

    ctx = contextlib.ExitStack()
    with ctx:
        sb = lambda name, shape, dtype: ctx.enter_context(nc.sbuf_tensor(name, shape, dtype))
        psum = lambda name, shape, dtype=F32: ctx.enter_context(nc.psum_tensor(name, shape, dtype))

        q_sb = sb("q_sb", [128, 8 * T], BF16)
        k_sb = sb("k_sb", [128, 8 * T], BF16)
        v_sb = sb("v_sb", [128, KT * NH * 33], BF16)
        ident_sb = sb("ident_sb", [128, 128], BF16)
        sel_sb = sb("sel_sb", [128, NPAIR * 32], BF16)
        ep_sb = sb("ep_sb", [32, NPAIR * 128], BF16)
        e4_sb = sb("e4_sb", [128, 4], BF16)
        g8_sb = sb("g8_sb", [128, 128], BF16)
        st1 = sb("st1", [128, 16], F32)
        xst = sb("xst", [128, 2 * D], F32)
        scr = sb("scr", [128, D], BF16)
        scr2 = sb("scr2", [128, D], F32)
        xn_tm = sb("xn_tm", [128, 2 * D], BF16)
        prest = xn_tm
        blin_sb = sb("blin_sb", [128, D], F32)
        biasB_sb = sb("biasB_sb", [128, 1], F32)
        eps_sb = sb("eps_sb", [128, 2], F32)
        mqk = scr2[0:1, 256:512]
        ones1_sb = scr2[0:1, 512:640]
        bnd = scr2[0:4, 640:720]

        ps_big = psum("ps_big", [128, 2048])
        ps_two = psum("ps_two", [128, 1024])
        ps_sm = psum("ps_sm", [128, 512])
        ps_tr = psum("ps_tr", [128, 256], BF16)

        sem = lambda name: ctx.enter_context(nc.semaphore(name))
        ld = sem("ld")
        lx = sem("lx")
        sv = sem("sv")
        sa = sem("sa")
        sm = sem("sm")
        s_sc = sem("ssc")
        s_xp = sem("sxp")
        s_av = sem("sav")
        s_ev = sem("sev")

        block = ctx.enter_context(nc.Block())

        G0_DMAS = 8 * 4 + 6
        G0_LD = 16 * G0_DMAS
        LDW = lambda dc: G0_LD + 16 * (dc + 1)
        LD_2A = G0_LD + 16 * 16
        LD_BND1 = LD_2A + 16
        LD_BND = LD_2A + 16 * 3
        LD_NB1 = LD_BND + 16
        LD_NB = LD_BND + 32
        LD_WOUT = lambda i: LD_NB + 16 * (i + 1)
        LD_TOT = LD_NB + 16 * NPAIR

        LX_CONST = 16 * 3
        LX_X1 = lambda t: LX_CONST + 16 * (t + 1)
        LX1 = LX_CONST + 16 * NT
        LX_PST = lambda t: LX1 + 16 * (t + 1)
        LX2 = LX1 + 16 * NT
        _pr = {t: (t if t < 3 else 2 * t - 2) for t in range(NT)}
        _ps = {t: (3 + 2 * t if t <= NT - 4 else NT + t) for t in range(NT)}
        LX_PLD = lambda t: LX2 + 16 * (_pr[t] + 1)
        LX_GST = lambda t: LX2 + 16 * (_ps[t] + 1)
        LX3 = LX2 + 16 * NT
        LX4 = LX2 + 16 * 2 * NT
        LX_EPI = lambda qt: LX4 + 16 * 12 * qt
        LX_EXG = lambda qt, ts: LX_EPI(qt) + 16 * (2 * ts + 1)
        LX_EGG = lambda qt, ts: LX_EPI(qt) + 16 * (2 * ts + 2)
        LX_EOUT = lambda qt, ts: LX_EPI(qt) + 16 * 8 + 16 * (ts + 1)
        LX_TOT = LX4 + 16 * 12 * NQT

        P1_SV, P1_SA, P1_SM = 12 * NT, 3 * NT, 8 * NT
        SV_2A = P1_SV + 12 * 16
        P2A_SVD = lambda dc: P1_SV + 12 * dc
        P2A_SM = lambda dc: P1_SM + 5 * dc
        SM_2A = P1_SM + 5 * 16
        SA_BF = P1_SA + 2
        SV_BF = SV_2A + 4
        P2B_SV = lambda t: SV_BF + 12 * t
        P2B_SM = lambda t: SM_2A + 2 * t
        P2B_SA = lambda t: SA_BF + t
        SV_2B, SM_2B, SA_2B = SV_BF + 12 * NT, SM_2A + 2 * NT, SA_BF + NT
        P2C_SV = lambda qt: SV_2B + 2 * qt
        P2C_SM = lambda qt: SM_2B + 2 * qt
        SV_2C, SM_2C = SV_2B + 2 * NQT, SM_2B + 2 * NQT
        P2D_SV = lambda t: SV_2C + 2 * t
        P2D_SM = lambda t: SM_2C + 2 * t
        SV_2D, SM_2D = SV_2C + 2 * NT, SM_2C + 2 * NT
        SA_SIL = lambda t: SA_2B + t + 1
        SA_3P = SA_2B + NT
        QT_SC = lambda qt: 256 * qt
        QT_AV = lambda qt: 256 * qt
        QT_XP = lambda qt: 256 * qt
        QT_EV = lambda qt: 16 * qt
        P3_SV = lambda qt: SV_2D + 89 * qt
        P3_SM = lambda qt: SM_2D + 25 * qt
        P3_SA = lambda qt: SA_3P + 12 * qt
        SV_TOT = SV_2D + 89 * NQT
        SM_TOT = SM_2D + 25 * NQT
        SA_TOT = SA_3P + 12 * NQT

        @block.gpsimd
        def _(g):
            n = 0
            for dst, src in [(ident_sb[:, :], ident_d), (e4_sb[:, :], e4_d),
                             (g8_sb[:, :], g8_d), (sel_sb[:, :], sel_d),
                             (ep_sb[0:32, :], ep_d)]:
                g.dma_start(out=dst, in_=src).then_inc(ld, 16)
                n += 1
            g.memset(v_sb[:, :], 1.0)
            assert n == 5

        @block.sync
        def _(s):
            for dst, src in [(blin_sb[:, :], blin_d), (ones1_sb[:, :], ones1_d),
                             (eps_sb[:, :], eps_d)]:
                s.dma_start(out=dst, in_=src).then_inc(lx, 16)
            for t in range(NT):
                if t >= 2:
                    s.wait_ge(sv, 12 * (t - 2) + 4)
                s.dma_start(out=xst[:, (t % 2) * D:(t % 2 + 1) * D],
                            in_=x_d[t * 128:(t + 1) * 128, :]).then_inc(lx, 16)

        ctx2 = ctx.enter_context(contextlib.ExitStack())
        sb2 = lambda name, shape, dtype: ctx2.enter_context(nc.sbuf_tensor(name, shape, dtype))
        xn_sb = sb2("xn_sb", [128, 8 * T], BF16)
        wv_sb = sb2("wv_sb", [128, 8 * D], BF16)
        wlin_sb = sb2("wlin_sb", [128, 8 * D], BF16)
        gwT_sb = sb2("gwT_sb", [128, 64], BF16)
        leftf_sb = sb2("leftf_sb", [128, 256], BF16)
        rightp_sb = sb2("rightp_sb", [128, D], BF16)
        wq_st = sb2("wq_st", [128, 2 * 1024], BF16)
        sqs = sb2("sqs", [128, 512], BF16)
        gates_bf = sb2("gates_bf", [128, NT * 8], BF16)
        gpad = sb2("gpad", [128, 128], BF16)
        gfm_sb = sb2("gfm_sb", [128, NT * 128], BF16)
        t_tmp = sb2("t_tmp", [32, 512], BF16)
        t1pad = sb2("t1pad", [128, T], BF16)

        @block.gpsimd
        def _(g):
            for c in range(8):
                g.dma_start(out=wv_sb[:, c * D:(c + 1) * D],
                            in_=wqkv_d[c * 128:(c + 1) * 128, 2 * D:3 * D]).then_inc(ld, 16)
                g.dma_start(out=wlin_sb[:, c * D:(c + 1) * D],
                            in_=wlinT_d[c * 128:(c + 1) * 128, :]).then_inc(ld, 16)
                g.dma_start(out=gwT_sb[:, c * 8:(c + 1) * 8],
                            in_=gwT_d[c * 128:(c + 1) * 128, :]).then_inc(ld, 16)
                g.dma_start(out=leftf_sb[:, c * 32:(c + 1) * 32],
                            in_=leftf_d[c * 128:(c + 1) * 128, :]).then_inc(ld, 16)
            g.dma_start(out=rightp_sb[:, :], in_=rightp_d).then_inc(ld, 16)
            g.memset(t1pad[:, :], 0.0)
            g.memset(gpad[:, :], 0.0)
            for dc in range(16 if ph >= 2 else 0):
                if dc >= 2:
                    g.wait_ge(sm, P2A_SM(dc - 1) + 1)
                g.dma_start(
                    out=wq_st[:, (dc % 2) * 1024:(dc % 2 + 1) * 1024].rearrange(
                        "p (c m) -> p c m", c=8),
                    in_=wqkv_d[:, dc * 128:(dc + 1) * 128].rearrange(
                        "(c p) m -> p c m", p=128)).then_inc(ld, 16)

        @block.vector
        def _(v):
            for t in range(NT):
                b = 12 * t
                xt = xst[:, (t % 2) * D:(t % 2 + 1) * D]
                v.wait_ge(lx, LX_X1(t))
                if t >= 2:
                    v.wait_ge(sm, 8 * (t - 2) + 8)
                v.tensor_scalar(xn_tm[:, (t % 2) * D:(t % 2 + 1) * D], xt, 1.0 / D,
                                0.0, op0=ALU.mult, op1=ALU.add,
                                accum_out=st1[:, 1:2]).then_inc(sv, 1)
                v.wait_ge(sa, 3 * t + 1)
                v.wait_ge(sv, b + 1)
                v.scalar_tensor_tensor(st1[:, 3:4], st1[:, 1:2], -1.0, st1[:, 1:2],
                                       op0=ALU.mult, op1=ALU.mult).then_inc(sv, 1)
                v.wait_ge(sv, b + 2)
                if t >= 1:
                    v.wait_ge(sa, 3 * (t - 1) + 2)
                v.tensor_scalar(st1[:, 5:6], st1[:, 2:3], 1.0 / D, st1[:, 3:4],
                                op0=ALU.mult, op1=ALU.add).then_inc(sv, 1)
                v.wait_ge(sa, 3 * t + 3)
                v.tensor_scalar(xn_tm[:, (t % 2) * D:(t % 2 + 1) * D], xt,
                                st1[:, 1:2], st1[:, 6:7],
                                op0=ALU.subtract, op1=ALU.mult).then_inc(sv, 1)
                for c in range(8):
                    v.wait_ge(sm, 8 * t + c + 1)
                    v.tensor_copy(xn_sb[:, c * T + t * 128:c * T + (t + 1) * 128],
                                  ps_tr[:, 0:128]).then_inc(sv, 1)

        @block.scalar
        def _(sc):
            for t in range(NT):
                xt = xst[:, (t % 2) * D:(t % 2 + 1) * D]
                sc.wait_ge(lx, LX_X1(t))
                if t >= 1:
                    sc.wait_ge(sv, 12 * (t - 1) + 3)
                sc.activation(scr[:, :], xt, AF.Square,
                              accum_out=st1[:, 2:3]).then_inc(sa, 1)
                sc.wait_ge(sv, 12 * t + 3)
                sc.activation(st1[:, 7:8], st1[:, 5:6], AF.Ln,
                              bias=eps_sb[:, 0:1], scale=1.0).then_inc(sa, 1)
                sc.wait_ge(sa, 3 * t + 2)
                if t >= 1:
                    sc.wait_ge(sv, 12 * (t - 1) + 4)
                sc.activation(st1[:, 6:7], st1[:, 7:8], AF.Exp,
                              bias=0.0, scale=-0.5).then_inc(sa, 1)

        @block.tensor
        def _(t_):
            t_.wait_ge(ld, 16 * 5)
            for t in range(NT):
                for c in range(8):
                    t_.wait_ge(sv, 12 * t + 4)
                    prev = 8 * t + c - 1
                    if prev >= 0:
                        pt, pc = prev // 8, prev % 8
                        t_.wait_ge(sv, 12 * pt + 4 + pc + 1)
                    t_.transpose(ps_tr[:, 0:128],
                                 xn_tm[:, (t % 2) * D + c * 128:(t % 2) * D + (c + 1) * 128],
                                 ident_sb[:, :]).then_inc(sm, 1)

        @block.tensor
        def _(t_):
            for dc in range(16 if ph >= 2 else 0):
                t_.wait_ge(ld, LDW(dc))
                if dc >= 1:
                    t_.wait_ge(sv, P2A_SVD(dc - 1) + 4)
                ins = None
                for c in range(8):
                    for qt in range(NQT):
                        ins = t_.matmul(
                            ps_big[:, qt * 512:(qt + 1) * 512],
                            wq_st[:, (dc % 2) * 1024 + c * 128:(dc % 2) * 1024 + (c + 1) * 128],
                            xn_sb[:, c * T + qt * 512:c * T + (qt + 1) * 512],
                            start=(c == 0), stop=(c == 7), skip_group_check=True)
                ins.then_inc(sm, 1)
                for qt in range(NQT):
                    t_.wait_ge(sv, P2A_SVD(dc) + 5 + 2 * qt)
                    t_.matmul(ps_sm[0:4, 0:512], e4_sb[:, :], sqs[:, :],
                              start=True, stop=True,
                              skip_group_check=True).then_inc(sm, 1)

        @block.vector
        def _(v):
            for dc in range(16 if ph >= 2 else 0):
                dst = q_sb if dc < 8 else k_sb
                cc = dc % 8
                base = P2A_SVD(dc)
                v.wait_ge(sm, P2A_SM(dc) + 1)
                for qt in range(NQT):
                    v.tensor_copy(dst[:, cc * T + qt * 512:cc * T + (qt + 1) * 512],
                                  ps_big[:, qt * 512:(qt + 1) * 512]).then_inc(sv, 1)
                for qt in range(NQT):
                    v.wait_ge(sv, base + 4)
                    v.wait_ge(sm, P2A_SM(dc) + 1 + qt)
                    src = dst[:, cc * T + qt * 512:cc * T + (qt + 1) * 512]
                    v.tensor_tensor(sqs[:, :], src, src, op=ALU.mult).then_inc(sv, 1)
                    v.wait_ge(sm, P2A_SM(dc) + 1 + qt + 1)
                    v.reduce_max(bnd[0:4, dc * 4 + qt:dc * 4 + qt + 1],
                                 ps_sm[0:4, 0:512], axis=AX.X).then_inc(sv, 1)

        @block.gpsimd
        def _(g):
            if ph < 2:
                return
            g.wait_ge(sv, SV_2A)
            g.dma_start(out=bnd_d, in_=bnd[0:4, 0:64]).then_inc(ld, 16)
            g.wait_ge(ld, LD_BND1)
            g.dma_start(out=mqk[:, 0:128].rearrange("a (p c) -> a p c", p=4),
                        in_=bnd_d[:, 0:32]).then_inc(ld, 16)
            g.dma_start(out=mqk[:, 128:256].rearrange("a (p c) -> a p c", p=4),
                        in_=bnd_d[:, 32:64]).then_inc(ld, 16)

        @block.vector
        def _(v):
            if ph < 2:
                return
            v.wait_ge(ld, LD_BND)
            v.reduce_max(st1[0:1, 8:9], mqk[:, 0:128], axis=AX.X)
            v.reduce_max(st1[0:1, 9:10], mqk[:, 128:256], axis=AX.X).then_inc(sv, 1)
            v.wait_ge(sv, SV_2A + 1)
            v.tensor_tensor(st1[0:1, 10:11], st1[0:1, 8:9], st1[0:1, 9:10],
                            op=ALU.mult).then_inc(sv, 1)
            v.wait_ge(sa, SA_BF)
            v.tensor_scalar(st1[0:1, 12:13], st1[0:1, 11:12], -SCALE * 1.02, None,
                            op0=ALU.mult).then_inc(sv, 1)
            v.wait_ge(sv, SV_2A + 3)
            v.tensor_scalar(mqk[:, 0:128], ones1_sb[0:1, :], st1[0:1, 12:13], None,
                            op0=ALU.mult).then_inc(sv, 1)

        @block.scalar
        def _(sc):
            if ph < 2:
                return
            sc.wait_ge(sv, SV_2A + 2)
            sc.activation(st1[0:1, 13:14], st1[0:1, 10:11], AF.Ln,
                          bias=0.0, scale=1.0).then_inc(sa, 1)
            sc.wait_ge(sa, P1_SA + 1)
            sc.activation(st1[0:1, 11:12], st1[0:1, 13:14], AF.Exp,
                          bias=0.0, scale=0.5).then_inc(sa, 1)

        @block.gpsimd
        def _(g):
            if ph < 2:
                return
            g.wait_ge(sv, SV_BF)
            g.dma_start(out=nb_d[:, :].rearrange("p a -> (p a)").rearrange(
                "(a q) -> a q", a=1), in_=mqk[:, 0:128]).then_inc(ld, 16)
            g.wait_ge(ld, LD_NB1)
            g.dma_start(out=biasB_sb[:, :], in_=nb_d).then_inc(ld, 16)

        @block.tensor
        def _(t_):
            for t in range(NT if ph >= 3 else 0):
                if t == 0:
                    t_.wait_ge(sv, SV_2A)
                if t >= 1:
                    t_.wait_ge(sv, P2B_SV(t - 1) + 7)
                    t_.wait_ge(sa, P2B_SA(t - 1) + 1)
                ins = None
                for c in range(8):
                    xnc = xn_sb[:, c * T + t * 128:c * T + (t + 1) * 128]
                    t_.matmul(ps_two[:, 0:512], xnc, wv_sb[:, c * D:c * D + 512],
                              start=(c == 0), stop=(c == 7), skip_group_check=True)
                    t_.matmul(ps_two[:, 512:1024], xnc, wv_sb[:, c * D + 512:(c + 1) * D],
                              start=(c == 0), stop=(c == 7), skip_group_check=True)
                    ins = t_.matmul(ps_sm[:, 0:8], xnc, gwT_sb[:, c * 8:(c + 1) * 8],
                                    start=(c == 0), stop=(c == 7), skip_group_check=True)
                ins.then_inc(sm, 1)
                t_.wait_ge(sv, P2B_SV(t) + 11)
                if t >= 1:
                    t_.wait_ge(sv, P2B_SV(t - 1) + 12)
                t_.transpose(ps_tr[:, 128:256], gpad[:, :], ident_sb[:, :]).then_inc(sm, 1)

        @block.scalar
        def _(sc):
            for t in range(NT if ph >= 3 else 0):
                sc.wait_ge(sv, P2B_SV(t) + 2)
                sc.activation(scr2[:, 0:8], ps_sm[:, 0:8], AF.Exp, bias=st1[:, 8:9],
                              accum_out=st1[:, 9:10]).then_inc(sa, 1)

        @block.vector
        def _(v):
            for t in range(NT if ph >= 3 else 0):
                b = P2B_SV(t)
                v.wait_ge(sm, P2B_SM(t) + 1)
                v.tensor_copy(
                    v_sb[:, t * NH * 33:(t + 1) * NH * 33].rearrange(
                        "p (h w) -> p h w", h=NH)[:, :, 0:32],
                    ps_two[:, :].rearrange("p (h w) -> p h w", h=NH)).then_inc(sv, 1)
                v.reduce_max(st1[:, 8:9], ps_sm[:, 0:8], axis=AX.X,
                             negate=True).then_inc(sv, 1)
                v.wait_ge(sv, b + 2)
                v.tensor_scalar(st1[:, 10:11], st1[:, 8:9], -1.0, None,
                                op0=ALU.mult).then_inc(sv, 1)
                v.wait_ge(sv, b + 3)
                v.tensor_scalar(scr2[:, 8:16], ps_sm[:, 0:8], st1[:, 10:11], None,
                                op0=ALU.is_ge).then_inc(sv, 1)
                v.wait_ge(sv, b + 4)
                v.scalar_tensor_tensor(scr2[:, 16:24], scr2[:, 8:16], -BIG,
                                       ps_sm[:, 0:8], op0=ALU.mult,
                                       op1=ALU.add).then_inc(sv, 1)
                v.wait_ge(sv, b + 5)
                v.reduce_max(st1[:, 11:12], scr2[:, 16:24], axis=AX.X).then_inc(sv, 1)
                v.wait_ge(sv, b + 6)
                v.tensor_scalar(scr2[:, 24:32], ps_sm[:, 0:8], st1[:, 11:12], None,
                                op0=ALU.is_ge).then_inc(sv, 1)
                v.wait_ge(sa, P2B_SA(t) + 1)
                with nc.allow_low_precision(reason="tiny gate sums"):
                    v.reciprocal(st1[:, 12:13], st1[:, 9:10]).then_inc(sv, 1)
                v.wait_ge(sv, b + 8)
                v.scalar_tensor_tensor(gates_bf[:, t * 8:(t + 1) * 8], scr2[:, 0:8],
                                       1.0, scr2[:, 24:32], op0=ALU.mult,
                                       op1=ALU.mult).then_inc(sv, 1)
                v.wait_ge(sv, b + 9)
                v.tensor_scalar(gates_bf[:, t * 8:(t + 1) * 8],
                                gates_bf[:, t * 8:(t + 1) * 8], st1[:, 12:13], None,
                                op0=ALU.mult).then_inc(sv, 1)
                v.wait_ge(sv, b + 10)
                if t >= 1:
                    v.wait_ge(sm, P2B_SM(t - 1) + 2)
                v.tensor_copy(gpad[:, 0:8],
                              gates_bf[:, t * 8:(t + 1) * 8]).then_inc(sv, 1)
                v.wait_ge(sm, P2B_SM(t) + 2)
                v.tensor_copy(gfm_sb[:, t * 128:(t + 1) * 128],
                              ps_tr[:, 128:256]).then_inc(sv, 1)

        @block.tensor
        def _(t_):
            for qt in range(NQT if ph >= 4 else 0):
                if qt == 0:
                    t_.wait_ge(sv, SV_2B)
                    t_.wait_ge(sa, SA_2B)
                else:
                    t_.wait_ge(sv, P2C_SV(qt - 1) + 1)
                ins = None
                for c in range(8):
                    ins = t_.matmul(ps_sm[0:32, :], leftf_sb[:, c * 32:(c + 1) * 32],
                                    xn_sb[:, c * T + qt * 512:c * T + (qt + 1) * 512],
                                    start=(c == 0), stop=(c == 7), skip_group_check=True)
                ins.then_inc(sm, 1)
                if qt >= 1:
                    t_.wait_ge(sv, P2C_SV(qt - 1) + 1)
                t_.matmul(ps_two[:, 0:512], g8_sb[:, :],
                          gfm_sb[:, qt * 512:(qt + 1) * 512],
                          start=True, stop=True, skip_group_check=True).then_inc(sm, 1)

        @block.vector
        def _(v):
            for qt in range(NQT if ph >= 4 else 0):
                b = P2C_SV(qt)
                v.wait_ge(sm, P2C_SM(qt) + 1)
                v.tensor_copy(t_tmp[0:32, :], ps_sm[0:32, :]).then_inc(sv, 1)
                v.wait_ge(sv, b + 1)
                v.wait_ge(sm, P2C_SM(qt) + 2)
                v.tensor_tensor(t1pad[0:32, qt * 512:(qt + 1) * 512],
                                ps_two[0:32, 0:512], t_tmp[0:32, :],
                                op=ALU.mult).then_inc(sv, 1)

        @block.tensor
        def _(t_):
            for t in range(NT if ph >= 4 else 0):
                t_.wait_ge(sv, P2C_SV(t // 4) + 2)
                if t >= 2:
                    t_.wait_ge(sv, P2D_SV(t - 2) + 2)
                for half in range(2):
                    hb = (t % 2) * 1024
                    ins = None
                    for c in range(8):
                        ins = t_.matmul(
                            ps_big[:, hb + half * 512:hb + (half + 1) * 512],
                            xn_sb[:, c * T + t * 128:c * T + (t + 1) * 128],
                            wlin_sb[:, c * D + half * 512:c * D + (half + 1) * 512],
                            start=(c == 0), stop=False, skip_group_check=True)
                    ins = t_.matmul(
                        ps_big[:, hb + half * 512:hb + (half + 1) * 512],
                        t1pad[:, t * 128:(t + 1) * 128],
                        rightp_sb[:, half * 512:(half + 1) * 512],
                        start=False, stop=True, skip_group_check=True)
                    ins.then_inc(sm, 1)

        @block.vector
        def _(v):
            for t in range(NT if ph >= 4 else 0):
                b = P2D_SV(t)
                hb = (t % 2) * 1024
                for half in range(2):
                    v.wait_ge(sm, P2D_SM(t) + half + 1)
                    if t >= 2:
                        v.wait_ge(lx, LX_PST(t - 2))
                    v.tensor_tensor(prest[:, (t % 2) * D + half * 512:(t % 2) * D + (half + 1) * 512],
                                    ps_big[:, hb + half * 512:hb + (half + 1) * 512],
                                    blin_sb[:, half * 512:(half + 1) * 512],
                                    op=ALU.add).then_inc(sv, 1)

        @block.sync
        def _(s):
            for t in range(NT if ph >= 4 else 0):
                s.wait_ge(sv, P2D_SV(t) + 2)
                s.dma_start(out=pre_d[t * 128:(t + 1) * 128, :],
                            in_=prest[:, (t % 2) * D:(t % 2 + 1) * D]).then_inc(lx, 16)

        ctx2.close()
        ctx3 = ctx.enter_context(contextlib.ExitStack())
        sb3 = lambda name, shape, dtype: ctx3.enter_context(nc.sbuf_tensor(name, shape, dtype))
        wout_sb = sb3("wout_sb", [128, NPAIR * D], BF16)
        attn = sb3("attn", [128, NPAIR * 512], BF16)
        expS = sb3("expS", [128, 4 * 1024], BF16)
        rpk = sb3("rpk", [32, 512], BF16)
        rb_bf = sb3("rb_bf", [128, 512], BF16)
        gst = sb3("gst", [128, 2 * D], BF16)
        spre = sb3("spre", [128, 2 * D], F32)
        bout_sb = sb3("bout_sb", [128, D], F32)
        gam_sb = sb3("gam_sb", [128, D], F32)
        bet_sb = sb3("bet_sb", [128, D], F32)

        @block.gpsimd
        def _(g):
            if ph >= 5:
                g.wait_ge(sm, SM_2D)
                g.wait_ge(sv, SV_2D)
            for p in range(NPAIR if ph >= 5 else 0):
                g.dma_start(out=wout_sb[:, p * D:(p + 1) * D],
                            in_=wout_d[p * 128:(p + 1) * 128, :]).then_inc(ld, 16)

        @block.sync
        def _(s):
            for dst, src in [(bout_sb[:, :], bout_d), (gam_sb[:, :], gam_d),
                             (bet_sb[:, :], bet_d)]:
                s.dma_start(out=dst, in_=src).then_inc(lx, 16) if False else None
            if ph >= 5.5:
                order = [("r", 0), ("r", 1), ("r", 2)]
                for k in range(NT - 3):
                    order += [("s", k), ("r", k + 3)]
                order += [("s", NT - 3), ("s", NT - 2), ("s", NT - 1)]
                for kind, t in order:
                    if kind == "r":
                        s.wait_ge(lx, LX_PST(NT - 1))
                        if t >= 2:
                            s.wait_ge(sa, SA_SIL(t - 2))
                        s.dma_start(out=prest[:, (t % 2) * D:(t % 2 + 1) * D],
                                    in_=pre_d[t * 128:(t + 1) * 128, :]).then_inc(lx, 16)
                    else:
                        s.wait_ge(sa, SA_SIL(t))
                        s.dma_start(out=g_d[t * 128:(t + 1) * 128, :],
                                    in_=gst[:, (t % 2) * D:(t % 2 + 1) * D]).then_inc(lx, 16)
            elif ph == 5:
                for t in range(NT):
                    s.wait_ge(lx, LX_PST(NT - 1))
                    s.dma_start(out=prest[:, (t % 2) * D:(t % 2 + 1) * D],
                                in_=pre_d[t * 128:(t + 1) * 128, :]).then_inc(lx, 16)

        @block.gpsimd
        def _(g):
            for dst, src in ([(bout_sb[:, :], bout_d), (gam_sb[:, :], gam_d),
                             (bet_sb[:, :], bet_d)] if ph >= 5 else []):
                g.dma_start(out=dst, in_=src).then_inc(ld, 16)
        LD_TOT2 = LD_TOT + 16 * 3

        @block.scalar
        def _(sc):
            for t in range(NT if ph >= 5.5 else 0):
                sc.wait_ge(lx, LX_PLD(t))
                if t >= 2:
                    sc.wait_ge(lx, LX_GST(t - 2))
                sc.activation(gst[:, (t % 2) * D:(t % 2 + 1) * D],
                              prest[:, (t % 2) * D:(t % 2 + 1) * D],
                              AF.Silu).then_inc(sa, 1)

        @block.tensor
        def _(t_):
            if ph >= 6:
                t_.wait_ge(ld, LD_TOT2)
            for qt in range(NQT if ph >= 6 else 0):
                scb, avb, xpb, evb = QT_SC(qt), QT_AV(qt), QT_XP(qt), QT_EV(qt)
                if qt > 0:
                    t_.wait_ge(sv, P3_SV(qt))
                NI = NPAIR * KT

                def emit_scores(i):
                    p, kt = i // KT, i % KT
                    cp, jp = p // 2, p % 2
                    sA, sB = 64 * jp, 64 * jp + 32
                    base = 256 * qt + 16 * p
                    par = kt % 2
                    t_.wait_ge(s_xp, max(0, base + kt - 1))
                    t_.matmul(ps_big[:, par * 1024:par * 1024 + 512],
                              k_sb[sA:sA + 32, cp * T + kt * 128:cp * T + (kt + 1) * 128],
                              q_sb[sA:sA + 32, cp * T + qt * 512:cp * T + (qt + 1) * 512],
                              start=True, stop=True, tile_position=(sA, 0),
                              skip_group_check=True)
                    t_.matmul(ps_big[:, par * 1024 + 512:par * 1024 + 1024],
                              k_sb[sB:sB + 32, cp * T + kt * 128:cp * T + (kt + 1) * 128],
                              q_sb[sB:sB + 32, cp * T + qt * 512:cp * T + (qt + 1) * 512],
                              start=True, stop=True, tile_position=(sB, 0),
                              skip_group_check=True).then_inc(s_sc, 1)

                def emit_av(j):
                    p, kt = j // KT, j % KT
                    cp, jp = p // 2, p % 2
                    hA, hB = 4 * cp + 2 * jp, 4 * cp + 2 * jp + 1
                    base = 256 * qt + 16 * p
                    slot = (kt % 4) * 1024
                    ub = (p % 2) * 512
                    t_.wait_ge(s_xp, base + kt + 1)
                    if p >= 2 and kt == 0:
                        t_.wait_ge(s_ev, evb + p - 1)
                    if qt > 0 and p < 2 and kt == 0:
                        t_.wait_ge(s_ev, evb)
                    t_.matmul(ps_two[0:33, ub:ub + 512],
                              v_sb[:, kt * NH * 33 + hA * 33:kt * NH * 33 + hA * 33 + 33],
                              expS[:, slot:slot + 512],
                              start=(kt == 0), stop=(kt == KT - 1),
                              tile_position=(0, 0), skip_group_check=True)
                    t_.matmul(ps_two[64:97, ub:ub + 512],
                              v_sb[:, kt * NH * 33 + hB * 33:kt * NH * 33 + hB * 33 + 33],
                              expS[:, slot + 512:slot + 1024],
                              start=(kt == 0), stop=(kt == KT - 1),
                              tile_position=(0, 64),
                              skip_group_check=True).then_inc(s_av, 1)

                for g in range(NI // 2 + 1):
                    if 2 * g < NI:
                        emit_scores(2 * g)
                        emit_scores(2 * g + 1)
                    if g >= 1:
                        emit_av(2 * g - 2)
                        emit_av(2 * g - 1)
                for p in range(NPAIR):
                    t_.wait_ge(s_ev, evb + p + 1)
                    ins = t_.matmul(ps_sm[0:32, :], sel_sb[:, p * 32:(p + 1) * 32],
                                    attn[:, p * 512:(p + 1) * 512],
                                    start=(p == 0), stop=(p == NPAIR - 1),
                                    skip_group_check=True)
                ins.then_inc(sm, 1)
                for p in range(NPAIR):
                    t_.wait_ge(sv, P3_SV(qt) + 1 + 3 * p)
                    t_.matmul(ps_big[:, 0:512], ep_sb[0:32, p * 128:(p + 1) * 128],
                              rpk[0:32, :], start=True, stop=True,
                              skip_group_check=True).then_inc(sm, 1)
                for ts in range(4):
                    for half in range(2):
                        t_.wait_ge(sv, P3_SV(qt) + 49)
                        if ts >= 1 or half >= 1:
                            pidx = 2 * ts + half - 2
                            if pidx >= 0:
                                t_.wait_ge(sv, P3_SV(qt) + 49 + (pidx // 2) * 10 +
                                           (2 if pidx % 2 == 1 else 1))
                        ins = None
                        for p in range(NPAIR):
                            ins = t_.matmul(
                                ps_big[:, 1024 + half * 512:1024 + (half + 1) * 512],
                                attn[:, p * 512 + ts * 128:p * 512 + (ts + 1) * 128],
                                wout_sb[:, p * D + half * 512:p * D + (half + 1) * 512],
                                start=(p == 0), stop=(p == NPAIR - 1),
                                skip_group_check=True)
                        ins.then_inc(sm, 1)

        @block.scalar
        def _(sc):
            if ph >= 6:
                sc.wait_ge(ld, LD_NB)
            for qt in range(NQT if ph >= 6 else 0):
                for p in range(NPAIR):
                    base = 256 * qt + 16 * p
                    for kt in range(KT):
                        par = kt % 2
                        slot = (kt % 4) * 1024
                        sc.wait_ge(s_sc, base + kt + 1)
                        sc.wait_ge(s_av, max(0, base + kt - 3))
                        sc.activation(expS[:, slot:slot + 1024],
                                      ps_big[:, par * 1024:par * 1024 + 1024],
                                      AF.Exp, bias=biasB_sb[:, :],
                                      scale=SCALE).then_inc(s_xp, 1)
                for ts in range(4):
                    b3 = P3_SA(qt) + 3 * ts
                    sc.wait_ge(sv, P3_SV(qt) + 49 + 10 * ts + 2)
                    sc.activation(scr[:, :], spre[:, (ts % 2) * D:(ts % 2 + 1) * D],
                                  AF.Square, accum_out=st1[:, 2:3]).then_inc(sa, 1)
                    sc.wait_ge(sv, P3_SV(qt) + 49 + 10 * ts + 5)
                    sc.activation(st1[:, 7:8], st1[:, 5:6], AF.Ln,
                                  bias=eps_sb[:, 1:2], scale=1.0).then_inc(sa, 1)
                    sc.wait_ge(sa, b3 + 2)
                    sc.activation(st1[:, 6:7], st1[:, 7:8], AF.Exp,
                                  bias=0.0, scale=-0.5).then_inc(sa, 1)

        @block.vector
        def _(v):
            for qt in range(NQT if ph >= 6 else 0):
                bq = P3_SV(qt)
                evb = QT_EV(qt)
                for p in range(NPAIR):
                    v.wait_ge(s_av, 256 * qt + 16 * p + 16)
                    v.tensor_copy(attn[:, p * 512:(p + 1) * 512],
                                  ps_two[:, (p % 2) * 512:(p % 2) * 512 + 512]).then_inc(s_ev, 1)
                v.wait_ge(sm, P3_SM(qt) + 1)
                with nc.allow_low_precision(reason="1/denom bf16: uniform per-(h,q) scale"):
                    v.reciprocal(rpk[0:32, :], ps_sm[0:32, :]).then_inc(sv, 1)
                for p in range(NPAIR):
                    v.wait_ge(sm, P3_SM(qt) + 2 + p)
                    v.tensor_copy(rb_bf[:, :], ps_big[:, 0:512]).then_inc(sv, 1)
                    v.wait_ge(sv, bq + 1 + 3 * p + 1)
                    v.tensor_tensor(attn[0:64, p * 512:(p + 1) * 512],
                                    attn[0:64, p * 512:(p + 1) * 512],
                                    rb_bf[0:64, :], op=ALU.mult).then_inc(sv, 1)
                    v.tensor_tensor(attn[64:128, p * 512:(p + 1) * 512],
                                    attn[64:128, p * 512:(p + 1) * 512],
                                    rb_bf[64:128, :], op=ALU.mult).then_inc(sv, 1)
                for ts in range(4):
                    be = bq + 49 + 10 * ts
                    tok0 = qt * 512 + ts * 128
                    sl = (ts % 2)
                    v.wait_ge(sm, P3_SM(qt) + 17 + 2 * ts + 1)
                    if True:
                        pts = 4 * qt + ts - 2
                        if pts >= 0:
                            v.wait_ge(lx, LX_EOUT(pts // 4, pts % 4))
                    gts = 4 * qt + ts - 2
                    for half in range(2):
                        if half == 1:
                            v.wait_ge(sm, P3_SM(qt) + 17 + 2 * ts + 2)
                        if gts >= 0:
                            v.wait_ge(sa, P3_SA(gts // 4) + 3 * (gts % 4) + 1)
                        v.tensor_tensor(spre[:, sl * D + half * 512:sl * D + (half + 1) * 512],
                                        ps_big[:, 1024 + half * 512:1024 + (half + 1) * 512],
                                        bout_sb[:, half * 512:(half + 1) * 512],
                                        op=ALU.add).then_inc(sv, 1)
                    sp = spre[:, sl * D:(sl + 1) * D]
                    v.wait_ge(sv, be + 2)
                    v.tensor_scalar(scr[:, :], sp, 1.0 / D, 0.0, op0=ALU.mult,
                                    op1=ALU.add,
                                    accum_out=st1[:, 1:2]).then_inc(sv, 1)
                    v.wait_ge(sa, P3_SA(qt) + 3 * ts + 1)
                    v.wait_ge(sv, be + 3)
                    v.scalar_tensor_tensor(st1[:, 3:4], st1[:, 1:2], -1.0, st1[:, 1:2],
                                           op0=ALU.mult, op1=ALU.mult).then_inc(sv, 1)
                    v.wait_ge(sv, be + 4)
                    v.tensor_scalar(st1[:, 5:6], st1[:, 2:3], 1.0 / D, st1[:, 3:4],
                                    op0=ALU.mult, op1=ALU.add).then_inc(sv, 1)
                    v.wait_ge(sa, P3_SA(qt) + 3 * ts + 3)
                    v.tensor_scalar(scr2[:, :], sp, st1[:, 1:2], st1[:, 6:7],
                                    op0=ALU.subtract, op1=ALU.mult).then_inc(sv, 1)
                    v.wait_ge(sv, be + 6)
                    v.tensor_tensor(scr2[:, :], scr2[:, :], gam_sb[:, :],
                                    op=ALU.mult).then_inc(sv, 1)
                    v.wait_ge(sv, be + 7)
                    v.tensor_tensor(scr2[:, :], scr2[:, :], bet_sb[:, :],
                                    op=ALU.add).then_inc(sv, 1)
                    v.wait_ge(sv, be + 8)
                    v.wait_ge(lx, LX_EGG(qt, ts))
                    v.tensor_tensor(scr2[:, :], scr2[:, :],
                                    gst[:, sl * D:(sl + 1) * D],
                                    op=ALU.mult).then_inc(sv, 1)
                    v.wait_ge(sv, be + 9)
                    v.wait_ge(lx, LX_EXG(qt, ts))
                    v.tensor_tensor(spre[:, sl * D:(sl + 1) * D], scr2[:, :],
                                    xst[:, sl * D:(sl + 1) * D],
                                    op=ALU.add).then_inc(sv, 1)

        @block.sync
        def _(s):
            for qt in range(NQT if ph >= 6 else 0):
                for ts in range(4):
                    tok0 = qt * 512 + ts * 128
                    sl = ts % 2
                    pts = 4 * qt + ts - 2
                    if pts >= 0:
                        s.wait_ge(sv, P3_SV(pts // 4) + 49 + 10 * (pts % 4) + 10)
                    s.dma_start(out=xst[:, sl * D:(sl + 1) * D],
                                in_=x_d[tok0:tok0 + 128, :]).then_inc(lx, 16)
                    s.dma_start(out=gst[:, sl * D:(sl + 1) * D],
                                in_=g_d[tok0:tok0 + 128, :]).then_inc(lx, 16)
                for ts in range(4):
                    s.wait_ge(sv, P3_SV(qt) + 49 + 10 * ts + 10)
                    s.dma_start(out=out_d[qt * 512 + ts * 128:qt * 512 + (ts + 1) * 128, :],
                                in_=spre[:, (ts % 2) * D:(ts % 2 + 1) * D]).then_inc(lx, 16)
            if ph >= 6:
                s.wait_ge(lx, LX_TOT)
            else:
                tgt = {1: ("sv", P1_SV), 2: ("ld", LD_NB), 3: ("sv", SV_2B),
                       4: ("lx", LX2), 5: ("lx", LX3), 5.5: ("lx", LX4)}[ph]
                s.wait_ge({"sv": sv, "ld": ld, "lx": lx}[tgt[0]], tgt[1])
                s.dma_start(out=out_d[0:128, :], in_=xst[:, 0:D]).then_inc(lx, 16)
                s.wait_ge(lx, (LX1 if ph <= 3 else LX2 if ph == 4 else
                               LX3 if ph == 5 else LX4) + 16)
    return nc


def _host_consts():
    ident = np.eye(128, dtype=np.float32)
    e4 = np.zeros((128, 4), np.float32)
    for j in range(4):
        e4[32 * j:32 * (j + 1), j] = 1.0
    g8 = np.zeros((128, 128), np.float32)
    for e in range(8):
        for r in range(4):
            g8[e, 4 * e + r] = 1.0
    sel = np.zeros((128, NPAIR * 32), np.float32)
    for p in range(NPAIR):
        sel[32, p * 32 + 2 * p] = 1.0
        sel[96, p * 32 + 2 * p + 1] = 1.0
    ep = np.zeros((32, NPAIR * 128), np.float32)
    for p in range(NPAIR):
        ep[2 * p, p * 128 + 0:p * 128 + 32] = 1.0
        ep[2 * p + 1, p * 128 + 64:p * 128 + 96] = 1.0
    ones1 = np.ones((1, 128), np.float32)
    return ident, e4, g8, sel, ep, ones1


def kernel(L0_attn_x, L0_attn_w_qkv, L0_attn_w_out, L0_attn_b_out, L0_attn_ln_gamma,
           L0_attn_ln_beta, L0_ffn_W_lin, L0_ffn_b_lin, L0_ffn_gate_w, L0_ffn_left,
           L0_ffn_right):
    x = np.ascontiguousarray(np.asarray(L0_attn_x, dtype=np.float32))
    wqkv = np.ascontiguousarray(np.asarray(L0_attn_w_qkv, dtype=np.float32))
    wout = np.asarray(L0_attn_w_out, dtype=np.float32)
    bout = np.asarray(L0_attn_b_out, dtype=np.float32)
    gam = np.asarray(L0_attn_ln_gamma, dtype=np.float32)
    bet = np.asarray(L0_attn_ln_beta, dtype=np.float32)
    wlin = np.asarray(L0_ffn_W_lin, dtype=np.float32)
    blin = np.asarray(L0_ffn_b_lin, dtype=np.float32)
    gw = np.asarray(L0_ffn_gate_w, dtype=np.float32)
    left = np.asarray(L0_ffn_left, dtype=np.float32)
    right = np.asarray(L0_ffn_right, dtype=np.float32)

    wlinT = np.ascontiguousarray(wlin.T)
    gwT = np.ascontiguousarray(gw.T)
    leftf = np.ascontiguousarray(left.transpose(1, 0, 2).reshape(D, 32))
    rightp = np.zeros((128, D), np.float32)
    rightp[0:32] = right.reshape(32, D)
    wout_h = np.zeros((NPAIR * 128, D), np.float32)
    for p in range(NPAIR):
        wout_h[128 * p + 0:128 * p + 32] = wout[32 * (2 * p):32 * (2 * p) + 32]
        wout_h[128 * p + 64:128 * p + 96] = wout[32 * (2 * p + 1):32 * (2 * p + 1) + 32]
    rep = lambda a: np.ascontiguousarray(np.broadcast_to(a.reshape(1, D), (128, D)))
    ident, e4, g8, sel, ep, ones1 = _host_consts()

    nc = build_nc()
    eps_rep = np.ascontiguousarray(np.broadcast_to(
        np.array([EPS1, EPS2], np.float32).reshape(1, 2), (128, 2)))
    shared = dict(wqkv=wqkv, wlinT=wlinT, wout_h=wout_h, gwT=gwT, leftf=leftf,
                  rightp=rightp, blin_rep=rep(blin), bout_rep=rep(bout),
                  gam_rep=rep(gam), bet_rep=rep(bet), ident=ident, e4=e4,
                  g8pad=g8, sel=sel, epair=ep, ones1=ones1, eps_rep=eps_rep)
    in_maps = [dict(shared, x=np.ascontiguousarray(x[b])) for b in range(NCORES)]
    res = run_bass_kernel_spmd(nc, in_maps, core_ids=list(range(NCORES)))
    out = np.stack([res.results[b]["out"] for b in range(NCORES)], axis=0)
    return out.astype(np.float32)


# version 05
# speedup vs baseline: 1.6708x; 1.0287x over previous
import contextlib
import numpy as np
import ml_dtypes
import concourse.bass as bass
import concourse.mybir as mybir
from concourse.bass_utils import run_bass_kernel_spmd

F32 = mybir.dt.float32
BF16 = mybir.dt.bfloat16
AF = mybir.ActivationFunctionType
ALU = mybir.AluOpType
AX = mybir.AxisListType

NCORES = 8
T, D = 2048, 1024
NH, DH = 32, 32
NT = T // 128
NQT = T // 512
KT = T // 128
NPAIR = 16
SCALE = float(DH) ** -0.5
EPS1, EPS2 = 1e-5, 1e-6
BIG = 1e30


def build_nc(ph=6):
    nc = bass.Bass("TRN2", target_bir_lowering=False)
    dram = lambda name, shape, dtype=F32, kind="ExternalInput": nc.dram_tensor(
        name, shape, dtype, kind=kind).ap()
    x_d = dram("x", [T, D])
    wqkv_d = dram("wqkv", [D, 3 * D])
    wlinT_d = dram("wlinT", [D, D])
    wout_d = dram("wout_h", [NPAIR * 128, D])
    gwT_d = dram("gwT", [D, 8])
    leftf_d = dram("leftf", [D, 32])
    rightp_d = dram("rightp", [128, D])
    blin_d = dram("blin_rep", [128, D])
    bout_d = dram("bout_rep", [128, D])
    gam_d = dram("gam_rep", [128, D])
    bet_d = dram("bet_rep", [128, D])
    ident_d = dram("ident", [128, 128])
    e4_d = dram("e4", [128, 4])
    g8_d = dram("g8pad", [128, 128])
    sel_d = dram("sel", [128, NPAIR * 32])
    ep_d = dram("epair", [32, NPAIR * 128])
    ones1_d = dram("ones1", [1, 128])
    eps_d = dram("eps_rep", [128, 2])
    out_d = dram("out", [T, D], F32, kind="ExternalOutput")
    pre_d = dram("pre_scr", [T, D], BF16, kind="Internal")
    g_d = dram("g_scr", [T, D], BF16, kind="Internal")
    bnd_d = dram("bnd_scr", [4, 64], F32, kind="Internal")
    nb_d = dram("negB_scr", [128, 1], F32, kind="Internal")

    ctx = contextlib.ExitStack()
    with ctx:
        sb = lambda name, shape, dtype: ctx.enter_context(nc.sbuf_tensor(name, shape, dtype))
        psum = lambda name, shape, dtype=F32: ctx.enter_context(nc.psum_tensor(name, shape, dtype))

        q_sb = sb("q_sb", [128, 8 * T], BF16)
        k_sb = sb("k_sb", [128, 8 * T], BF16)
        v_sb = sb("v_sb", [128, KT * NH * 33], BF16)
        ident_sb = sb("ident_sb", [128, 128], BF16)
        sel_sb = sb("sel_sb", [128, NPAIR * 32], BF16)
        ep_sb = sb("ep_sb", [32, NPAIR * 128], BF16)
        e4_sb = sb("e4_sb", [128, 4], BF16)
        g8_sb = sb("g8_sb", [128, 128], BF16)
        st1 = sb("st1", [128, 16], F32)
        xst = sb("xst", [128, 2 * D], F32)
        scr = sb("scr", [128, D], BF16)
        scr2 = sb("scr2", [128, D], F32)
        xn_tm = sb("xn_tm", [128, 2 * D], BF16)
        prest = xn_tm
        blin_sb = sb("blin_sb", [128, D], F32)
        biasB_sb = sb("biasB_sb", [128, 1], F32)
        eps_sb = sb("eps_sb", [128, 2], F32)
        mqk = scr2[0:1, 256:512]
        ones1_sb = scr2[0:1, 512:640]
        bnd = scr2[0:4, 640:720]

        ps_big = psum("ps_big", [128, 2048])
        ps_two = psum("ps_two", [128, 1024])
        ps_sm = psum("ps_sm", [128, 512])
        ps_tr = psum("ps_tr", [128, 256], BF16)

        sem = lambda name: ctx.enter_context(nc.semaphore(name))
        ld = sem("ld")
        lx = sem("lx")
        sv = sem("sv")
        sa = sem("sa")
        sm = sem("sm")
        s_sc = sem("ssc")
        s_xp = sem("sxp")
        s_av = sem("sav")
        s_ev = sem("sev")

        block = ctx.enter_context(nc.Block())

        G0_DMAS = 8 * 4 + 6
        G0_LD = 16 * G0_DMAS
        LDW = lambda dc: G0_LD + 16 * (dc + 1)
        LD_2A = G0_LD + 16 * 16
        LD_BND1 = LD_2A + 16
        LD_BND = LD_2A + 16 * 3
        LD_NB1 = LD_BND + 16
        LD_NB = LD_BND + 32
        LD_WOUT = lambda i: LD_NB + 16 * (i + 1)
        LD_TOT = LD_NB + 16 * NPAIR

        LX_CONST = 16 * 3
        LX_X1 = lambda t: LX_CONST + 16 * (t + 1)
        LX1 = LX_CONST + 16 * NT
        LX_PST = lambda t: LX1 + 16 * (t + 1)
        LX2 = LX1 + 16 * NT
        _pr = {t: (t if t < 3 else 2 * t - 2) for t in range(NT)}
        _ps = {t: (3 + 2 * t if t <= NT - 4 else NT + t) for t in range(NT)}
        LX_PLD = lambda t: LX2 + 16 * (_pr[t] + 1)
        LX_GST = lambda t: LX2 + 16 * (_ps[t] + 1)
        LX3 = LX2 + 16 * NT
        LX4 = LX2 + 16 * 2 * NT
        LX_EPI = lambda qt: LX4 + 16 * 12 * qt
        LX_EXG = lambda qt, ts: LX_EPI(qt) + 16 * (2 * ts + 1)
        LX_EGG = lambda qt, ts: LX_EPI(qt) + 16 * (2 * ts + 2)
        LX_EOUT = lambda qt, ts: LX_EPI(qt) + 16 * 8 + 16 * (ts + 1)
        LX_TOT = LX4 + 16 * 12 * NQT

        P1_SV, P1_SA, P1_SM = 12 * NT, 3 * NT, 8 * NT
        SV_2A = P1_SV + 12 * 16
        P2A_SVD = lambda dc: P1_SV + 12 * dc
        P2A_SM = lambda dc: P1_SM + 5 * dc
        SM_2A = P1_SM + 5 * 16
        SA_BF = P1_SA + 2
        SV_BF = SV_2A + 4
        P2B_SV = lambda t: SV_BF + 12 * t
        P2B_SM = lambda t: SM_2A + 2 * t
        P2B_SA = lambda t: SA_BF + t
        SV_2B, SM_2B, SA_2B = SV_BF + 12 * NT, SM_2A + 2 * NT, SA_BF + NT
        P2C_SV = lambda qt: SV_2B + 2 * qt
        P2C_SM = lambda qt: SM_2B + 2 * qt
        SV_2C, SM_2C = SV_2B + 2 * NQT, SM_2B + 2 * NQT
        P2D_SV = lambda t: SV_2C + 2 * t
        P2D_SM = lambda t: SM_2C + 2 * t
        SV_2D, SM_2D = SV_2C + 2 * NT, SM_2C + 2 * NT
        SA_SIL = lambda t: SA_2B + t + 1
        SA_3P = SA_2B + NT
        QT_SC = lambda qt: 256 * qt
        QT_AV = lambda qt: 256 * qt
        QT_XP = lambda qt: 256 * qt
        QT_EV = lambda qt: 16 * qt
        P3_SV = lambda qt: SV_2D + 89 * qt
        P3_SM = lambda qt: SM_2D + 25 * qt
        P3_SA = lambda qt: SA_3P + 12 * qt
        SV_TOT = SV_2D + 89 * NQT
        SM_TOT = SM_2D + 25 * NQT
        SA_TOT = SA_3P + 12 * NQT

        @block.gpsimd
        def _(g):
            n = 0
            for dst, src in [(ident_sb[:, :], ident_d), (e4_sb[:, :], e4_d),
                             (g8_sb[:, :], g8_d), (sel_sb[:, :], sel_d),
                             (ep_sb[0:32, :], ep_d)]:
                g.dma_start(out=dst, in_=src).then_inc(ld, 16)
                n += 1
            g.memset(v_sb[:, :], 1.0)
            assert n == 5

        @block.sync
        def _(s):
            for dst, src in [(blin_sb[:, :], blin_d), (ones1_sb[:, :], ones1_d),
                             (eps_sb[:, :], eps_d)]:
                s.dma_start(out=dst, in_=src).then_inc(lx, 16)
            for t in range(NT):
                if t >= 2:
                    s.wait_ge(sv, 12 * (t - 2) + 4)
                s.dma_start(out=xst[:, (t % 2) * D:(t % 2 + 1) * D],
                            in_=x_d[t * 128:(t + 1) * 128, :]).then_inc(lx, 16)

        ctx2 = ctx.enter_context(contextlib.ExitStack())
        sb2 = lambda name, shape, dtype: ctx2.enter_context(nc.sbuf_tensor(name, shape, dtype))
        xn_sb = sb2("xn_sb", [128, 8 * T], BF16)
        wv_sb = sb2("wv_sb", [128, 8 * D], BF16)
        wlin_sb = sb2("wlin_sb", [128, 8 * D], BF16)
        gwT_sb = sb2("gwT_sb", [128, 64], BF16)
        leftf_sb = sb2("leftf_sb", [128, 256], BF16)
        rightp_sb = sb2("rightp_sb", [128, D], BF16)
        wq_st = sb2("wq_st", [128, 2 * 1024], BF16)
        sqs = sb2("sqs", [128, 512], BF16)
        gates_bf = sb2("gates_bf", [128, NT * 8], BF16)
        gpad = sb2("gpad", [128, 128], BF16)
        gfm_sb = sb2("gfm_sb", [128, NT * 128], BF16)
        t_tmp = sb2("t_tmp", [32, 512], BF16)
        t1pad = sb2("t1pad", [128, T], BF16)

        @block.gpsimd
        def _(g):
            for c in range(8):
                g.dma_start(out=wv_sb[:, c * D:(c + 1) * D],
                            in_=wqkv_d[c * 128:(c + 1) * 128, 2 * D:3 * D]).then_inc(ld, 16)
                g.dma_start(out=wlin_sb[:, c * D:(c + 1) * D],
                            in_=wlinT_d[c * 128:(c + 1) * 128, :]).then_inc(ld, 16)
                g.dma_start(out=gwT_sb[:, c * 8:(c + 1) * 8],
                            in_=gwT_d[c * 128:(c + 1) * 128, :]).then_inc(ld, 16)
                g.dma_start(out=leftf_sb[:, c * 32:(c + 1) * 32],
                            in_=leftf_d[c * 128:(c + 1) * 128, :]).then_inc(ld, 16)
            g.dma_start(out=rightp_sb[:, :], in_=rightp_d).then_inc(ld, 16)
            g.memset(t1pad[:, :], 0.0)
            g.memset(gpad[:, :], 0.0)
            for dc in range(16 if ph >= 2 else 0):
                if dc >= 2:
                    g.wait_ge(sm, P2A_SM(dc - 1) + 1)
                g.dma_start(
                    out=wq_st[:, (dc % 2) * 1024:(dc % 2 + 1) * 1024].rearrange(
                        "p (c m) -> p c m", c=8),
                    in_=wqkv_d[:, dc * 128:(dc + 1) * 128].rearrange(
                        "(c p) m -> p c m", p=128)).then_inc(ld, 16)

        @block.vector
        def _(v):
            for t in range(NT):
                b = 12 * t
                xt = xst[:, (t % 2) * D:(t % 2 + 1) * D]
                v.wait_ge(lx, LX_X1(t))
                if t >= 2:
                    v.wait_ge(sm, 8 * (t - 2) + 8)
                v.tensor_scalar(xn_tm[:, (t % 2) * D:(t % 2 + 1) * D], xt, 1.0 / D,
                                0.0, op0=ALU.mult, op1=ALU.add,
                                accum_out=st1[:, 1:2]).then_inc(sv, 1)
                v.wait_ge(sa, 3 * t + 1)
                v.wait_ge(sv, b + 1)
                v.scalar_tensor_tensor(st1[:, 3:4], st1[:, 1:2], -1.0, st1[:, 1:2],
                                       op0=ALU.mult, op1=ALU.mult).then_inc(sv, 1)
                v.wait_ge(sv, b + 2)
                if t >= 1:
                    v.wait_ge(sa, 3 * (t - 1) + 2)
                v.tensor_scalar(st1[:, 5:6], st1[:, 2:3], 1.0 / D, st1[:, 3:4],
                                op0=ALU.mult, op1=ALU.add).then_inc(sv, 1)
                v.wait_ge(sa, 3 * t + 3)
                v.tensor_scalar(xn_tm[:, (t % 2) * D:(t % 2 + 1) * D], xt,
                                st1[:, 1:2], st1[:, 6:7],
                                op0=ALU.subtract, op1=ALU.mult).then_inc(sv, 1)
                for c in range(8):
                    v.wait_ge(sm, 8 * t + c + 1)
                    v.tensor_copy(xn_sb[:, c * T + t * 128:c * T + (t + 1) * 128],
                                  ps_tr[:, 0:128]).then_inc(sv, 1)

        @block.scalar
        def _(sc):
            for t in range(NT):
                xt = xst[:, (t % 2) * D:(t % 2 + 1) * D]
                sc.wait_ge(lx, LX_X1(t))
                if t >= 1:
                    sc.wait_ge(sv, 12 * (t - 1) + 3)
                sc.activation(scr[:, :], xt, AF.Square,
                              accum_out=st1[:, 2:3]).then_inc(sa, 1)
                sc.wait_ge(sv, 12 * t + 3)
                sc.activation(st1[:, 7:8], st1[:, 5:6], AF.Ln,
                              bias=eps_sb[:, 0:1], scale=1.0).then_inc(sa, 1)
                sc.wait_ge(sa, 3 * t + 2)
                if t >= 1:
                    sc.wait_ge(sv, 12 * (t - 1) + 4)
                sc.activation(st1[:, 6:7], st1[:, 7:8], AF.Exp,
                              bias=0.0, scale=-0.5).then_inc(sa, 1)

        @block.tensor
        def _(t_):
            t_.wait_ge(ld, 16 * 5)
            for t in range(NT):
                for c in range(8):
                    t_.wait_ge(sv, 12 * t + 4)
                    prev = 8 * t + c - 1
                    if prev >= 0:
                        pt, pc = prev // 8, prev % 8
                        t_.wait_ge(sv, 12 * pt + 4 + pc + 1)
                    t_.transpose(ps_tr[:, 0:128],
                                 xn_tm[:, (t % 2) * D + c * 128:(t % 2) * D + (c + 1) * 128],
                                 ident_sb[:, :]).then_inc(sm, 1)

        @block.tensor
        def _(t_):
            for dc in range(16 if ph >= 2 else 0):
                t_.wait_ge(ld, LDW(dc))
                if dc >= 1:
                    t_.wait_ge(sv, P2A_SVD(dc - 1) + 4)
                ins = None
                for c in range(8):
                    for qt in range(NQT):
                        ins = t_.matmul(
                            ps_big[:, qt * 512:(qt + 1) * 512],
                            wq_st[:, (dc % 2) * 1024 + c * 128:(dc % 2) * 1024 + (c + 1) * 128],
                            xn_sb[:, c * T + qt * 512:c * T + (qt + 1) * 512],
                            start=(c == 0), stop=(c == 7), skip_group_check=True)
                ins.then_inc(sm, 1)
                for qt in range(NQT):
                    t_.wait_ge(sv, P2A_SVD(dc) + 5 + 2 * qt)
                    t_.matmul(ps_sm[0:4, 0:512], e4_sb[:, :], sqs[:, :],
                              start=True, stop=True,
                              skip_group_check=True).then_inc(sm, 1)

        @block.vector
        def _(v):
            for dc in range(16 if ph >= 2 else 0):
                dst = q_sb if dc < 8 else k_sb
                cc = dc % 8
                base = P2A_SVD(dc)
                v.wait_ge(sm, P2A_SM(dc) + 1)
                for qt in range(NQT):
                    v.tensor_copy(dst[:, cc * T + qt * 512:cc * T + (qt + 1) * 512],
                                  ps_big[:, qt * 512:(qt + 1) * 512]).then_inc(sv, 1)
                for qt in range(NQT):
                    v.wait_ge(sv, base + 4)
                    v.wait_ge(sm, P2A_SM(dc) + 1 + qt)
                    src = dst[:, cc * T + qt * 512:cc * T + (qt + 1) * 512]
                    v.tensor_tensor(sqs[:, :], src, src, op=ALU.mult).then_inc(sv, 1)
                    v.wait_ge(sm, P2A_SM(dc) + 1 + qt + 1)
                    v.reduce_max(bnd[0:4, dc * 4 + qt:dc * 4 + qt + 1],
                                 ps_sm[0:4, 0:512], axis=AX.X).then_inc(sv, 1)

        @block.gpsimd
        def _(g):
            if ph < 2:
                return
            g.wait_ge(sv, SV_2A)
            g.dma_start(out=bnd_d, in_=bnd[0:4, 0:64]).then_inc(ld, 16)
            g.wait_ge(ld, LD_BND1)
            g.dma_start(out=mqk[:, 0:128].rearrange("a (p c) -> a p c", p=4),
                        in_=bnd_d[:, 0:32]).then_inc(ld, 16)
            g.dma_start(out=mqk[:, 128:256].rearrange("a (p c) -> a p c", p=4),
                        in_=bnd_d[:, 32:64]).then_inc(ld, 16)

        @block.vector
        def _(v):
            if ph < 2:
                return
            v.wait_ge(ld, LD_BND)
            v.reduce_max(st1[0:1, 8:9], mqk[:, 0:128], axis=AX.X)
            v.reduce_max(st1[0:1, 9:10], mqk[:, 128:256], axis=AX.X).then_inc(sv, 1)
            v.wait_ge(sv, SV_2A + 1)
            v.tensor_tensor(st1[0:1, 10:11], st1[0:1, 8:9], st1[0:1, 9:10],
                            op=ALU.mult).then_inc(sv, 1)
            v.wait_ge(sa, SA_BF)
            v.tensor_scalar(st1[0:1, 12:13], st1[0:1, 11:12], -SCALE * 1.02, None,
                            op0=ALU.mult).then_inc(sv, 1)
            v.wait_ge(sv, SV_2A + 3)
            v.tensor_scalar(mqk[:, 0:128], ones1_sb[0:1, :], st1[0:1, 12:13], None,
                            op0=ALU.mult).then_inc(sv, 1)

        @block.scalar
        def _(sc):
            if ph < 2:
                return
            sc.wait_ge(sv, SV_2A + 2)
            sc.activation(st1[0:1, 13:14], st1[0:1, 10:11], AF.Ln,
                          bias=0.0, scale=1.0).then_inc(sa, 1)
            sc.wait_ge(sa, P1_SA + 1)
            sc.activation(st1[0:1, 11:12], st1[0:1, 13:14], AF.Exp,
                          bias=0.0, scale=0.5).then_inc(sa, 1)

        @block.gpsimd
        def _(g):
            if ph < 2:
                return
            g.wait_ge(sv, SV_BF)
            g.dma_start(out=nb_d[:, :].rearrange("p a -> (p a)").rearrange(
                "(a q) -> a q", a=1), in_=mqk[:, 0:128]).then_inc(ld, 16)
            g.wait_ge(ld, LD_NB1)
            g.dma_start(out=biasB_sb[:, :], in_=nb_d).then_inc(ld, 16)

        @block.tensor
        def _(t_):
            for t in range(NT if ph >= 3 else 0):
                if t == 0:
                    t_.wait_ge(sv, SV_2A)
                if t >= 1:
                    t_.wait_ge(sv, P2B_SV(t - 1) + 7)
                    t_.wait_ge(sa, P2B_SA(t - 1) + 1)
                ins = None
                for c in range(8):
                    xnc = xn_sb[:, c * T + t * 128:c * T + (t + 1) * 128]
                    t_.matmul(ps_two[:, 0:512], xnc, wv_sb[:, c * D:c * D + 512],
                              start=(c == 0), stop=(c == 7), skip_group_check=True)
                    t_.matmul(ps_two[:, 512:1024], xnc, wv_sb[:, c * D + 512:(c + 1) * D],
                              start=(c == 0), stop=(c == 7), skip_group_check=True)
                    ins = t_.matmul(ps_sm[:, 0:8], xnc, gwT_sb[:, c * 8:(c + 1) * 8],
                                    start=(c == 0), stop=(c == 7), skip_group_check=True)
                ins.then_inc(sm, 1)
                t_.wait_ge(sv, P2B_SV(t) + 11)
                if t >= 1:
                    t_.wait_ge(sv, P2B_SV(t - 1) + 12)
                t_.transpose(ps_tr[:, 128:256], gpad[:, :], ident_sb[:, :]).then_inc(sm, 1)

        @block.scalar
        def _(sc):
            for t in range(NT if ph >= 3 else 0):
                sc.wait_ge(sv, P2B_SV(t) + 2)
                sc.activation(scr2[:, 0:8], ps_sm[:, 0:8], AF.Exp, bias=st1[:, 8:9],
                              accum_out=st1[:, 9:10]).then_inc(sa, 1)

        @block.vector
        def _(v):
            for t in range(NT if ph >= 3 else 0):
                b = P2B_SV(t)
                v.wait_ge(sm, P2B_SM(t) + 1)
                v.tensor_copy(
                    v_sb[:, t * NH * 33:(t + 1) * NH * 33].rearrange(
                        "p (h w) -> p h w", h=NH)[:, :, 0:32],
                    ps_two[:, :].rearrange("p (h w) -> p h w", h=NH)).then_inc(sv, 1)
                v.reduce_max(st1[:, 8:9], ps_sm[:, 0:8], axis=AX.X,
                             negate=True).then_inc(sv, 1)
                v.wait_ge(sv, b + 2)
                v.tensor_scalar(st1[:, 10:11], st1[:, 8:9], -1.0, None,
                                op0=ALU.mult).then_inc(sv, 1)
                v.wait_ge(sv, b + 3)
                v.tensor_scalar(scr2[:, 8:16], ps_sm[:, 0:8], st1[:, 10:11], None,
                                op0=ALU.is_ge).then_inc(sv, 1)
                v.wait_ge(sv, b + 4)
                v.scalar_tensor_tensor(scr2[:, 16:24], scr2[:, 8:16], -BIG,
                                       ps_sm[:, 0:8], op0=ALU.mult,
                                       op1=ALU.add).then_inc(sv, 1)
                v.wait_ge(sv, b + 5)
                v.reduce_max(st1[:, 11:12], scr2[:, 16:24], axis=AX.X).then_inc(sv, 1)
                v.wait_ge(sv, b + 6)
                v.tensor_scalar(scr2[:, 24:32], ps_sm[:, 0:8], st1[:, 11:12], None,
                                op0=ALU.is_ge).then_inc(sv, 1)
                v.wait_ge(sa, P2B_SA(t) + 1)
                with nc.allow_low_precision(reason="tiny gate sums"):
                    v.reciprocal(st1[:, 12:13], st1[:, 9:10]).then_inc(sv, 1)
                v.wait_ge(sv, b + 8)
                v.scalar_tensor_tensor(gates_bf[:, t * 8:(t + 1) * 8], scr2[:, 0:8],
                                       1.0, scr2[:, 24:32], op0=ALU.mult,
                                       op1=ALU.mult).then_inc(sv, 1)
                v.wait_ge(sv, b + 9)
                v.tensor_scalar(gates_bf[:, t * 8:(t + 1) * 8],
                                gates_bf[:, t * 8:(t + 1) * 8], st1[:, 12:13], None,
                                op0=ALU.mult).then_inc(sv, 1)
                v.wait_ge(sv, b + 10)
                if t >= 1:
                    v.wait_ge(sm, P2B_SM(t - 1) + 2)
                v.tensor_copy(gpad[:, 0:8],
                              gates_bf[:, t * 8:(t + 1) * 8]).then_inc(sv, 1)
                v.wait_ge(sm, P2B_SM(t) + 2)
                v.tensor_copy(gfm_sb[:, t * 128:(t + 1) * 128],
                              ps_tr[:, 128:256]).then_inc(sv, 1)

        @block.tensor
        def _(t_):
            for qt in range(NQT if ph >= 4 else 0):
                if qt == 0:
                    t_.wait_ge(sv, SV_2B)
                    t_.wait_ge(sa, SA_2B)
                else:
                    t_.wait_ge(sv, P2C_SV(qt - 1) + 1)
                ins = None
                for c in range(8):
                    ins = t_.matmul(ps_sm[0:32, :], leftf_sb[:, c * 32:(c + 1) * 32],
                                    xn_sb[:, c * T + qt * 512:c * T + (qt + 1) * 512],
                                    start=(c == 0), stop=(c == 7), skip_group_check=True)
                ins.then_inc(sm, 1)
                if qt >= 1:
                    t_.wait_ge(sv, P2C_SV(qt - 1) + 1)
                t_.matmul(ps_two[:, 0:512], g8_sb[:, :],
                          gfm_sb[:, qt * 512:(qt + 1) * 512],
                          start=True, stop=True, skip_group_check=True).then_inc(sm, 1)

        @block.vector
        def _(v):
            for qt in range(NQT if ph >= 4 else 0):
                b = P2C_SV(qt)
                v.wait_ge(sm, P2C_SM(qt) + 1)
                v.tensor_copy(t_tmp[0:32, :], ps_sm[0:32, :]).then_inc(sv, 1)
                v.wait_ge(sv, b + 1)
                v.wait_ge(sm, P2C_SM(qt) + 2)
                v.tensor_tensor(t1pad[0:32, qt * 512:(qt + 1) * 512],
                                ps_two[0:32, 0:512], t_tmp[0:32, :],
                                op=ALU.mult).then_inc(sv, 1)

        @block.tensor
        def _(t_):
            for t in range(NT if ph >= 4 else 0):
                t_.wait_ge(sv, P2C_SV(t // 4) + 2)
                if t >= 2:
                    t_.wait_ge(sv, P2D_SV(t - 2) + 2)
                for half in range(2):
                    hb = (t % 2) * 1024
                    ins = None
                    for c in range(8):
                        ins = t_.matmul(
                            ps_big[:, hb + half * 512:hb + (half + 1) * 512],
                            xn_sb[:, c * T + t * 128:c * T + (t + 1) * 128],
                            wlin_sb[:, c * D + half * 512:c * D + (half + 1) * 512],
                            start=(c == 0), stop=False, skip_group_check=True)
                    ins = t_.matmul(
                        ps_big[:, hb + half * 512:hb + (half + 1) * 512],
                        t1pad[:, t * 128:(t + 1) * 128],
                        rightp_sb[:, half * 512:(half + 1) * 512],
                        start=False, stop=True, skip_group_check=True)
                    ins.then_inc(sm, 1)

        @block.vector
        def _(v):
            for t in range(NT if ph >= 4 else 0):
                b = P2D_SV(t)
                hb = (t % 2) * 1024
                for half in range(2):
                    v.wait_ge(sm, P2D_SM(t) + half + 1)
                    if t >= 2:
                        v.wait_ge(lx, LX_PST(t - 2))
                    v.tensor_tensor(prest[:, (t % 2) * D + half * 512:(t % 2) * D + (half + 1) * 512],
                                    ps_big[:, hb + half * 512:hb + (half + 1) * 512],
                                    blin_sb[:, half * 512:(half + 1) * 512],
                                    op=ALU.add).then_inc(sv, 1)

        @block.sync
        def _(s):
            for t in range(NT if ph >= 4 else 0):
                s.wait_ge(sv, P2D_SV(t) + 2)
                s.dma_start(out=pre_d[t * 128:(t + 1) * 128, :],
                            in_=prest[:, (t % 2) * D:(t % 2 + 1) * D]).then_inc(lx, 16)

        ctx2.close()
        ctx3 = ctx.enter_context(contextlib.ExitStack())
        sb3 = lambda name, shape, dtype: ctx3.enter_context(nc.sbuf_tensor(name, shape, dtype))
        wout_sb = sb3("wout_sb", [128, NPAIR * D], BF16)
        attn = sb3("attn", [128, NPAIR * 512], BF16)
        expS = sb3("expS", [128, 4 * 1024], BF16)
        rpk = sb3("rpk", [32, 512], BF16)
        rb_bf = sb3("rb_bf", [128, 2 * 512], BF16)
        gst = sb3("gst", [128, 2 * D], BF16)
        spre = sb3("spre", [128, 2 * D], F32)
        bout_sb = sb3("bout_sb", [128, D], F32)
        gam_sb = sb3("gam_sb", [128, D], F32)
        bet_sb = sb3("bet_sb", [128, D], F32)

        @block.gpsimd
        def _(g):
            if ph >= 5:
                g.wait_ge(sm, SM_2D)
                g.wait_ge(sv, SV_2D)
            for p in range(NPAIR if ph >= 5 else 0):
                g.dma_start(out=wout_sb[:, p * D:(p + 1) * D],
                            in_=wout_d[p * 128:(p + 1) * 128, :]).then_inc(ld, 16)

        @block.sync
        def _(s):
            for dst, src in [(bout_sb[:, :], bout_d), (gam_sb[:, :], gam_d),
                             (bet_sb[:, :], bet_d)]:
                s.dma_start(out=dst, in_=src).then_inc(lx, 16) if False else None
            if ph >= 5.5:
                order = [("r", 0), ("r", 1), ("r", 2)]
                for k in range(NT - 3):
                    order += [("s", k), ("r", k + 3)]
                order += [("s", NT - 3), ("s", NT - 2), ("s", NT - 1)]
                for kind, t in order:
                    if kind == "r":
                        s.wait_ge(lx, LX_PST(NT - 1))
                        if t >= 2:
                            s.wait_ge(sa, SA_SIL(t - 2))
                        s.dma_start(out=prest[:, (t % 2) * D:(t % 2 + 1) * D],
                                    in_=pre_d[t * 128:(t + 1) * 128, :]).then_inc(lx, 16)
                    else:
                        s.wait_ge(sa, SA_SIL(t))
                        s.dma_start(out=g_d[t * 128:(t + 1) * 128, :],
                                    in_=gst[:, (t % 2) * D:(t % 2 + 1) * D]).then_inc(lx, 16)
            elif ph == 5:
                for t in range(NT):
                    s.wait_ge(lx, LX_PST(NT - 1))
                    s.dma_start(out=prest[:, (t % 2) * D:(t % 2 + 1) * D],
                                in_=pre_d[t * 128:(t + 1) * 128, :]).then_inc(lx, 16)

        @block.gpsimd
        def _(g):
            for dst, src in ([(bout_sb[:, :], bout_d), (gam_sb[:, :], gam_d),
                             (bet_sb[:, :], bet_d)] if ph >= 5 else []):
                g.dma_start(out=dst, in_=src).then_inc(ld, 16)
        LD_TOT2 = LD_TOT + 16 * 3

        @block.scalar
        def _(sc):
            for t in range(NT if ph >= 5.5 else 0):
                sc.wait_ge(lx, LX_PLD(t))
                if t >= 2:
                    sc.wait_ge(lx, LX_GST(t - 2))
                sc.activation(gst[:, (t % 2) * D:(t % 2 + 1) * D],
                              prest[:, (t % 2) * D:(t % 2 + 1) * D],
                              AF.Silu).then_inc(sa, 1)

        @block.tensor
        def _(t_):
            if ph >= 6:
                t_.wait_ge(ld, LD_TOT2)
            for qt in range(NQT if ph >= 6 else 0):
                scb, avb, xpb, evb = QT_SC(qt), QT_AV(qt), QT_XP(qt), QT_EV(qt)
                if qt > 0:
                    t_.wait_ge(sv, P3_SV(qt))
                NI = NPAIR * KT

                def emit_scores(i):
                    p, kt = i // KT, i % KT
                    cp, jp = p // 2, p % 2
                    sA, sB = 64 * jp, 64 * jp + 32
                    base = 256 * qt + 16 * p
                    par = kt % 2
                    t_.wait_ge(s_xp, max(0, base + kt - 1))
                    t_.matmul(ps_big[:, par * 1024:par * 1024 + 512],
                              k_sb[sA:sA + 32, cp * T + kt * 128:cp * T + (kt + 1) * 128],
                              q_sb[sA:sA + 32, cp * T + qt * 512:cp * T + (qt + 1) * 512],
                              start=True, stop=True, tile_position=(sA, 0),
                              skip_group_check=True)
                    t_.matmul(ps_big[:, par * 1024 + 512:par * 1024 + 1024],
                              k_sb[sB:sB + 32, cp * T + kt * 128:cp * T + (kt + 1) * 128],
                              q_sb[sB:sB + 32, cp * T + qt * 512:cp * T + (qt + 1) * 512],
                              start=True, stop=True, tile_position=(sB, 0),
                              skip_group_check=True).then_inc(s_sc, 1)

                def emit_av(j):
                    p, kt = j // KT, j % KT
                    cp, jp = p // 2, p % 2
                    hA, hB = 4 * cp + 2 * jp, 4 * cp + 2 * jp + 1
                    base = 256 * qt + 16 * p
                    slot = (kt % 4) * 1024
                    ub = (p % 2) * 512
                    t_.wait_ge(s_xp, base + kt + 1)
                    if p >= 2 and kt == 0:
                        t_.wait_ge(s_ev, evb + p - 1)
                    if qt > 0 and p < 2 and kt == 0:
                        t_.wait_ge(s_ev, evb)
                    t_.matmul(ps_two[0:33, ub:ub + 512],
                              v_sb[:, kt * NH * 33 + hA * 33:kt * NH * 33 + hA * 33 + 33],
                              expS[:, slot:slot + 512],
                              start=(kt == 0), stop=(kt == KT - 1),
                              tile_position=(0, 0), skip_group_check=True)
                    t_.matmul(ps_two[64:97, ub:ub + 512],
                              v_sb[:, kt * NH * 33 + hB * 33:kt * NH * 33 + hB * 33 + 33],
                              expS[:, slot + 512:slot + 1024],
                              start=(kt == 0), stop=(kt == KT - 1),
                              tile_position=(0, 64),
                              skip_group_check=True).then_inc(s_av, 1)

                for g in range(NI // 2 + 1):
                    if 2 * g < NI:
                        emit_scores(2 * g)
                        emit_scores(2 * g + 1)
                    if g >= 1:
                        emit_av(2 * g - 2)
                        emit_av(2 * g - 1)
                for p in range(NPAIR):
                    t_.wait_ge(s_ev, evb + p + 1)
                    ins = t_.matmul(ps_sm[0:32, :], sel_sb[:, p * 32:(p + 1) * 32],
                                    attn[:, p * 512:(p + 1) * 512],
                                    start=(p == 0), stop=(p == NPAIR - 1),
                                    skip_group_check=True)
                ins.then_inc(sm, 1)
                for p in range(NPAIR):
                    t_.wait_ge(sv, P3_SV(qt) + 1 + max(0, 3 * (p - 2) + 1) if p >= 2
                               else P3_SV(qt) + 1)
                    t_.matmul(ps_big[:, (p % 2) * 512:(p % 2) * 512 + 512],
                              ep_sb[0:32, p * 128:(p + 1) * 128],
                              rpk[0:32, :], start=True, stop=True,
                              skip_group_check=True).then_inc(sm, 1)
                for ts in range(4):
                    for half in range(2):
                        if ts >= 1 or half >= 1:
                            pidx = 2 * ts + half - 2
                            if pidx >= 0:
                                t_.wait_ge(sv, P3_SV(qt) + 49 + (pidx // 2) * 10 +
                                           (2 if pidx % 2 == 1 else 1))
                        ins = None
                        for p in range(NPAIR):
                            t_.wait_ge(sv, P3_SV(qt) + 1 + 3 * p + 3)
                            ins = t_.matmul(
                                ps_big[:, 1024 + half * 512:1024 + (half + 1) * 512],
                                attn[:, p * 512 + ts * 128:p * 512 + (ts + 1) * 128],
                                wout_sb[:, p * D + half * 512:p * D + (half + 1) * 512],
                                start=(p == 0), stop=(p == NPAIR - 1),
                                skip_group_check=True)
                        ins.then_inc(sm, 1)

        @block.scalar
        def _(sc):
            if ph >= 6:
                sc.wait_ge(ld, LD_NB)
            for qt in range(NQT if ph >= 6 else 0):
                for p in range(NPAIR):
                    base = 256 * qt + 16 * p
                    for kt in range(KT):
                        par = kt % 2
                        slot = (kt % 4) * 1024
                        sc.wait_ge(s_sc, base + kt + 1)
                        sc.wait_ge(s_av, max(0, base + kt - 3))
                        sc.activation(expS[:, slot:slot + 1024],
                                      ps_big[:, par * 1024:par * 1024 + 1024],
                                      AF.Exp, bias=biasB_sb[:, :],
                                      scale=SCALE).then_inc(s_xp, 1)
                for ts in range(4):
                    b3 = P3_SA(qt) + 3 * ts
                    sc.wait_ge(sv, P3_SV(qt) + 49 + 10 * ts + 2)
                    sc.activation(scr[:, :], spre[:, (ts % 2) * D:(ts % 2 + 1) * D],
                                  AF.Square, accum_out=st1[:, 2:3]).then_inc(sa, 1)
                    sc.wait_ge(sv, P3_SV(qt) + 49 + 10 * ts + 5)
                    sc.activation(st1[:, 7:8], st1[:, 5:6], AF.Ln,
                                  bias=eps_sb[:, 1:2], scale=1.0).then_inc(sa, 1)
                    sc.wait_ge(sa, b3 + 2)
                    sc.activation(st1[:, 6:7], st1[:, 7:8], AF.Exp,
                                  bias=0.0, scale=-0.5).then_inc(sa, 1)

        @block.vector
        def _(v):
            for qt in range(NQT if ph >= 6 else 0):
                bq = P3_SV(qt)
                evb = QT_EV(qt)
                for p in range(NPAIR):
                    v.wait_ge(s_av, 256 * qt + 16 * p + 16)
                    v.tensor_copy(attn[:, p * 512:(p + 1) * 512],
                                  ps_two[:, (p % 2) * 512:(p % 2) * 512 + 512]).then_inc(s_ev, 1)
                v.wait_ge(sm, P3_SM(qt) + 1)
                with nc.allow_low_precision(reason="1/denom bf16: uniform per-(h,q) scale"):
                    v.reciprocal(rpk[0:32, :], ps_sm[0:32, :]).then_inc(sv, 1)
                for p in range(NPAIR):
                    sl2 = (p % 2) * 512
                    v.wait_ge(sm, P3_SM(qt) + 2 + p)
                    v.tensor_copy(rb_bf[:, sl2:sl2 + 512],
                                  ps_big[:, sl2:sl2 + 512]).then_inc(sv, 1)
                    v.wait_ge(sv, bq + 1 + 3 * p + 1)
                    v.tensor_tensor(attn[0:64, p * 512:(p + 1) * 512],
                                    attn[0:64, p * 512:(p + 1) * 512],
                                    rb_bf[0:64, sl2:sl2 + 512], op=ALU.mult).then_inc(sv, 1)
                    v.tensor_tensor(attn[64:128, p * 512:(p + 1) * 512],
                                    attn[64:128, p * 512:(p + 1) * 512],
                                    rb_bf[64:128, sl2:sl2 + 512], op=ALU.mult).then_inc(sv, 1)
                for ts in range(4):
                    be = bq + 49 + 10 * ts
                    tok0 = qt * 512 + ts * 128
                    sl = (ts % 2)
                    v.wait_ge(sm, P3_SM(qt) + 17 + 2 * ts + 1)
                    if True:
                        pts = 4 * qt + ts - 2
                        if pts >= 0:
                            v.wait_ge(lx, LX_EOUT(pts // 4, pts % 4))
                    gts = 4 * qt + ts - 2
                    for half in range(2):
                        if half == 1:
                            v.wait_ge(sm, P3_SM(qt) + 17 + 2 * ts + 2)
                        if gts >= 0:
                            v.wait_ge(sa, P3_SA(gts // 4) + 3 * (gts % 4) + 1)
                        v.tensor_tensor(spre[:, sl * D + half * 512:sl * D + (half + 1) * 512],
                                        ps_big[:, 1024 + half * 512:1024 + (half + 1) * 512],
                                        bout_sb[:, half * 512:(half + 1) * 512],
                                        op=ALU.add).then_inc(sv, 1)
                    sp = spre[:, sl * D:(sl + 1) * D]
                    v.wait_ge(sv, be + 2)
                    v.tensor_scalar(scr[:, :], sp, 1.0 / D, 0.0, op0=ALU.mult,
                                    op1=ALU.add,
                                    accum_out=st1[:, 1:2]).then_inc(sv, 1)
                    v.wait_ge(sa, P3_SA(qt) + 3 * ts + 1)
                    v.wait_ge(sv, be + 3)
                    v.scalar_tensor_tensor(st1[:, 3:4], st1[:, 1:2], -1.0, st1[:, 1:2],
                                           op0=ALU.mult, op1=ALU.mult).then_inc(sv, 1)
                    v.wait_ge(sv, be + 4)
                    v.tensor_scalar(st1[:, 5:6], st1[:, 2:3], 1.0 / D, st1[:, 3:4],
                                    op0=ALU.mult, op1=ALU.add).then_inc(sv, 1)
                    v.wait_ge(sa, P3_SA(qt) + 3 * ts + 3)
                    v.tensor_scalar(scr2[:, :], sp, st1[:, 1:2], st1[:, 6:7],
                                    op0=ALU.subtract, op1=ALU.mult).then_inc(sv, 1)
                    v.wait_ge(sv, be + 6)
                    v.tensor_tensor(scr2[:, :], scr2[:, :], gam_sb[:, :],
                                    op=ALU.mult).then_inc(sv, 1)
                    v.wait_ge(sv, be + 7)
                    v.tensor_tensor(scr2[:, :], scr2[:, :], bet_sb[:, :],
                                    op=ALU.add).then_inc(sv, 1)
                    v.wait_ge(sv, be + 8)
                    v.wait_ge(lx, LX_EGG(qt, ts))
                    v.tensor_tensor(scr2[:, :], scr2[:, :],
                                    gst[:, sl * D:(sl + 1) * D],
                                    op=ALU.mult).then_inc(sv, 1)
                    v.wait_ge(sv, be + 9)
                    v.wait_ge(lx, LX_EXG(qt, ts))
                    v.tensor_tensor(spre[:, sl * D:(sl + 1) * D], scr2[:, :],
                                    xst[:, sl * D:(sl + 1) * D],
                                    op=ALU.add).then_inc(sv, 1)

        @block.sync
        def _(s):
            for qt in range(NQT if ph >= 6 else 0):
                for ts in range(4):
                    tok0 = qt * 512 + ts * 128
                    sl = ts % 2
                    pts = 4 * qt + ts - 2
                    if pts >= 0:
                        s.wait_ge(sv, P3_SV(pts // 4) + 49 + 10 * (pts % 4) + 10)
                    s.dma_start(out=xst[:, sl * D:(sl + 1) * D],
                                in_=x_d[tok0:tok0 + 128, :]).then_inc(lx, 16)
                    s.dma_start(out=gst[:, sl * D:(sl + 1) * D],
                                in_=g_d[tok0:tok0 + 128, :]).then_inc(lx, 16)
                for ts in range(4):
                    s.wait_ge(sv, P3_SV(qt) + 49 + 10 * ts + 10)
                    s.dma_start(out=out_d[qt * 512 + ts * 128:qt * 512 + (ts + 1) * 128, :],
                                in_=spre[:, (ts % 2) * D:(ts % 2 + 1) * D]).then_inc(lx, 16)
            if ph >= 6:
                s.wait_ge(lx, LX_TOT)
            else:
                tgt = {1: ("sv", P1_SV), 2: ("ld", LD_NB), 3: ("sv", SV_2B),
                       4: ("lx", LX2), 5: ("lx", LX3), 5.5: ("lx", LX4)}[ph]
                s.wait_ge({"sv": sv, "ld": ld, "lx": lx}[tgt[0]], tgt[1])
                s.dma_start(out=out_d[0:128, :], in_=xst[:, 0:D]).then_inc(lx, 16)
                s.wait_ge(lx, (LX1 if ph <= 3 else LX2 if ph == 4 else
                               LX3 if ph == 5 else LX4) + 16)
    return nc


def _host_consts():
    ident = np.eye(128, dtype=np.float32)
    e4 = np.zeros((128, 4), np.float32)
    for j in range(4):
        e4[32 * j:32 * (j + 1), j] = 1.0
    g8 = np.zeros((128, 128), np.float32)
    for e in range(8):
        for r in range(4):
            g8[e, 4 * e + r] = 1.0
    sel = np.zeros((128, NPAIR * 32), np.float32)
    for p in range(NPAIR):
        sel[32, p * 32 + 2 * p] = 1.0
        sel[96, p * 32 + 2 * p + 1] = 1.0
    ep = np.zeros((32, NPAIR * 128), np.float32)
    for p in range(NPAIR):
        ep[2 * p, p * 128 + 0:p * 128 + 32] = 1.0
        ep[2 * p + 1, p * 128 + 64:p * 128 + 96] = 1.0
    ones1 = np.ones((1, 128), np.float32)
    return ident, e4, g8, sel, ep, ones1


def kernel(L0_attn_x, L0_attn_w_qkv, L0_attn_w_out, L0_attn_b_out, L0_attn_ln_gamma,
           L0_attn_ln_beta, L0_ffn_W_lin, L0_ffn_b_lin, L0_ffn_gate_w, L0_ffn_left,
           L0_ffn_right):
    x = np.ascontiguousarray(np.asarray(L0_attn_x, dtype=np.float32))
    wqkv = np.ascontiguousarray(np.asarray(L0_attn_w_qkv, dtype=np.float32))
    wout = np.asarray(L0_attn_w_out, dtype=np.float32)
    bout = np.asarray(L0_attn_b_out, dtype=np.float32)
    gam = np.asarray(L0_attn_ln_gamma, dtype=np.float32)
    bet = np.asarray(L0_attn_ln_beta, dtype=np.float32)
    wlin = np.asarray(L0_ffn_W_lin, dtype=np.float32)
    blin = np.asarray(L0_ffn_b_lin, dtype=np.float32)
    gw = np.asarray(L0_ffn_gate_w, dtype=np.float32)
    left = np.asarray(L0_ffn_left, dtype=np.float32)
    right = np.asarray(L0_ffn_right, dtype=np.float32)

    wlinT = np.ascontiguousarray(wlin.T)
    gwT = np.ascontiguousarray(gw.T)
    leftf = np.ascontiguousarray(left.transpose(1, 0, 2).reshape(D, 32))
    rightp = np.zeros((128, D), np.float32)
    rightp[0:32] = right.reshape(32, D)
    wout_h = np.zeros((NPAIR * 128, D), np.float32)
    for p in range(NPAIR):
        wout_h[128 * p + 0:128 * p + 32] = wout[32 * (2 * p):32 * (2 * p) + 32]
        wout_h[128 * p + 64:128 * p + 96] = wout[32 * (2 * p + 1):32 * (2 * p + 1) + 32]
    rep = lambda a: np.ascontiguousarray(np.broadcast_to(a.reshape(1, D), (128, D)))
    ident, e4, g8, sel, ep, ones1 = _host_consts()

    nc = build_nc()
    eps_rep = np.ascontiguousarray(np.broadcast_to(
        np.array([EPS1, EPS2], np.float32).reshape(1, 2), (128, 2)))
    shared = dict(wqkv=wqkv, wlinT=wlinT, wout_h=wout_h, gwT=gwT, leftf=leftf,
                  rightp=rightp, blin_rep=rep(blin), bout_rep=rep(bout),
                  gam_rep=rep(gam), bet_rep=rep(bet), ident=ident, e4=e4,
                  g8pad=g8, sel=sel, epair=ep, ones1=ones1, eps_rep=eps_rep)
    in_maps = [dict(shared, x=np.ascontiguousarray(x[b])) for b in range(NCORES)]
    res = run_bass_kernel_spmd(nc, in_maps, core_ids=list(range(NCORES)))
    out = np.stack([res.results[b]["out"] for b in range(NCORES)], axis=0)
    return out.astype(np.float32)


# version 06
# speedup vs baseline: 1.6885x; 1.0106x over previous
import contextlib
import numpy as np
import ml_dtypes
import concourse.bass as bass
import concourse.mybir as mybir
from concourse.bass_utils import run_bass_kernel_spmd

F32 = mybir.dt.float32
BF16 = mybir.dt.bfloat16
AF = mybir.ActivationFunctionType
ALU = mybir.AluOpType
AX = mybir.AxisListType

NCORES = 8
T, D = 2048, 1024
NH, DH = 32, 32
NT = T // 128
NQT = T // 512
KT = T // 128
NPAIR = 16
SCALE = float(DH) ** -0.5
EPS1, EPS2 = 1e-5, 1e-6
BIG = 1e30


def build_nc(ph=6):
    nc = bass.Bass("TRN2", target_bir_lowering=False)
    dram = lambda name, shape, dtype=F32, kind="ExternalInput": nc.dram_tensor(
        name, shape, dtype, kind=kind).ap()
    x_d = dram("x", [T, D])
    wqkv_d = dram("wqkv", [D, 3 * D])
    wlinT_d = dram("wlinT", [D, D])
    wout_d = dram("wout_h", [NPAIR * 128, D])
    gwT_d = dram("gwT", [D, 8])
    leftf_d = dram("leftf", [D, 32])
    rightp_d = dram("rightp", [128, D])
    blin_d = dram("blin_rep", [128, D])
    bout_d = dram("bout_rep", [128, D])
    gam_d = dram("gam_rep", [128, D])
    bet_d = dram("bet_rep", [128, D])
    ident_d = dram("ident", [128, 128])
    e4_d = dram("e4", [128, 4])
    g8_d = dram("g8pad", [128, 128])
    sel_d = dram("sel", [128, NPAIR * 32])
    ep_d = dram("epair", [32, NPAIR * 128])
    ones1_d = dram("ones1", [1, 128])
    eps_d = dram("eps_rep", [128, 2])
    out_d = dram("out", [T, D], F32, kind="ExternalOutput")
    pre_d = dram("pre_scr", [T, D], BF16, kind="Internal")
    g_d = dram("g_scr", [T, D], BF16, kind="Internal")
    bnd_d = dram("bnd_scr", [4, 64], F32, kind="Internal")
    nb_d = dram("negB_scr", [128, 1], F32, kind="Internal")

    ctx = contextlib.ExitStack()
    with ctx:
        sb = lambda name, shape, dtype: ctx.enter_context(nc.sbuf_tensor(name, shape, dtype))
        psum = lambda name, shape, dtype=F32: ctx.enter_context(nc.psum_tensor(name, shape, dtype))

        q_sb = sb("q_sb", [128, 8 * T], BF16)
        k_sb = sb("k_sb", [128, 8 * T], BF16)
        v_sb = sb("v_sb", [128, KT * NH * 33], BF16)
        ident_sb = sb("ident_sb", [128, 128], BF16)
        sel_sb = sb("sel_sb", [128, NPAIR * 32], BF16)
        ep_sb = sb("ep_sb", [32, NPAIR * 128], BF16)
        e4_sb = sb("e4_sb", [128, 4], BF16)
        g8_sb = sb("g8_sb", [128, 128], BF16)
        st1 = sb("st1", [128, 16], F32)
        xst = sb("xst", [128, 2 * D], F32)
        scr = sb("scr", [128, D], BF16)
        scr2 = sb("scr2", [128, D], F32)
        xn_tm = sb("xn_tm", [128, 2 * D], BF16)
        prest = xn_tm
        blin_sb = sb("blin_sb", [128, D], F32)
        biasB_sb = sb("biasB_sb", [128, 1], F32)
        eps_sb = sb("eps_sb", [128, 2], F32)
        mqk = scr2[0:1, 256:512]
        ones1_sb = scr2[0:1, 512:640]
        bnd = scr2[0:4, 640:720]

        ps_big = psum("ps_big", [128, 2048])
        ps_two = psum("ps_two", [128, 1024])
        ps_sm = psum("ps_sm", [128, 512])
        ps_tr = psum("ps_tr", [128, 256], BF16)

        sem = lambda name: ctx.enter_context(nc.semaphore(name))
        ld = sem("ld")
        lx = sem("lx")
        sv = sem("sv")
        sa = sem("sa")
        sm = sem("sm")
        s_sc = sem("ssc")
        s_xp = sem("sxp")
        s_av = sem("sav")
        s_ev = sem("sev")

        block = ctx.enter_context(nc.Block())

        G0_DMAS = 8 * 4 + 6
        G0_LD = 16 * G0_DMAS
        LDW = lambda dc: G0_LD + 16 * (dc + 1)
        LD_2A = G0_LD + 16 * 16
        LD_BND1 = LD_2A + 16
        LD_BND = LD_2A + 16 * 3
        LD_NB1 = LD_BND + 16
        LD_NB = LD_BND + 32
        LD_WOUT = lambda i: LD_NB + 16 * (i + 1)
        LD_TOT = LD_NB + 16 * NPAIR

        LX_CONST = 16 * 3
        LX_X1 = lambda t: LX_CONST + 16 * (t + 1)
        LX1 = LX_CONST + 16 * NT
        LX_PST = lambda t: LX1 + 16 * (t + 1)
        LX2 = LX1 + 16 * NT
        _pr = {t: (t if t < 3 else 2 * t - 2) for t in range(NT)}
        _ps = {t: (3 + 2 * t if t <= NT - 4 else NT + t) for t in range(NT)}
        LX_PLD = lambda t: LX2 + 16 * (_pr[t] + 1)
        LX_GST = lambda t: LX2 + 16 * (_ps[t] + 1)
        LX3 = LX2 + 16 * NT
        LX4 = LX2 + 16 * 2 * NT
        LX_EPI = lambda qt: LX4 + 16 * 12 * qt
        LX_EXG = lambda qt, ts: LX_EPI(qt) + 16 * (2 * ts + 1)
        LX_EGG = lambda qt, ts: LX_EPI(qt) + 16 * (2 * ts + 2)
        LX_EOUT = lambda qt, ts: LX_EPI(qt) + 16 * 8 + 16 * (ts + 1)
        LX_TOT = LX4 + 16 * 12 * NQT

        P1_SV, P1_SA, P1_SM = 12 * NT, 3 * NT, 8 * NT
        SV_2A = P1_SV + 12 * 16
        P2A_SVD = lambda dc: P1_SV + 12 * dc
        P2A_SM = lambda dc: P1_SM + 8 * dc
        SM_2A = P1_SM + 8 * 16
        SA_BF = P1_SA + 2
        SV_BF = SV_2A + 4
        P2B_SV = lambda t: SV_BF + 12 * t
        P2B_SM = lambda t: SM_2A + 2 * t
        P2B_SA = lambda t: SA_BF + t
        SV_2B, SM_2B, SA_2B = SV_BF + 12 * NT, SM_2A + 2 * NT, SA_BF + NT
        P2C_SV = lambda qt: SV_2B + 2 * qt
        P2C_SM = lambda qt: SM_2B + 2 * qt
        SV_2C, SM_2C = SV_2B + 2 * NQT, SM_2B + 2 * NQT
        P2D_SV = lambda t: SV_2C + 2 * t
        P2D_SM = lambda t: SM_2C + 2 * t
        SV_2D, SM_2D = SV_2C + 2 * NT, SM_2C + 2 * NT
        SA_SIL = lambda t: SA_2B + t + 1
        SA_3P = SA_2B + NT
        QT_SC = lambda qt: 256 * qt
        QT_AV = lambda qt: 256 * qt
        QT_XP = lambda qt: 256 * qt
        QT_EV = lambda qt: 16 * qt
        P3_SV = lambda qt: SV_2D + 89 * qt
        P3_SM = lambda qt: SM_2D + 25 * qt
        P3_SA = lambda qt: SA_3P + 12 * qt
        SV_TOT = SV_2D + 89 * NQT
        SM_TOT = SM_2D + 25 * NQT
        SA_TOT = SA_3P + 12 * NQT

        @block.gpsimd
        def _(g):
            n = 0
            for dst, src in [(ident_sb[:, :], ident_d), (e4_sb[:, :], e4_d),
                             (g8_sb[:, :], g8_d), (sel_sb[:, :], sel_d),
                             (ep_sb[0:32, :], ep_d)]:
                g.dma_start(out=dst, in_=src).then_inc(ld, 16)
                n += 1
            g.memset(v_sb[:, :], 1.0)
            assert n == 5

        @block.sync
        def _(s):
            for dst, src in [(blin_sb[:, :], blin_d), (ones1_sb[:, :], ones1_d),
                             (eps_sb[:, :], eps_d)]:
                s.dma_start(out=dst, in_=src).then_inc(lx, 16)
            for t in range(NT):
                if t >= 2:
                    s.wait_ge(sv, 12 * (t - 2) + 4)
                s.dma_start(out=xst[:, (t % 2) * D:(t % 2 + 1) * D],
                            in_=x_d[t * 128:(t + 1) * 128, :]).then_inc(lx, 16)

        ctx2 = ctx.enter_context(contextlib.ExitStack())
        sb2 = lambda name, shape, dtype: ctx2.enter_context(nc.sbuf_tensor(name, shape, dtype))
        xn_sb = sb2("xn_sb", [128, 8 * T], BF16)
        wv_sb = sb2("wv_sb", [128, 8 * D], BF16)
        wlin_sb = sb2("wlin_sb", [128, 8 * D], BF16)
        gwT_sb = sb2("gwT_sb", [128, 64], BF16)
        leftf_sb = sb2("leftf_sb", [128, 256], BF16)
        rightp_sb = sb2("rightp_sb", [128, D], BF16)
        wq_st = sb2("wq_st", [128, 2 * 1024], BF16)
        sqs = sb2("sqs", [128, 512], BF16)
        gates_bf = sb2("gates_bf", [128, NT * 8], BF16)
        gpad = sb2("gpad", [128, 128], BF16)
        gfm_sb = sb2("gfm_sb", [128, NT * 128], BF16)
        t_tmp = sb2("t_tmp", [32, 512], BF16)
        t1pad = sb2("t1pad", [128, T], BF16)

        @block.gpsimd
        def _(g):
            for c in range(8):
                g.dma_start(out=wv_sb[:, c * D:(c + 1) * D],
                            in_=wqkv_d[c * 128:(c + 1) * 128, 2 * D:3 * D]).then_inc(ld, 16)
                g.dma_start(out=wlin_sb[:, c * D:(c + 1) * D],
                            in_=wlinT_d[c * 128:(c + 1) * 128, :]).then_inc(ld, 16)
                g.dma_start(out=gwT_sb[:, c * 8:(c + 1) * 8],
                            in_=gwT_d[c * 128:(c + 1) * 128, :]).then_inc(ld, 16)
                g.dma_start(out=leftf_sb[:, c * 32:(c + 1) * 32],
                            in_=leftf_d[c * 128:(c + 1) * 128, :]).then_inc(ld, 16)
            g.dma_start(out=rightp_sb[:, :], in_=rightp_d).then_inc(ld, 16)
            g.memset(t1pad[:, :], 0.0)
            g.memset(gpad[:, :], 0.0)
            for dc in range(16 if ph >= 2 else 0):
                if dc >= 2:
                    g.wait_ge(sm, P2A_SM(dc - 1) + 4)
                g.dma_start(
                    out=wq_st[:, (dc % 2) * 1024:(dc % 2 + 1) * 1024].rearrange(
                        "p (c m) -> p c m", c=8),
                    in_=wqkv_d[:, dc * 128:(dc + 1) * 128].rearrange(
                        "(c p) m -> p c m", p=128)).then_inc(ld, 16)

        @block.vector
        def _(v):
            for t in range(NT):
                b = 12 * t
                xt = xst[:, (t % 2) * D:(t % 2 + 1) * D]
                v.wait_ge(lx, LX_X1(t))
                if t >= 2:
                    v.wait_ge(sm, 8 * (t - 2) + 8)
                v.tensor_scalar(xn_tm[:, (t % 2) * D:(t % 2 + 1) * D], xt, 1.0 / D,
                                0.0, op0=ALU.mult, op1=ALU.add,
                                accum_out=st1[:, 1:2]).then_inc(sv, 1)
                v.wait_ge(sa, 3 * t + 1)
                v.wait_ge(sv, b + 1)
                v.scalar_tensor_tensor(st1[:, 3:4], st1[:, 1:2], -1.0, st1[:, 1:2],
                                       op0=ALU.mult, op1=ALU.mult).then_inc(sv, 1)
                v.wait_ge(sv, b + 2)
                if t >= 1:
                    v.wait_ge(sa, 3 * (t - 1) + 2)
                v.tensor_scalar(st1[:, 5:6], st1[:, 2:3], 1.0 / D, st1[:, 3:4],
                                op0=ALU.mult, op1=ALU.add).then_inc(sv, 1)
                v.wait_ge(sa, 3 * t + 3)
                v.tensor_scalar(xn_tm[:, (t % 2) * D:(t % 2 + 1) * D], xt,
                                st1[:, 1:2], st1[:, 6:7],
                                op0=ALU.subtract, op1=ALU.mult).then_inc(sv, 1)
                for c in range(8):
                    v.wait_ge(sm, 8 * t + c + 1)
                    v.tensor_copy(xn_sb[:, c * T + t * 128:c * T + (t + 1) * 128],
                                  ps_tr[:, 0:128]).then_inc(sv, 1)

        @block.scalar
        def _(sc):
            for t in range(NT):
                xt = xst[:, (t % 2) * D:(t % 2 + 1) * D]
                sc.wait_ge(lx, LX_X1(t))
                if t >= 1:
                    sc.wait_ge(sv, 12 * (t - 1) + 3)
                sc.activation(scr[:, :], xt, AF.Square,
                              accum_out=st1[:, 2:3]).then_inc(sa, 1)
                sc.wait_ge(sv, 12 * t + 3)
                sc.activation(st1[:, 7:8], st1[:, 5:6], AF.Ln,
                              bias=eps_sb[:, 0:1], scale=1.0).then_inc(sa, 1)
                sc.wait_ge(sa, 3 * t + 2)
                if t >= 1:
                    sc.wait_ge(sv, 12 * (t - 1) + 4)
                sc.activation(st1[:, 6:7], st1[:, 7:8], AF.Exp,
                              bias=0.0, scale=-0.5).then_inc(sa, 1)

        @block.tensor
        def _(t_):
            t_.wait_ge(ld, 16 * 5)
            for t in range(NT):
                for c in range(8):
                    t_.wait_ge(sv, 12 * t + 4)
                    prev = 8 * t + c - 1
                    if prev >= 0:
                        pt, pc = prev // 8, prev % 8
                        t_.wait_ge(sv, 12 * pt + 4 + pc + 1)
                    t_.transpose(ps_tr[:, 0:128],
                                 xn_tm[:, (t % 2) * D + c * 128:(t % 2) * D + (c + 1) * 128],
                                 ident_sb[:, :]).then_inc(sm, 1)

        @block.tensor
        def _(t_):
            for dc in range(16 if ph >= 2 else 0):
                t_.wait_ge(ld, LDW(dc))
                for qt in range(NQT):
                    idx = 4 * dc + qt
                    if idx >= 4:
                        pdc, pqt = (idx - 4) // 4, (idx - 4) % 4
                        t_.wait_ge(sv, P2A_SVD(pdc) + pqt + 1)
                    ins = None
                    for c in range(8):
                        ins = t_.matmul(
                            ps_big[:, qt * 512:(qt + 1) * 512],
                            wq_st[:, (dc % 2) * 1024 + c * 128:(dc % 2) * 1024 + (c + 1) * 128],
                            xn_sb[:, c * T + qt * 512:c * T + (qt + 1) * 512],
                            start=(c == 0), stop=(c == 7), skip_group_check=True)
                    ins.then_inc(sm, 1)
                for qt in range(NQT):
                    t_.wait_ge(sv, P2A_SVD(dc) + 5 + 2 * qt)
                    t_.matmul(ps_sm[0:4, 0:512], e4_sb[:, :], sqs[:, :],
                              start=True, stop=True,
                              skip_group_check=True).then_inc(sm, 1)

        @block.vector
        def _(v):
            for dc in range(16 if ph >= 2 else 0):
                dst = q_sb if dc < 8 else k_sb
                cc = dc % 8
                base = P2A_SVD(dc)
                for qt in range(NQT):
                    v.wait_ge(sm, P2A_SM(dc) + qt + 1)
                    v.tensor_copy(dst[:, cc * T + qt * 512:cc * T + (qt + 1) * 512],
                                  ps_big[:, qt * 512:(qt + 1) * 512]).then_inc(sv, 1)
                for qt in range(NQT):
                    v.wait_ge(sv, base + 4)
                    v.wait_ge(sm, P2A_SM(dc) + 4 + qt)
                    src = dst[:, cc * T + qt * 512:cc * T + (qt + 1) * 512]
                    v.tensor_tensor(sqs[:, :], src, src, op=ALU.mult).then_inc(sv, 1)
                    v.wait_ge(sm, P2A_SM(dc) + 4 + qt + 1)
                    v.reduce_max(bnd[0:4, dc * 4 + qt:dc * 4 + qt + 1],
                                 ps_sm[0:4, 0:512], axis=AX.X).then_inc(sv, 1)

        @block.gpsimd
        def _(g):
            if ph < 2:
                return
            g.wait_ge(sv, SV_2A)
            g.dma_start(out=bnd_d, in_=bnd[0:4, 0:64]).then_inc(ld, 16)
            g.wait_ge(ld, LD_BND1)
            g.dma_start(out=mqk[:, 0:128].rearrange("a (p c) -> a p c", p=4),
                        in_=bnd_d[:, 0:32]).then_inc(ld, 16)
            g.dma_start(out=mqk[:, 128:256].rearrange("a (p c) -> a p c", p=4),
                        in_=bnd_d[:, 32:64]).then_inc(ld, 16)

        @block.vector
        def _(v):
            if ph < 2:
                return
            v.wait_ge(ld, LD_BND)
            v.reduce_max(st1[0:1, 8:9], mqk[:, 0:128], axis=AX.X)
            v.reduce_max(st1[0:1, 9:10], mqk[:, 128:256], axis=AX.X).then_inc(sv, 1)
            v.wait_ge(sv, SV_2A + 1)
            v.tensor_tensor(st1[0:1, 10:11], st1[0:1, 8:9], st1[0:1, 9:10],
                            op=ALU.mult).then_inc(sv, 1)
            v.wait_ge(sa, SA_BF)
            v.tensor_scalar(st1[0:1, 12:13], st1[0:1, 11:12], -SCALE * 1.02, None,
                            op0=ALU.mult).then_inc(sv, 1)
            v.wait_ge(sv, SV_2A + 3)
            v.tensor_scalar(mqk[:, 0:128], ones1_sb[0:1, :], st1[0:1, 12:13], None,
                            op0=ALU.mult).then_inc(sv, 1)

        @block.scalar
        def _(sc):
            if ph < 2:
                return
            sc.wait_ge(sv, SV_2A + 2)
            sc.activation(st1[0:1, 13:14], st1[0:1, 10:11], AF.Ln,
                          bias=0.0, scale=1.0).then_inc(sa, 1)
            sc.wait_ge(sa, P1_SA + 1)
            sc.activation(st1[0:1, 11:12], st1[0:1, 13:14], AF.Exp,
                          bias=0.0, scale=0.5).then_inc(sa, 1)

        @block.gpsimd
        def _(g):
            if ph < 2:
                return
            g.wait_ge(sv, SV_BF)
            g.dma_start(out=nb_d[:, :].rearrange("p a -> (p a)").rearrange(
                "(a q) -> a q", a=1), in_=mqk[:, 0:128]).then_inc(ld, 16)
            g.wait_ge(ld, LD_NB1)
            g.dma_start(out=biasB_sb[:, :], in_=nb_d).then_inc(ld, 16)

        @block.tensor
        def _(t_):
            for t in range(NT if ph >= 3 else 0):
                if t == 0:
                    t_.wait_ge(sv, SV_2A)
                if t >= 2:
                    t_.wait_ge(sv, P2B_SV(t - 2) + 1)
                if t >= 1:
                    t_.wait_ge(sv, P2B_SV(t - 1) + 7)
                    t_.wait_ge(sa, P2B_SA(t - 1) + 1)
                vb = (t % 2) * 1024
                ins = None
                for c in range(8):
                    xnc = xn_sb[:, c * T + t * 128:c * T + (t + 1) * 128]
                    t_.matmul(ps_big[:, vb:vb + 512], xnc, wv_sb[:, c * D:c * D + 512],
                              start=(c == 0), stop=(c == 7), skip_group_check=True)
                    t_.matmul(ps_big[:, vb + 512:vb + 1024], xnc, wv_sb[:, c * D + 512:(c + 1) * D],
                              start=(c == 0), stop=(c == 7), skip_group_check=True)
                    ins = t_.matmul(ps_sm[:, 0:8], xnc, gwT_sb[:, c * 8:(c + 1) * 8],
                                    start=(c == 0), stop=(c == 7), skip_group_check=True)
                ins.then_inc(sm, 1)
                t_.wait_ge(sv, P2B_SV(t) + 11)
                if t >= 1:
                    t_.wait_ge(sv, P2B_SV(t - 1) + 12)
                t_.transpose(ps_tr[:, 128:256], gpad[:, :], ident_sb[:, :]).then_inc(sm, 1)

        @block.scalar
        def _(sc):
            for t in range(NT if ph >= 3 else 0):
                sc.wait_ge(sv, P2B_SV(t) + 2)
                sc.activation(scr2[:, 0:8], ps_sm[:, 0:8], AF.Exp, bias=st1[:, 8:9],
                              accum_out=st1[:, 9:10]).then_inc(sa, 1)

        @block.vector
        def _(v):
            for t in range(NT if ph >= 3 else 0):
                b = P2B_SV(t)
                v.wait_ge(sm, P2B_SM(t) + 1)
                vb = (t % 2) * 1024
                v.tensor_copy(
                    v_sb[:, t * NH * 33:(t + 1) * NH * 33].rearrange(
                        "p (h w) -> p h w", h=NH)[:, :, 0:32],
                    ps_big[:, vb:vb + 1024].rearrange("p (h w) -> p h w", h=NH)).then_inc(sv, 1)
                v.reduce_max(st1[:, 8:9], ps_sm[:, 0:8], axis=AX.X,
                             negate=True).then_inc(sv, 1)
                v.wait_ge(sv, b + 2)
                v.tensor_scalar(st1[:, 10:11], st1[:, 8:9], -1.0, None,
                                op0=ALU.mult).then_inc(sv, 1)
                v.wait_ge(sv, b + 3)
                v.tensor_scalar(scr2[:, 8:16], ps_sm[:, 0:8], st1[:, 10:11], None,
                                op0=ALU.is_ge).then_inc(sv, 1)
                v.wait_ge(sv, b + 4)
                v.scalar_tensor_tensor(scr2[:, 16:24], scr2[:, 8:16], -BIG,
                                       ps_sm[:, 0:8], op0=ALU.mult,
                                       op1=ALU.add).then_inc(sv, 1)
                v.wait_ge(sv, b + 5)
                v.reduce_max(st1[:, 11:12], scr2[:, 16:24], axis=AX.X).then_inc(sv, 1)
                v.wait_ge(sv, b + 6)
                v.tensor_scalar(scr2[:, 24:32], ps_sm[:, 0:8], st1[:, 11:12], None,
                                op0=ALU.is_ge).then_inc(sv, 1)
                v.wait_ge(sa, P2B_SA(t) + 1)
                with nc.allow_low_precision(reason="tiny gate sums"):
                    v.reciprocal(st1[:, 12:13], st1[:, 9:10]).then_inc(sv, 1)
                v.wait_ge(sv, b + 8)
                v.scalar_tensor_tensor(gates_bf[:, t * 8:(t + 1) * 8], scr2[:, 0:8],
                                       1.0, scr2[:, 24:32], op0=ALU.mult,
                                       op1=ALU.mult).then_inc(sv, 1)
                v.wait_ge(sv, b + 9)
                v.tensor_scalar(gates_bf[:, t * 8:(t + 1) * 8],
                                gates_bf[:, t * 8:(t + 1) * 8], st1[:, 12:13], None,
                                op0=ALU.mult).then_inc(sv, 1)
                v.wait_ge(sv, b + 10)
                if t >= 1:
                    v.wait_ge(sm, P2B_SM(t - 1) + 2)
                v.tensor_copy(gpad[:, 0:8],
                              gates_bf[:, t * 8:(t + 1) * 8]).then_inc(sv, 1)
                v.wait_ge(sm, P2B_SM(t) + 2)
                v.tensor_copy(gfm_sb[:, t * 128:(t + 1) * 128],
                              ps_tr[:, 128:256]).then_inc(sv, 1)

        @block.tensor
        def _(t_):
            for qt in range(NQT if ph >= 4 else 0):
                if qt == 0:
                    t_.wait_ge(sv, SV_2B)
                    t_.wait_ge(sa, SA_2B)
                else:
                    t_.wait_ge(sv, P2C_SV(qt - 1) + 1)
                ins = None
                for c in range(8):
                    ins = t_.matmul(ps_sm[0:32, :], leftf_sb[:, c * 32:(c + 1) * 32],
                                    xn_sb[:, c * T + qt * 512:c * T + (qt + 1) * 512],
                                    start=(c == 0), stop=(c == 7), skip_group_check=True)
                ins.then_inc(sm, 1)
                if qt >= 1:
                    t_.wait_ge(sv, P2C_SV(qt - 1) + 1)
                t_.matmul(ps_two[:, 0:512], g8_sb[:, :],
                          gfm_sb[:, qt * 512:(qt + 1) * 512],
                          start=True, stop=True, skip_group_check=True).then_inc(sm, 1)

        @block.vector
        def _(v):
            for qt in range(NQT if ph >= 4 else 0):
                b = P2C_SV(qt)
                v.wait_ge(sm, P2C_SM(qt) + 1)
                v.tensor_copy(t_tmp[0:32, :], ps_sm[0:32, :]).then_inc(sv, 1)
                v.wait_ge(sv, b + 1)
                v.wait_ge(sm, P2C_SM(qt) + 2)
                v.tensor_tensor(t1pad[0:32, qt * 512:(qt + 1) * 512],
                                ps_two[0:32, 0:512], t_tmp[0:32, :],
                                op=ALU.mult).then_inc(sv, 1)

        @block.tensor
        def _(t_):
            for t in range(NT if ph >= 4 else 0):
                if t <= 1:
                    t_.wait_ge(sv, SV_2B)
                t_.wait_ge(sv, P2C_SV(t // 4) + 2)
                if t >= 2:
                    t_.wait_ge(sv, P2D_SV(t - 2) + 2)
                for half in range(2):
                    hb = (t % 2) * 1024
                    ins = None
                    for c in range(8):
                        ins = t_.matmul(
                            ps_big[:, hb + half * 512:hb + (half + 1) * 512],
                            xn_sb[:, c * T + t * 128:c * T + (t + 1) * 128],
                            wlin_sb[:, c * D + half * 512:c * D + (half + 1) * 512],
                            start=(c == 0), stop=False, skip_group_check=True)
                    ins = t_.matmul(
                        ps_big[:, hb + half * 512:hb + (half + 1) * 512],
                        t1pad[:, t * 128:(t + 1) * 128],
                        rightp_sb[:, half * 512:(half + 1) * 512],
                        start=False, stop=True, skip_group_check=True)
                    ins.then_inc(sm, 1)

        @block.vector
        def _(v):
            for t in range(NT if ph >= 4 else 0):
                b = P2D_SV(t)
                hb = (t % 2) * 1024
                for half in range(2):
                    v.wait_ge(sm, P2D_SM(t) + half + 1)
                    if t >= 2:
                        v.wait_ge(lx, LX_PST(t - 2))
                    v.tensor_tensor(prest[:, (t % 2) * D + half * 512:(t % 2) * D + (half + 1) * 512],
                                    ps_big[:, hb + half * 512:hb + (half + 1) * 512],
                                    blin_sb[:, half * 512:(half + 1) * 512],
                                    op=ALU.add).then_inc(sv, 1)

        @block.sync
        def _(s):
            for t in range(NT if ph >= 4 else 0):
                s.wait_ge(sv, P2D_SV(t) + 2)
                s.dma_start(out=pre_d[t * 128:(t + 1) * 128, :],
                            in_=prest[:, (t % 2) * D:(t % 2 + 1) * D]).then_inc(lx, 16)

        ctx2.close()
        ctx3 = ctx.enter_context(contextlib.ExitStack())
        sb3 = lambda name, shape, dtype: ctx3.enter_context(nc.sbuf_tensor(name, shape, dtype))
        wout_sb = sb3("wout_sb", [128, NPAIR * D], BF16)
        attn = sb3("attn", [128, NPAIR * 512], BF16)
        expS = sb3("expS", [128, 4 * 1024], BF16)
        rpk = sb3("rpk", [32, 512], BF16)
        rb_bf = sb3("rb_bf", [128, 2 * 512], BF16)
        gst = sb3("gst", [128, 2 * D], BF16)
        spre = sb3("spre", [128, 2 * D], F32)
        bout_sb = sb3("bout_sb", [128, D], F32)
        gam_sb = sb3("gam_sb", [128, D], F32)
        bet_sb = sb3("bet_sb", [128, D], F32)

        @block.gpsimd
        def _(g):
            if ph >= 5:
                g.wait_ge(sm, SM_2D)
                g.wait_ge(sv, SV_2D)
            for p in range(NPAIR if ph >= 5 else 0):
                g.dma_start(out=wout_sb[:, p * D:(p + 1) * D],
                            in_=wout_d[p * 128:(p + 1) * 128, :]).then_inc(ld, 16)

        @block.sync
        def _(s):
            for dst, src in [(bout_sb[:, :], bout_d), (gam_sb[:, :], gam_d),
                             (bet_sb[:, :], bet_d)]:
                s.dma_start(out=dst, in_=src).then_inc(lx, 16) if False else None
            if ph >= 5.5:
                order = [("r", 0), ("r", 1), ("r", 2)]
                for k in range(NT - 3):
                    order += [("s", k), ("r", k + 3)]
                order += [("s", NT - 3), ("s", NT - 2), ("s", NT - 1)]
                for kind, t in order:
                    if kind == "r":
                        s.wait_ge(lx, LX_PST(NT - 1))
                        if t >= 2:
                            s.wait_ge(sa, SA_SIL(t - 2))
                        s.dma_start(out=prest[:, (t % 2) * D:(t % 2 + 1) * D],
                                    in_=pre_d[t * 128:(t + 1) * 128, :]).then_inc(lx, 16)
                    else:
                        s.wait_ge(sa, SA_SIL(t))
                        s.dma_start(out=g_d[t * 128:(t + 1) * 128, :],
                                    in_=gst[:, (t % 2) * D:(t % 2 + 1) * D]).then_inc(lx, 16)
            elif ph == 5:
                for t in range(NT):
                    s.wait_ge(lx, LX_PST(NT - 1))
                    s.dma_start(out=prest[:, (t % 2) * D:(t % 2 + 1) * D],
                                in_=pre_d[t * 128:(t + 1) * 128, :]).then_inc(lx, 16)

        @block.gpsimd
        def _(g):
            for dst, src in ([(bout_sb[:, :], bout_d), (gam_sb[:, :], gam_d),
                             (bet_sb[:, :], bet_d)] if ph >= 5 else []):
                g.dma_start(out=dst, in_=src).then_inc(ld, 16)
        LD_TOT2 = LD_TOT + 16 * 3

        @block.scalar
        def _(sc):
            for t in range(NT if ph >= 5.5 else 0):
                sc.wait_ge(lx, LX_PLD(t))
                if t >= 2:
                    sc.wait_ge(lx, LX_GST(t - 2))
                sc.activation(gst[:, (t % 2) * D:(t % 2 + 1) * D],
                              prest[:, (t % 2) * D:(t % 2 + 1) * D],
                              AF.Silu).then_inc(sa, 1)

        @block.tensor
        def _(t_):
            if ph >= 6:
                t_.wait_ge(ld, LD_TOT2)
            for qt in range(NQT if ph >= 6 else 0):
                scb, avb, xpb, evb = QT_SC(qt), QT_AV(qt), QT_XP(qt), QT_EV(qt)
                if qt > 0:
                    t_.wait_ge(sv, P3_SV(qt))
                NI = NPAIR * KT

                def emit_scores(i):
                    p, kt = i // KT, i % KT
                    cp, jp = p // 2, p % 2
                    sA, sB = 64 * jp, 64 * jp + 32
                    base = 256 * qt + 16 * p
                    par = kt % 2
                    t_.wait_ge(s_xp, max(0, base + kt - 1))
                    t_.matmul(ps_big[:, par * 1024:par * 1024 + 512],
                              k_sb[sA:sA + 32, cp * T + kt * 128:cp * T + (kt + 1) * 128],
                              q_sb[sA:sA + 32, cp * T + qt * 512:cp * T + (qt + 1) * 512],
                              start=True, stop=True, tile_position=(sA, 0),
                              skip_group_check=True)
                    t_.matmul(ps_big[:, par * 1024 + 512:par * 1024 + 1024],
                              k_sb[sB:sB + 32, cp * T + kt * 128:cp * T + (kt + 1) * 128],
                              q_sb[sB:sB + 32, cp * T + qt * 512:cp * T + (qt + 1) * 512],
                              start=True, stop=True, tile_position=(sB, 0),
                              skip_group_check=True).then_inc(s_sc, 1)

                def emit_av(j):
                    p, kt = j // KT, j % KT
                    cp, jp = p // 2, p % 2
                    hA, hB = 4 * cp + 2 * jp, 4 * cp + 2 * jp + 1
                    base = 256 * qt + 16 * p
                    slot = (kt % 4) * 1024
                    ub = (p % 2) * 512
                    t_.wait_ge(s_xp, base + kt + 1)
                    if p >= 2 and kt == 0:
                        t_.wait_ge(s_ev, evb + p - 1)
                    if qt > 0 and p < 2 and kt == 0:
                        t_.wait_ge(s_ev, evb)
                    t_.matmul(ps_two[0:33, ub:ub + 512],
                              v_sb[:, kt * NH * 33 + hA * 33:kt * NH * 33 + hA * 33 + 33],
                              expS[:, slot:slot + 512],
                              start=(kt == 0), stop=(kt == KT - 1),
                              tile_position=(0, 0), skip_group_check=True)
                    t_.matmul(ps_two[64:97, ub:ub + 512],
                              v_sb[:, kt * NH * 33 + hB * 33:kt * NH * 33 + hB * 33 + 33],
                              expS[:, slot + 512:slot + 1024],
                              start=(kt == 0), stop=(kt == KT - 1),
                              tile_position=(0, 64),
                              skip_group_check=True).then_inc(s_av, 1)

                for g in range(NI // 2 + 1):
                    if 2 * g < NI:
                        emit_scores(2 * g)
                        emit_scores(2 * g + 1)
                    if g >= 1:
                        emit_av(2 * g - 2)
                        emit_av(2 * g - 1)
                for p in range(NPAIR):
                    t_.wait_ge(s_ev, evb + p + 1)
                    ins = t_.matmul(ps_sm[0:32, :], sel_sb[:, p * 32:(p + 1) * 32],
                                    attn[:, p * 512:(p + 1) * 512],
                                    start=(p == 0), stop=(p == NPAIR - 1),
                                    skip_group_check=True)
                ins.then_inc(sm, 1)
                for p in range(NPAIR):
                    t_.wait_ge(sv, P3_SV(qt) + 1 + max(0, 3 * (p - 2) + 1) if p >= 2
                               else P3_SV(qt) + 1)
                    t_.matmul(ps_big[:, (p % 2) * 512:(p % 2) * 512 + 512],
                              ep_sb[0:32, p * 128:(p + 1) * 128],
                              rpk[0:32, :], start=True, stop=True,
                              skip_group_check=True).then_inc(sm, 1)
                for ts in range(4):
                    for half in range(2):
                        if ts >= 1 or half >= 1:
                            pidx = 2 * ts + half - 2
                            if pidx >= 0:
                                t_.wait_ge(sv, P3_SV(qt) + 49 + (pidx // 2) * 10 +
                                           (2 if pidx % 2 == 1 else 1))
                        ins = None
                        for p in range(NPAIR):
                            t_.wait_ge(sv, P3_SV(qt) + 1 + 3 * p + 3)
                            ins = t_.matmul(
                                ps_big[:, 1024 + half * 512:1024 + (half + 1) * 512],
                                attn[:, p * 512 + ts * 128:p * 512 + (ts + 1) * 128],
                                wout_sb[:, p * D + half * 512:p * D + (half + 1) * 512],
                                start=(p == 0), stop=(p == NPAIR - 1),
                                skip_group_check=True)
                        ins.then_inc(sm, 1)

        @block.scalar
        def _(sc):
            if ph >= 6:
                sc.wait_ge(ld, LD_NB)
            for qt in range(NQT if ph >= 6 else 0):
                for p in range(NPAIR):
                    base = 256 * qt + 16 * p
                    for kt in range(KT):
                        par = kt % 2
                        slot = (kt % 4) * 1024
                        sc.wait_ge(s_sc, base + kt + 1)
                        sc.wait_ge(s_av, max(0, base + kt - 3))
                        sc.activation(expS[:, slot:slot + 1024],
                                      ps_big[:, par * 1024:par * 1024 + 1024],
                                      AF.Exp, bias=biasB_sb[:, :],
                                      scale=SCALE).then_inc(s_xp, 1)
                for ts in range(4):
                    b3 = P3_SA(qt) + 3 * ts
                    sc.wait_ge(sv, P3_SV(qt) + 49 + 10 * ts + 2)
                    sc.activation(scr[:, :], spre[:, (ts % 2) * D:(ts % 2 + 1) * D],
                                  AF.Square, accum_out=st1[:, 2:3]).then_inc(sa, 1)
                    sc.wait_ge(sv, P3_SV(qt) + 49 + 10 * ts + 5)
                    sc.activation(st1[:, 7:8], st1[:, 5:6], AF.Ln,
                                  bias=eps_sb[:, 1:2], scale=1.0).then_inc(sa, 1)
                    sc.wait_ge(sa, b3 + 2)
                    sc.activation(st1[:, 6:7], st1[:, 7:8], AF.Exp,
                                  bias=0.0, scale=-0.5).then_inc(sa, 1)

        @block.vector
        def _(v):
            for qt in range(NQT if ph >= 6 else 0):
                bq = P3_SV(qt)
                evb = QT_EV(qt)
                for p in range(NPAIR):
                    v.wait_ge(s_av, 256 * qt + 16 * p + 16)
                    v.tensor_copy(attn[:, p * 512:(p + 1) * 512],
                                  ps_two[:, (p % 2) * 512:(p % 2) * 512 + 512]).then_inc(s_ev, 1)
                v.wait_ge(sm, P3_SM(qt) + 1)
                with nc.allow_low_precision(reason="1/denom bf16: uniform per-(h,q) scale"):
                    v.reciprocal(rpk[0:32, :], ps_sm[0:32, :]).then_inc(sv, 1)
                for p in range(NPAIR):
                    sl2 = (p % 2) * 512
                    v.wait_ge(sm, P3_SM(qt) + 2 + p)
                    v.tensor_copy(rb_bf[:, sl2:sl2 + 512],
                                  ps_big[:, sl2:sl2 + 512]).then_inc(sv, 1)
                    v.wait_ge(sv, bq + 1 + 3 * p + 1)
                    v.tensor_tensor(attn[0:64, p * 512:(p + 1) * 512],
                                    attn[0:64, p * 512:(p + 1) * 512],
                                    rb_bf[0:64, sl2:sl2 + 512], op=ALU.mult).then_inc(sv, 1)
                    v.tensor_tensor(attn[64:128, p * 512:(p + 1) * 512],
                                    attn[64:128, p * 512:(p + 1) * 512],
                                    rb_bf[64:128, sl2:sl2 + 512], op=ALU.mult).then_inc(sv, 1)
                for ts in range(4):
                    be = bq + 49 + 10 * ts
                    tok0 = qt * 512 + ts * 128
                    sl = (ts % 2)
                    v.wait_ge(sm, P3_SM(qt) + 17 + 2 * ts + 1)
                    if True:
                        pts = 4 * qt + ts - 2
                        if pts >= 0:
                            v.wait_ge(lx, LX_EOUT(pts // 4, pts % 4))
                    gts = 4 * qt + ts - 2
                    for half in range(2):
                        if half == 1:
                            v.wait_ge(sm, P3_SM(qt) + 17 + 2 * ts + 2)
                        if gts >= 0:
                            v.wait_ge(sa, P3_SA(gts // 4) + 3 * (gts % 4) + 1)
                        v.tensor_tensor(spre[:, sl * D + half * 512:sl * D + (half + 1) * 512],
                                        ps_big[:, 1024 + half * 512:1024 + (half + 1) * 512],
                                        bout_sb[:, half * 512:(half + 1) * 512],
                                        op=ALU.add).then_inc(sv, 1)
                    sp = spre[:, sl * D:(sl + 1) * D]
                    v.wait_ge(sv, be + 2)
                    v.tensor_scalar(scr[:, :], sp, 1.0 / D, 0.0, op0=ALU.mult,
                                    op1=ALU.add,
                                    accum_out=st1[:, 1:2]).then_inc(sv, 1)
                    v.wait_ge(sa, P3_SA(qt) + 3 * ts + 1)
                    v.wait_ge(sv, be + 3)
                    v.scalar_tensor_tensor(st1[:, 3:4], st1[:, 1:2], -1.0, st1[:, 1:2],
                                           op0=ALU.mult, op1=ALU.mult).then_inc(sv, 1)
                    v.wait_ge(sv, be + 4)
                    v.tensor_scalar(st1[:, 5:6], st1[:, 2:3], 1.0 / D, st1[:, 3:4],
                                    op0=ALU.mult, op1=ALU.add).then_inc(sv, 1)
                    v.wait_ge(sa, P3_SA(qt) + 3 * ts + 3)
                    v.tensor_scalar(scr2[:, :], sp, st1[:, 1:2], st1[:, 6:7],
                                    op0=ALU.subtract, op1=ALU.mult).then_inc(sv, 1)
                    v.wait_ge(sv, be + 6)
                    v.tensor_tensor(scr2[:, :], scr2[:, :], gam_sb[:, :],
                                    op=ALU.mult).then_inc(sv, 1)
                    v.wait_ge(sv, be + 7)
                    v.tensor_tensor(scr2[:, :], scr2[:, :], bet_sb[:, :],
                                    op=ALU.add).then_inc(sv, 1)
                    v.wait_ge(sv, be + 8)
                    v.wait_ge(lx, LX_EGG(qt, ts))
                    v.tensor_tensor(scr2[:, :], scr2[:, :],
                                    gst[:, sl * D:(sl + 1) * D],
                                    op=ALU.mult).then_inc(sv, 1)
                    v.wait_ge(sv, be + 9)
                    v.wait_ge(lx, LX_EXG(qt, ts))
                    v.tensor_tensor(spre[:, sl * D:(sl + 1) * D], scr2[:, :],
                                    xst[:, sl * D:(sl + 1) * D],
                                    op=ALU.add).then_inc(sv, 1)

        @block.sync
        def _(s):
            for qt in range(NQT if ph >= 6 else 0):
                for ts in range(4):
                    tok0 = qt * 512 + ts * 128
                    sl = ts % 2
                    pts = 4 * qt + ts - 2
                    if pts >= 0:
                        s.wait_ge(sv, P3_SV(pts // 4) + 49 + 10 * (pts % 4) + 10)
                    s.dma_start(out=xst[:, sl * D:(sl + 1) * D],
                                in_=x_d[tok0:tok0 + 128, :]).then_inc(lx, 16)
                    s.dma_start(out=gst[:, sl * D:(sl + 1) * D],
                                in_=g_d[tok0:tok0 + 128, :]).then_inc(lx, 16)
                for ts in range(4):
                    s.wait_ge(sv, P3_SV(qt) + 49 + 10 * ts + 10)
                    s.dma_start(out=out_d[qt * 512 + ts * 128:qt * 512 + (ts + 1) * 128, :],
                                in_=spre[:, (ts % 2) * D:(ts % 2 + 1) * D]).then_inc(lx, 16)
            if ph >= 6:
                s.wait_ge(lx, LX_TOT)
            else:
                tgt = {1: ("sv", P1_SV), 2: ("ld", LD_NB), 3: ("sv", SV_2B),
                       4: ("lx", LX2), 5: ("lx", LX3), 5.5: ("lx", LX4)}[ph]
                s.wait_ge({"sv": sv, "ld": ld, "lx": lx}[tgt[0]], tgt[1])
                s.dma_start(out=out_d[0:128, :], in_=xst[:, 0:D]).then_inc(lx, 16)
                s.wait_ge(lx, (LX1 if ph <= 3 else LX2 if ph == 4 else
                               LX3 if ph == 5 else LX4) + 16)
    return nc


def _host_consts():
    ident = np.eye(128, dtype=np.float32)
    e4 = np.zeros((128, 4), np.float32)
    for j in range(4):
        e4[32 * j:32 * (j + 1), j] = 1.0
    g8 = np.zeros((128, 128), np.float32)
    for e in range(8):
        for r in range(4):
            g8[e, 4 * e + r] = 1.0
    sel = np.zeros((128, NPAIR * 32), np.float32)
    for p in range(NPAIR):
        sel[32, p * 32 + 2 * p] = 1.0
        sel[96, p * 32 + 2 * p + 1] = 1.0
    ep = np.zeros((32, NPAIR * 128), np.float32)
    for p in range(NPAIR):
        ep[2 * p, p * 128 + 0:p * 128 + 32] = 1.0
        ep[2 * p + 1, p * 128 + 64:p * 128 + 96] = 1.0
    ones1 = np.ones((1, 128), np.float32)
    return ident, e4, g8, sel, ep, ones1


def kernel(L0_attn_x, L0_attn_w_qkv, L0_attn_w_out, L0_attn_b_out, L0_attn_ln_gamma,
           L0_attn_ln_beta, L0_ffn_W_lin, L0_ffn_b_lin, L0_ffn_gate_w, L0_ffn_left,
           L0_ffn_right):
    x = np.ascontiguousarray(np.asarray(L0_attn_x, dtype=np.float32))
    wqkv = np.ascontiguousarray(np.asarray(L0_attn_w_qkv, dtype=np.float32))
    wout = np.asarray(L0_attn_w_out, dtype=np.float32)
    bout = np.asarray(L0_attn_b_out, dtype=np.float32)
    gam = np.asarray(L0_attn_ln_gamma, dtype=np.float32)
    bet = np.asarray(L0_attn_ln_beta, dtype=np.float32)
    wlin = np.asarray(L0_ffn_W_lin, dtype=np.float32)
    blin = np.asarray(L0_ffn_b_lin, dtype=np.float32)
    gw = np.asarray(L0_ffn_gate_w, dtype=np.float32)
    left = np.asarray(L0_ffn_left, dtype=np.float32)
    right = np.asarray(L0_ffn_right, dtype=np.float32)

    wlinT = np.ascontiguousarray(wlin.T)
    gwT = np.ascontiguousarray(gw.T)
    leftf = np.ascontiguousarray(left.transpose(1, 0, 2).reshape(D, 32))
    rightp = np.zeros((128, D), np.float32)
    rightp[0:32] = right.reshape(32, D)
    wout_h = np.zeros((NPAIR * 128, D), np.float32)
    for p in range(NPAIR):
        wout_h[128 * p + 0:128 * p + 32] = wout[32 * (2 * p):32 * (2 * p) + 32]
        wout_h[128 * p + 64:128 * p + 96] = wout[32 * (2 * p + 1):32 * (2 * p + 1) + 32]
    rep = lambda a: np.ascontiguousarray(np.broadcast_to(a.reshape(1, D), (128, D)))
    ident, e4, g8, sel, ep, ones1 = _host_consts()

    nc = build_nc()
    eps_rep = np.ascontiguousarray(np.broadcast_to(
        np.array([EPS1, EPS2], np.float32).reshape(1, 2), (128, 2)))
    shared = dict(wqkv=wqkv, wlinT=wlinT, wout_h=wout_h, gwT=gwT, leftf=leftf,
                  rightp=rightp, blin_rep=rep(blin), bout_rep=rep(bout),
                  gam_rep=rep(gam), bet_rep=rep(bet), ident=ident, e4=e4,
                  g8pad=g8, sel=sel, epair=ep, ones1=ones1, eps_rep=eps_rep)
    in_maps = [dict(shared, x=np.ascontiguousarray(x[b])) for b in range(NCORES)]
    res = run_bass_kernel_spmd(nc, in_maps, core_ids=list(range(NCORES)))
    out = np.stack([res.results[b]["out"] for b in range(NCORES)], axis=0)
    return out.astype(np.float32)


# version 11 (final)
# speedup vs baseline: 1.7989x; 1.0654x over previous
import contextlib
import numpy as np
import ml_dtypes
import concourse.bass as bass
import concourse.mybir as mybir
from concourse.bass_utils import run_bass_kernel_spmd

F32 = mybir.dt.float32
BF16 = mybir.dt.bfloat16
AF = mybir.ActivationFunctionType
ALU = mybir.AluOpType
AX = mybir.AxisListType

NCORES = 8
T, D = 2048, 1024
NH, DH = 32, 32
NT = T // 128
NQT = T // 512
KT = T // 128
NPAIR = 16
SCALE = float(DH) ** -0.5
EPS1, EPS2 = 1e-5, 1e-6
BIG = 1e30


def build_nc(ph=6):
    nc = bass.Bass("TRN2", target_bir_lowering=False)
    dram = lambda name, shape, dtype=F32, kind="ExternalInput": nc.dram_tensor(
        name, shape, dtype, kind=kind).ap()
    x_d = dram("x", [T, D])
    wqkv_d = dram("wqkv", [D, 3 * D])
    wlinT_d = dram("wlinT", [D, D])
    wout_d = dram("wout_h", [NPAIR * 128, D])
    gwT_d = dram("gwT", [D, 8])
    leftf_d = dram("leftf", [D, 32])
    rightp_d = dram("rightp", [128, D])
    blin_d = dram("blin_rep", [128, D])
    bout_d = dram("bout_rep", [128, D])
    gam_d = dram("gam_rep", [128, D])
    bet_d = dram("bet_rep", [128, D])
    ident_d = dram("ident", [128, 128])
    e4_d = dram("e4", [128, 4])
    g8_d = dram("g8pad", [128, 128])
    sel_d = dram("sel", [128, NPAIR * 32])
    ep_d = dram("epair", [32, NPAIR * 128])
    ones1_d = dram("ones1", [1, 128])
    eps_d = dram("eps_rep", [128, 2])
    out_d = dram("out", [T, D], F32, kind="ExternalOutput")
    pre_d = dram("pre_scr", [T, D], BF16, kind="Internal")
    g_d = dram("g_scr", [T, D], BF16, kind="Internal")
    bnd_d = dram("bnd_scr", [4, 64], F32, kind="Internal")
    nb_d = dram("negB_scr", [128, 1], F32, kind="Internal")

    ctx = contextlib.ExitStack()
    with ctx:
        sb = lambda name, shape, dtype: ctx.enter_context(nc.sbuf_tensor(name, shape, dtype))
        psum = lambda name, shape, dtype=F32: ctx.enter_context(nc.psum_tensor(name, shape, dtype))

        q_sb = sb("q_sb", [128, 8 * T], BF16)
        k_sb = sb("k_sb", [128, 8 * T], BF16)
        v_sb = sb("v_sb", [128, KT * NH * 33], BF16)
        ident_sb = sb("ident_sb", [128, 128], BF16)
        sel_sb = sb("sel_sb", [128, NPAIR * 32], BF16)
        ep_sb = sb("ep_sb", [32, NPAIR * 128], BF16)
        e4_sb = sb("e4_sb", [128, 4], BF16)
        g8_sb = sb("g8_sb", [128, 128], BF16)
        st1 = sb("st1", [128, 32], F32)
        xst = sb("xst", [128, 2 * D], F32)
        scr2 = sb("scr2", [128, D], F32)
        xn_tm = sb("xn_tm", [128, 2 * D], BF16)
        prest = xn_tm
        blin_sb = sb("blin_sb", [128, D], F32)
        biasB_sb = sb("biasB_sb", [128, 1], F32)
        eps_sb = sb("eps_sb", [128, 2], F32)
        mqk = scr2[0:1, 256:512]
        ones1_sb = scr2[0:1, 512:640]
        bnd = scr2[0:4, 640:720]

        ps_big = psum("ps_big", [128, 2048])
        ps_two = psum("ps_two", [128, 1024])
        ps_sm = psum("ps_sm", [128, 512])
        ps_tr = psum("ps_tr", [128, 256], BF16)

        sem = lambda name: ctx.enter_context(nc.semaphore(name))
        ld = sem("ld")
        lx = sem("lx")
        sv = sem("sv")
        sa = sem("sa")
        sm = sem("sm")
        s_sc = sem("ssc")
        s_xp = sem("sxp")
        s_av = sem("sav")
        s_ev = sem("sev")

        block = ctx.enter_context(nc.Block())

        G0_DMAS = 8 * 4 + 6
        G0_LD = 16 * G0_DMAS
        LDW = lambda dc: G0_LD + 16 * (dc + 1)
        LD_2A = G0_LD + 16 * 16
        LD_BND1 = LD_2A + 16
        LD_BND = LD_2A + 16 * 3
        LD_NB1 = LD_BND + 16
        LD_NB = LD_BND + 32
        LD_WOUT = lambda i: LD_NB + 16 * (i + 1)
        LD_TOT = LD_NB + 16 * NPAIR

        LX_CONST = 16 * 3
        LX_X1 = lambda t: LX_CONST + 16 * (t + 1)
        LX1 = LX_CONST + 16 * NT
        LX_PST = lambda t: LX1 + 16 * (t + 1)
        LX2 = LX1 + 16 * NT
        _pr = {t: (t if t < 3 else 2 * t - 2) for t in range(NT)}
        _ps = {t: (3 + 2 * t if t <= NT - 4 else NT + t) for t in range(NT)}
        LX_PLD = lambda t: LX2 + 16 * (_pr[t] + 1)
        LX_GST = lambda t: LX2 + 16 * (_ps[t] + 1)
        LX3 = LX2 + 16 * NT
        LX4 = LX2 + 16 * 2 * NT
        LX_EPI = lambda qt: LX4 + 16 * 12 * qt
        LX_EXG = lambda qt, ts: LX_EPI(qt) + 16 * (2 * ts + 1)
        LX_EGG = lambda qt, ts: LX_EPI(qt) + 16 * (2 * ts + 2)
        LX_EOUT = lambda qt, ts: LX_EPI(qt) + 16 * 8 + 16 * (ts + 1)
        LX_TOT = LX4 + 16 * 12 * NQT

        P1_SV, P1_SA, P1_SM = 8 * NT, 3 * NT, 8 * NT
        SV_2A = P1_SV + 12 * 16
        P2A_SVD = lambda dc: P1_SV + 12 * dc
        P2A_SM = lambda dc: P1_SM + 8 * dc
        SM_2A = P1_SM + 8 * 16
        SA_BF = P1_SA + 2
        SV_BF = SV_2A + 4
        P2B_SV = lambda t: SV_BF + 12 * t
        P2B_SM = lambda t: SM_2A + 2 * t
        P2B_SA = lambda t: SA_BF + t
        SV_2B, SM_2B, SA_2B = SV_BF + 12 * NT, SM_2A + 2 * NT, SA_BF + NT
        P2C_SV = lambda qt: SV_2B + 2 * qt
        P2C_SM = lambda qt: SM_2B + 2 * qt
        SV_2C, SM_2C = SV_2B + 2 * NQT, SM_2B + 2 * NQT
        P2D_SV = lambda t: SV_2C + 2 * t
        P2D_SM = lambda t: SM_2C + 2 * t
        SV_2D, SM_2D = SV_2C + 2 * NT, SM_2C + 2 * NT
        SA_SIL = lambda t: SA_2B + t + 1
        SA_3P = SA_2B + NT
        QT_SC = lambda qt: 256 * qt
        QT_AV = lambda qt: 256 * qt
        QT_XP = lambda qt: 256 * qt
        QT_EV = lambda qt: 16 * qt
        P3_SV = lambda qt: SV_2D + 57 * qt
        P3_SM = lambda qt: SM_2D + 25 * qt
        P3_SA = lambda qt: SA_3P + 12 * qt
        SV_TOT = SV_2D + 57 * NQT
        SM_TOT = SM_2D + 25 * NQT
        SA_TOT = SA_3P + 12 * NQT

        @block.gpsimd
        def _(g):
            n = 0
            for dst, src in [(ident_sb[:, :], ident_d), (e4_sb[:, :], e4_d),
                             (g8_sb[:, :], g8_d), (sel_sb[:, :], sel_d),
                             (ep_sb[0:32, :], ep_d)]:
                g.dma_start(out=dst, in_=src).then_inc(ld, 16)
                n += 1
            g.memset(v_sb[:, :], 1.0)
            assert n == 5

        @block.sync
        def _(s):
            for dst, src in [(blin_sb[:, :], blin_d), (ones1_sb[:, :], ones1_d),
                             (eps_sb[:, :], eps_d)]:
                s.dma_start(out=dst, in_=src).then_inc(lx, 16)
            for t in range(NT):
                if t >= 2:
                    s.wait_ge(sv, 8 * (t - 2) + 4)
                s.dma_start(out=xst[:, (t % 2) * D:(t % 2 + 1) * D],
                            in_=x_d[t * 128:(t + 1) * 128, :]).then_inc(lx, 16)

        ctx2 = ctx.enter_context(contextlib.ExitStack())
        sb2 = lambda name, shape, dtype: ctx2.enter_context(nc.sbuf_tensor(name, shape, dtype))
        xn_sb = sb2("xn_sb", [128, 8 * T], BF16)
        wv_sb = sb2("wv_sb", [128, 8 * D], BF16)
        wlin_sb = sb2("wlin_sb", [128, 8 * D], BF16)
        gwT_sb = sb2("gwT_sb", [128, 64], BF16)
        leftf_sb = sb2("leftf_sb", [128, 256], BF16)
        rightp_sb = sb2("rightp_sb", [128, D], BF16)
        wq_st = sb2("wq_st", [128, 2 * 1024], BF16)
        sqs = sb2("sqs", [128, 512], BF16)
        gates_bf = sb2("gates_bf", [128, NT * 8], BF16)
        gpad = sb2("gpad", [128, 128], BF16)
        gfm_sb = sb2("gfm_sb", [128, NT * 128], BF16)
        t_tmp = sb2("t_tmp", [32, 512], BF16)
        t1pad = sb2("t1pad", [128, T], BF16)
        scr = sb2("scr", [128, D], BF16)

        @block.gpsimd
        def _(g):
            for c in range(8):
                g.dma_start(out=wv_sb[:, c * D:(c + 1) * D],
                            in_=wqkv_d[c * 128:(c + 1) * 128, 2 * D:3 * D]).then_inc(ld, 16)
                g.dma_start(out=wlin_sb[:, c * D:(c + 1) * D],
                            in_=wlinT_d[c * 128:(c + 1) * 128, :]).then_inc(ld, 16)
                g.dma_start(out=gwT_sb[:, c * 8:(c + 1) * 8],
                            in_=gwT_d[c * 128:(c + 1) * 128, :]).then_inc(ld, 16)
                g.dma_start(out=leftf_sb[:, c * 32:(c + 1) * 32],
                            in_=leftf_d[c * 128:(c + 1) * 128, :]).then_inc(ld, 16)
            g.dma_start(out=rightp_sb[:, :], in_=rightp_d).then_inc(ld, 16)
            g.memset(t1pad[:, :], 0.0)
            g.memset(gpad[:, :], 0.0)
            for dc in range(16 if ph >= 2 else 0):
                if dc >= 2:
                    g.wait_ge(sm, P2A_SM(dc - 1) + 4)
                g.dma_start(
                    out=wq_st[:, (dc % 2) * 1024:(dc % 2 + 1) * 1024].rearrange(
                        "p (c m) -> p c m", c=8),
                    in_=wqkv_d[:, dc * 128:(dc + 1) * 128].rearrange(
                        "(c p) m -> p c m", p=128)).then_inc(ld, 16)

        @block.vector
        def _(v):
            for t in range(NT):
                b = 8 * t
                xt = xst[:, (t % 2) * D:(t % 2 + 1) * D]
                v.wait_ge(lx, LX_X1(t))
                if t >= 2:
                    v.wait_ge(sm, 8 * (t - 2) + 8)
                v.tensor_scalar(xn_tm[:, (t % 2) * D:(t % 2 + 1) * D], xt, 1.0 / D,
                                0.0, op0=ALU.mult, op1=ALU.add,
                                accum_out=st1[:, 1:2]).then_inc(sv, 1)
                v.wait_ge(sa, 3 * t + 1)
                v.wait_ge(sv, b + 1)
                v.scalar_tensor_tensor(st1[:, 3:4], st1[:, 1:2], -1.0, st1[:, 1:2],
                                       op0=ALU.mult, op1=ALU.mult).then_inc(sv, 1)
                v.wait_ge(sv, b + 2)
                if t >= 1:
                    v.wait_ge(sa, 3 * (t - 1) + 2)
                v.tensor_scalar(st1[:, 5:6], st1[:, 2:3], 1.0 / D, st1[:, 3:4],
                                op0=ALU.mult, op1=ALU.add).then_inc(sv, 1)
                v.wait_ge(sa, 3 * t + 3)
                v.tensor_scalar(xn_tm[:, (t % 2) * D:(t % 2 + 1) * D], xt,
                                st1[:, 1:2], st1[:, 6:7],
                                op0=ALU.subtract, op1=ALU.mult).then_inc(sv, 1)
                xnv = xn_sb[:, :].rearrange("p (c tok) -> p c tok", c=8)
                for j in range(4):
                    v.wait_ge(sm, 8 * t + 2 * (j + 1))
                    v.tensor_copy(xnv[:, 2 * j:2 * j + 2, t * 128:(t + 1) * 128],
                                  ps_tr[:, 0:256].rearrange("p (c tok) -> p c tok", c=2)
                                  ).then_inc(sv, 1)

        @block.scalar
        def _(sc):
            for t in range(NT):
                xt = xst[:, (t % 2) * D:(t % 2 + 1) * D]
                sc.wait_ge(lx, LX_X1(t))
                if t >= 1:
                    sc.wait_ge(sv, 8 * (t - 1) + 3)
                sc.activation(scr[:, :], xt, AF.Square,
                              accum_out=st1[:, 2:3]).then_inc(sa, 1)
                sc.wait_ge(sv, 8 * t + 3)
                sc.activation(st1[:, 7:8], st1[:, 5:6], AF.Ln,
                              bias=eps_sb[:, 0:1], scale=1.0).then_inc(sa, 1)
                sc.wait_ge(sa, 3 * t + 2)
                if t >= 1:
                    sc.wait_ge(sv, 8 * (t - 1) + 4)
                sc.activation(st1[:, 6:7], st1[:, 7:8], AF.Exp,
                              bias=0.0, scale=-0.5).then_inc(sa, 1)

        @block.tensor
        def _(t_):
            t_.wait_ge(ld, 16 * 5)
            for t in range(NT):
                for j in range(4):
                    t_.wait_ge(sv, 8 * t + 4)
                    prev = 4 * t + j - 1
                    if prev >= 0:
                        pt, pj = prev // 4, prev % 4
                        t_.wait_ge(sv, 8 * pt + 4 + pj + 1)
                    for h in range(2):
                        c = 2 * j + h
                        t_.transpose(ps_tr[:, h * 128:(h + 1) * 128],
                                     xn_tm[:, (t % 2) * D + c * 128:(t % 2) * D + (c + 1) * 128],
                                     ident_sb[:, :]).then_inc(sm, 1)

        @block.tensor
        def _(t_):
            for dc in range(16 if ph >= 2 else 0):
                t_.wait_ge(ld, LDW(dc))
                for qt in range(NQT):
                    idx = 4 * dc + qt
                    if idx >= 4:
                        pdc, pqt = (idx - 4) // 4, (idx - 4) % 4
                        t_.wait_ge(sv, P2A_SVD(pdc) + pqt + 1)
                    ins = None
                    for c in range(8):
                        ins = t_.matmul(
                            ps_big[:, qt * 512:(qt + 1) * 512],
                            wq_st[:, (dc % 2) * 1024 + c * 128:(dc % 2) * 1024 + (c + 1) * 128],
                            xn_sb[:, c * T + qt * 512:c * T + (qt + 1) * 512],
                            start=(c == 0), stop=(c == 7), skip_group_check=True)
                    ins.then_inc(sm, 1)
                for qt in range(NQT):
                    t_.wait_ge(sv, P2A_SVD(dc) + 5 + 2 * qt)
                    t_.matmul(ps_sm[0:4, 0:512], e4_sb[:, :], sqs[:, :],
                              start=True, stop=True,
                              skip_group_check=True).then_inc(sm, 1)

        @block.vector
        def _(v):
            for dc in range(16 if ph >= 2 else 0):
                dst = q_sb if dc < 8 else k_sb
                cc = dc % 8
                base = P2A_SVD(dc)
                for qt in range(NQT):
                    v.wait_ge(sm, P2A_SM(dc) + qt + 1)
                    v.tensor_copy(dst[:, cc * T + qt * 512:cc * T + (qt + 1) * 512],
                                  ps_big[:, qt * 512:(qt + 1) * 512]).then_inc(sv, 1)
                for qt in range(NQT):
                    v.wait_ge(sv, base + 4)
                    v.wait_ge(sm, P2A_SM(dc) + 4 + qt)
                    src = dst[:, cc * T + qt * 512:cc * T + (qt + 1) * 512]
                    v.tensor_tensor(sqs[:, :], src, src, op=ALU.mult).then_inc(sv, 1)
                    v.wait_ge(sm, P2A_SM(dc) + 4 + qt + 1)
                    v.reduce_max(bnd[0:4, dc * 4 + qt:dc * 4 + qt + 1],
                                 ps_sm[0:4, 0:512], axis=AX.X).then_inc(sv, 1)

        @block.gpsimd
        def _(g):
            if ph < 2:
                return
            g.wait_ge(sv, SV_2A)
            g.dma_start(out=bnd_d, in_=bnd[0:4, 0:64]).then_inc(ld, 16)
            g.wait_ge(ld, LD_BND1)
            g.dma_start(out=mqk[:, 0:128].rearrange("a (p c) -> a p c", p=4),
                        in_=bnd_d[:, 0:32]).then_inc(ld, 16)
            g.dma_start(out=mqk[:, 128:256].rearrange("a (p c) -> a p c", p=4),
                        in_=bnd_d[:, 32:64]).then_inc(ld, 16)

        @block.vector
        def _(v):
            if ph < 2:
                return
            v.wait_ge(ld, LD_BND)
            v.reduce_max(st1[0:1, 8:9], mqk[:, 0:128], axis=AX.X)
            v.reduce_max(st1[0:1, 9:10], mqk[:, 128:256], axis=AX.X).then_inc(sv, 1)
            v.wait_ge(sv, SV_2A + 1)
            v.tensor_tensor(st1[0:1, 10:11], st1[0:1, 8:9], st1[0:1, 9:10],
                            op=ALU.mult).then_inc(sv, 1)
            v.wait_ge(sa, SA_BF)
            v.tensor_scalar(st1[0:1, 12:13], st1[0:1, 11:12], -SCALE * 1.02, None,
                            op0=ALU.mult).then_inc(sv, 1)
            v.wait_ge(sv, SV_2A + 3)
            v.tensor_scalar(mqk[:, 0:128], ones1_sb[0:1, :], st1[0:1, 12:13], None,
                            op0=ALU.mult).then_inc(sv, 1)

        @block.scalar
        def _(sc):
            if ph < 2:
                return
            sc.wait_ge(sv, SV_2A + 2)
            sc.activation(st1[0:1, 13:14], st1[0:1, 10:11], AF.Ln,
                          bias=0.0, scale=1.0).then_inc(sa, 1)
            sc.wait_ge(sa, P1_SA + 1)
            sc.activation(st1[0:1, 11:12], st1[0:1, 13:14], AF.Exp,
                          bias=0.0, scale=0.5).then_inc(sa, 1)

        @block.gpsimd
        def _(g):
            if ph < 2:
                return
            g.wait_ge(sv, SV_BF)
            g.dma_start(out=nb_d[:, :].rearrange("p a -> (p a)").rearrange(
                "(a q) -> a q", a=1), in_=mqk[:, 0:128]).then_inc(ld, 16)
            g.wait_ge(ld, LD_NB1)
            g.dma_start(out=biasB_sb[:, :], in_=nb_d).then_inc(ld, 16)

        @block.tensor
        def _(t_):
            for t in range(NT if ph >= 3 else 0):
                if t == 0:
                    t_.wait_ge(sv, SV_2A)
                if t >= 2:
                    t_.wait_ge(sv, P2B_SV(t - 2) + 1)
                if t >= 1:
                    t_.wait_ge(sv, P2B_SV(t - 1) + 7)
                    t_.wait_ge(sa, P2B_SA(t - 1) + 1)
                vb = (t % 2) * 1024
                ins = None
                for c in range(8):
                    xnc = xn_sb[:, c * T + t * 128:c * T + (t + 1) * 128]
                    t_.matmul(ps_big[:, vb:vb + 512], xnc, wv_sb[:, c * D:c * D + 512],
                              start=(c == 0), stop=(c == 7), skip_group_check=True)
                    t_.matmul(ps_big[:, vb + 512:vb + 1024], xnc, wv_sb[:, c * D + 512:(c + 1) * D],
                              start=(c == 0), stop=(c == 7), skip_group_check=True)
                    ins = t_.matmul(ps_sm[:, 0:8], xnc, gwT_sb[:, c * 8:(c + 1) * 8],
                                    start=(c == 0), stop=(c == 7), skip_group_check=True)
                ins.then_inc(sm, 1)
                t_.wait_ge(sv, P2B_SV(t) + 11)
                if t >= 1:
                    t_.wait_ge(sv, P2B_SV(t - 1) + 12)
                t_.transpose(ps_tr[:, 128:256], gpad[:, :], ident_sb[:, :]).then_inc(sm, 1)

        @block.scalar
        def _(sc):
            for t in range(NT if ph >= 3 else 0):
                sc.wait_ge(sv, P2B_SV(t) + 2)
                sc.activation(scr2[:, 0:8], ps_sm[:, 0:8], AF.Exp, bias=st1[:, 8:9],
                              accum_out=st1[:, 9:10]).then_inc(sa, 1)

        @block.vector
        def _(v):
            for t in range(NT if ph >= 3 else 0):
                b = P2B_SV(t)
                v.wait_ge(sm, P2B_SM(t) + 1)
                vb = (t % 2) * 1024
                v.tensor_copy(
                    v_sb[:, t * NH * 33:(t + 1) * NH * 33].rearrange(
                        "p (h w) -> p h w", h=NH)[:, :, 0:32],
                    ps_big[:, vb:vb + 1024].rearrange("p (h w) -> p h w", h=NH)).then_inc(sv, 1)
                v.reduce_max(st1[:, 8:9], ps_sm[:, 0:8], axis=AX.X,
                             negate=True).then_inc(sv, 1)
                v.wait_ge(sv, b + 2)
                v.tensor_scalar(st1[:, 10:11], st1[:, 8:9], -1.0, None,
                                op0=ALU.mult).then_inc(sv, 1)
                v.wait_ge(sv, b + 3)
                v.tensor_scalar(scr2[:, 8:16], ps_sm[:, 0:8], st1[:, 10:11], None,
                                op0=ALU.is_ge).then_inc(sv, 1)
                v.wait_ge(sv, b + 4)
                v.scalar_tensor_tensor(scr2[:, 16:24], scr2[:, 8:16], -BIG,
                                       ps_sm[:, 0:8], op0=ALU.mult,
                                       op1=ALU.add).then_inc(sv, 1)
                v.wait_ge(sv, b + 5)
                v.reduce_max(st1[:, 11:12], scr2[:, 16:24], axis=AX.X).then_inc(sv, 1)
                v.wait_ge(sv, b + 6)
                v.tensor_scalar(scr2[:, 24:32], ps_sm[:, 0:8], st1[:, 11:12], None,
                                op0=ALU.is_ge).then_inc(sv, 1)
                v.wait_ge(sa, P2B_SA(t) + 1)
                with nc.allow_low_precision(reason="tiny gate sums"):
                    v.reciprocal(st1[:, 12:13], st1[:, 9:10]).then_inc(sv, 1)
                v.wait_ge(sv, b + 8)
                v.scalar_tensor_tensor(gates_bf[:, t * 8:(t + 1) * 8], scr2[:, 0:8],
                                       1.0, scr2[:, 24:32], op0=ALU.mult,
                                       op1=ALU.mult).then_inc(sv, 1)
                v.wait_ge(sv, b + 9)
                v.tensor_scalar(gates_bf[:, t * 8:(t + 1) * 8],
                                gates_bf[:, t * 8:(t + 1) * 8], st1[:, 12:13], None,
                                op0=ALU.mult).then_inc(sv, 1)
                v.wait_ge(sv, b + 10)
                if t >= 1:
                    v.wait_ge(sm, P2B_SM(t - 1) + 2)
                v.tensor_copy(gpad[:, 0:8],
                              gates_bf[:, t * 8:(t + 1) * 8]).then_inc(sv, 1)
                v.wait_ge(sm, P2B_SM(t) + 2)
                v.tensor_copy(gfm_sb[:, t * 128:(t + 1) * 128],
                              ps_tr[:, 128:256]).then_inc(sv, 1)

        @block.tensor
        def _(t_):
            for qt in range(NQT if ph >= 4 else 0):
                if qt == 0:
                    t_.wait_ge(sv, SV_2B)
                    t_.wait_ge(sa, SA_2B)
                else:
                    t_.wait_ge(sv, P2C_SV(qt - 1) + 1)
                ins = None
                for c in range(8):
                    ins = t_.matmul(ps_sm[0:32, :], leftf_sb[:, c * 32:(c + 1) * 32],
                                    xn_sb[:, c * T + qt * 512:c * T + (qt + 1) * 512],
                                    start=(c == 0), stop=(c == 7), skip_group_check=True)
                ins.then_inc(sm, 1)
                if qt >= 1:
                    t_.wait_ge(sv, P2C_SV(qt - 1) + 1)
                t_.matmul(ps_two[:, 0:512], g8_sb[:, :],
                          gfm_sb[:, qt * 512:(qt + 1) * 512],
                          start=True, stop=True, skip_group_check=True).then_inc(sm, 1)

        @block.vector
        def _(v):
            for qt in range(NQT if ph >= 4 else 0):
                b = P2C_SV(qt)
                v.wait_ge(sm, P2C_SM(qt) + 1)
                v.tensor_copy(t_tmp[0:32, :], ps_sm[0:32, :]).then_inc(sv, 1)
                v.wait_ge(sv, b + 1)
                v.wait_ge(sm, P2C_SM(qt) + 2)
                v.tensor_tensor(t1pad[0:32, qt * 512:(qt + 1) * 512],
                                ps_two[0:32, 0:512], t_tmp[0:32, :],
                                op=ALU.mult).then_inc(sv, 1)

        @block.tensor
        def _(t_):
            for t in range(NT if ph >= 4 else 0):
                if t <= 1:
                    t_.wait_ge(sv, SV_2B)
                t_.wait_ge(sv, P2C_SV(t // 4) + 2)
                if t >= 2:
                    t_.wait_ge(sv, P2D_SV(t - 2) + 2)
                for half in range(2):
                    hb = (t % 2) * 1024
                    ins = None
                    for c in range(8):
                        ins = t_.matmul(
                            ps_big[:, hb + half * 512:hb + (half + 1) * 512],
                            xn_sb[:, c * T + t * 128:c * T + (t + 1) * 128],
                            wlin_sb[:, c * D + half * 512:c * D + (half + 1) * 512],
                            start=(c == 0), stop=False, skip_group_check=True)
                    ins = t_.matmul(
                        ps_big[:, hb + half * 512:hb + (half + 1) * 512],
                        t1pad[:, t * 128:(t + 1) * 128],
                        rightp_sb[:, half * 512:(half + 1) * 512],
                        start=False, stop=True, skip_group_check=True)
                    ins.then_inc(sm, 1)

        @block.vector
        def _(v):
            for t in range(NT if ph >= 4 else 0):
                b = P2D_SV(t)
                hb = (t % 2) * 1024
                for half in range(2):
                    v.wait_ge(sm, P2D_SM(t) + half + 1)
                    if t >= 2:
                        v.wait_ge(lx, LX_PST(t - 2))
                    v.tensor_tensor(prest[:, (t % 2) * D + half * 512:(t % 2) * D + (half + 1) * 512],
                                    ps_big[:, hb + half * 512:hb + (half + 1) * 512],
                                    blin_sb[:, half * 512:(half + 1) * 512],
                                    op=ALU.add).then_inc(sv, 1)

        @block.sync
        def _(s):
            for t in range(NT if ph >= 4 else 0):
                s.wait_ge(sv, P2D_SV(t) + 2)
                s.dma_start(out=pre_d[t * 128:(t + 1) * 128, :],
                            in_=prest[:, (t % 2) * D:(t % 2 + 1) * D]).then_inc(lx, 16)

        ctx2.close()
        ctx3 = ctx.enter_context(contextlib.ExitStack())
        sb3 = lambda name, shape, dtype: ctx3.enter_context(nc.sbuf_tensor(name, shape, dtype))
        wout_sb = sb3("wout_sb", [128, NPAIR * D], BF16)
        attn = sb3("attn", [128, NPAIR * 512], BF16)
        expS = sb3("expS", [128, 4 * 1024], BF16)
        rpk = sb3("rpk", [32, 512], BF16)
        gst = sb3("gst", [128, 2 * D], BF16)
        spre = sb3("spre", [128, 4 * D], F32)
        bout_sb = sb3("bout_sb", [128, D], F32)
        gam_sb = sb3("gam_sb", [128, D], F32)
        bet_sb = blin_sb

        @block.gpsimd
        def _(g):
            if ph >= 5:
                g.wait_ge(sm, SM_2D)
                g.wait_ge(sv, SV_2D)
            for p in range(NPAIR if ph >= 5 else 0):
                g.dma_start(out=wout_sb[:, p * D:(p + 1) * D],
                            in_=wout_d[p * 128:(p + 1) * 128, :]).then_inc(ld, 16)

        @block.sync
        def _(s):
            for dst, src in [(bout_sb[:, :], bout_d), (gam_sb[:, :], gam_d),
                             (bet_sb[:, :], bet_d)]:
                s.dma_start(out=dst, in_=src).then_inc(lx, 16) if False else None
            if ph >= 5.5:
                order = [("r", 0), ("r", 1), ("r", 2)]
                for k in range(NT - 3):
                    order += [("s", k), ("r", k + 3)]
                order += [("s", NT - 3), ("s", NT - 2), ("s", NT - 1)]
                for kind, t in order:
                    if kind == "r":
                        s.wait_ge(lx, LX_PST(NT - 1))
                        if t >= 2:
                            s.wait_ge(sa, SA_SIL(t - 2))
                        s.dma_start(out=prest[:, (t % 2) * D:(t % 2 + 1) * D],
                                    in_=pre_d[t * 128:(t + 1) * 128, :]).then_inc(lx, 16)
                    else:
                        s.wait_ge(sa, SA_SIL(t))
                        s.dma_start(out=g_d[t * 128:(t + 1) * 128, :],
                                    in_=gst[:, (t % 2) * D:(t % 2 + 1) * D]).then_inc(lx, 16)
            elif ph == 5:
                for t in range(NT):
                    s.wait_ge(lx, LX_PST(NT - 1))
                    s.dma_start(out=prest[:, (t % 2) * D:(t % 2 + 1) * D],
                                in_=pre_d[t * 128:(t + 1) * 128, :]).then_inc(lx, 16)

        @block.gpsimd
        def _(g):
            for dst, src in ([(bout_sb[:, :], bout_d), (gam_sb[:, :], gam_d),
                             (bet_sb[:, :], bet_d)] if ph >= 5 else []):
                g.dma_start(out=dst, in_=src).then_inc(ld, 16)
        LD_TOT2 = LD_TOT + 16 * 3

        @block.scalar
        def _(sc):
            for t in range(NT if ph >= 5.5 else 0):
                sc.wait_ge(lx, LX_PLD(t))
                if t >= 2:
                    sc.wait_ge(lx, LX_GST(t - 2))
                sc.activation(gst[:, (t % 2) * D:(t % 2 + 1) * D],
                              prest[:, (t % 2) * D:(t % 2 + 1) * D],
                              AF.Silu).then_inc(sa, 1)

        @block.tensor
        def _(t_):
            if ph >= 6:
                t_.wait_ge(ld, LD_TOT2)
            for qt in range(NQT if ph >= 6 else 0):
                scb, avb, xpb, evb = QT_SC(qt), QT_AV(qt), QT_XP(qt), QT_EV(qt)
                if qt > 0:
                    t_.wait_ge(sv, P3_SV(qt - 1) + 25)
                NI = NPAIR * KT

                def emit_scores(i):
                    p, kt = i // KT, i % KT
                    cp, jp = p // 2, p % 2
                    sA, sB = 64 * jp, 64 * jp + 32
                    base = 256 * qt + 16 * p
                    par = kt % 2
                    t_.wait_ge(s_xp, max(0, base + kt - 1))
                    t_.matmul(ps_big[:, par * 1024:par * 1024 + 512],
                              k_sb[sA:sA + 32, cp * T + kt * 128:cp * T + (kt + 1) * 128],
                              q_sb[sA:sA + 32, cp * T + qt * 512:cp * T + (qt + 1) * 512],
                              start=True, stop=True, tile_position=(sA, 0),
                              skip_group_check=True)
                    t_.matmul(ps_big[:, par * 1024 + 512:par * 1024 + 1024],
                              k_sb[sB:sB + 32, cp * T + kt * 128:cp * T + (kt + 1) * 128],
                              q_sb[sB:sB + 32, cp * T + qt * 512:cp * T + (qt + 1) * 512],
                              start=True, stop=True, tile_position=(sB, 0),
                              skip_group_check=True).then_inc(s_sc, 1)

                def emit_av(j):
                    p, kt = j // KT, j % KT
                    cp, jp = p // 2, p % 2
                    hA, hB = 4 * cp + 2 * jp, 4 * cp + 2 * jp + 1
                    base = 256 * qt + 16 * p
                    slot = (kt % 4) * 1024
                    ub = (p % 2) * 512
                    t_.wait_ge(s_xp, base + kt + 1)
                    if p >= 2 and kt == 0:
                        t_.wait_ge(s_ev, evb + p - 1)
                    if qt > 0 and p < 2 and kt == 0:
                        t_.wait_ge(s_ev, evb)
                    t_.matmul(ps_two[0:33, ub:ub + 512],
                              v_sb[:, kt * NH * 33 + hA * 33:kt * NH * 33 + hA * 33 + 33],
                              expS[:, slot:slot + 512],
                              start=(kt == 0), stop=(kt == KT - 1),
                              tile_position=(0, 0), skip_group_check=True)
                    t_.matmul(ps_two[64:97, ub:ub + 512],
                              v_sb[:, kt * NH * 33 + hB * 33:kt * NH * 33 + hB * 33 + 33],
                              expS[:, slot + 512:slot + 1024],
                              start=(kt == 0), stop=(kt == KT - 1),
                              tile_position=(0, 64),
                              skip_group_check=True).then_inc(s_av, 1)

                for g in range(NI // 2 + 1):
                    if 2 * g < NI:
                        emit_scores(2 * g)
                        emit_scores(2 * g + 1)
                    if g >= 1:
                        emit_av(2 * g - 2)
                        emit_av(2 * g - 1)
                for p in range(NPAIR):
                    t_.wait_ge(s_ev, evb + p + 1)
                    ins = t_.matmul(ps_sm[0:32, :], sel_sb[:, p * 32:(p + 1) * 32],
                                    attn[:, p * 512:(p + 1) * 512],
                                    start=(p == 0), stop=(p == NPAIR - 1),
                                    skip_group_check=True)
                ins.then_inc(sm, 1)
                for p in range(NPAIR):
                    t_.wait_ge(sv, max(P3_SV(qt) + 1, P3_SV(qt) + p))
                    t_.matmul(ps_big[:, (p % 2) * 512:(p % 2) * 512 + 512],
                              ep_sb[0:32, p * 128:(p + 1) * 128],
                              rpk[0:32, :], start=True, stop=True,
                              skip_group_check=True).then_inc(sm, 1)
                for ts in range(4):
                    for half in range(2):
                        if ts >= 1 or half >= 1:
                            pidx = 2 * ts + half - 2
                            if pidx >= 0:
                                t_.wait_ge(sv, P3_SV(qt) + 17 + pidx + 1)
                        ins = None
                        for p in range(NPAIR):
                            t_.wait_ge(sv, P3_SV(qt) + p + 2)
                            ins = t_.matmul(
                                ps_big[:, 1024 + half * 512:1024 + (half + 1) * 512],
                                attn[:, p * 512 + ts * 128:p * 512 + (ts + 1) * 128],
                                wout_sb[:, p * D + half * 512:p * D + (half + 1) * 512],
                                start=(p == 0), stop=(p == NPAIR - 1),
                                skip_group_check=True)
                        ins.then_inc(sm, 1)

        @block.scalar
        def _(sc):
            if ph >= 6:
                sc.wait_ge(ld, LD_NB)

            def sq_act(sc, qt, ts):
                o8 = ts * 8
                sc.wait_ge(sv, P3_SV(qt) + 17 + 2 * ts + 2)
                sc.activation(expS[:, 0:1024], spre[:, ts * D:(ts + 1) * D],
                              AF.Square, accum_out=st1[:, o8 + 2:o8 + 3]).then_inc(sa, 1)

            def lnr_acts(sc, qt, ts):
                o8 = ts * 8
                CB = P3_SV(qt) + 25 + 8 * ts
                sc.wait_ge(sv, CB + 3)
                sc.activation(st1[:, o8 + 7:o8 + 8], st1[:, o8 + 5:o8 + 6], AF.Ln,
                              bias=eps_sb[:, 1:2], scale=1.0).then_inc(sa, 1)
                sc.wait_ge(sa, P3_SA(qt) + 4 + 2 * ts + 1)
                sc.activation(st1[:, o8 + 6:o8 + 7], st1[:, o8 + 7:o8 + 8], AF.Exp,
                              bias=0.0, scale=-0.5).then_inc(sa, 1)

            for qt in range(NQT if ph >= 6 else 0):
                for p in range(NPAIR):
                    base = 256 * qt + 16 * p
                    for kt in range(KT):
                        par = kt % 2
                        slot = (kt % 4) * 1024
                        sc.wait_ge(s_sc, base + kt + 1)
                        sc.wait_ge(s_av, max(0, base + kt - 3))
                        sc.activation(expS[:, slot:slot + 1024],
                                      ps_big[:, par * 1024:par * 1024 + 1024],
                                      AF.Exp, bias=biasB_sb[:, :],
                                      scale=SCALE).then_inc(s_xp, 1)
                        if qt > 0 and p % 4 == 0 and p > 0 and kt == 4:
                            lnr_acts(sc, qt - 1, p // 4 - 1)
                if qt > 0:
                    lnr_acts(sc, qt - 1, 3)
                for ts in range(4):
                    sq_act(sc, qt, ts)
            if ph >= 6:
                for ts in range(4):
                    lnr_acts(sc, NQT - 1, ts)

        @block.vector
        def _(v):
            def chain(v, qt, ts):
                CB = P3_SV(qt) + 25 + 8 * ts
                sl = ts % 2
                o8 = ts * 8
                spt = spre[:, ts * D:(ts + 1) * D]
                v.wait_ge(sa, P3_SA(qt) + ts + 1)
                v.tensor_scalar(scr2[:, :], spt, 1.0 / D, 0.0, op0=ALU.mult,
                                op1=ALU.add, accum_out=st1[:, o8 + 1:o8 + 2]).then_inc(sv, 1)
                v.wait_ge(sv, CB + 1)
                v.scalar_tensor_tensor(st1[:, o8 + 3:o8 + 4], st1[:, o8 + 1:o8 + 2],
                                       -1.0, st1[:, o8 + 1:o8 + 2],
                                       op0=ALU.mult, op1=ALU.mult).then_inc(sv, 1)
                v.wait_ge(sv, CB + 2)
                v.tensor_scalar(st1[:, o8 + 5:o8 + 6], st1[:, o8 + 2:o8 + 3], 1.0 / D,
                                st1[:, o8 + 3:o8 + 4],
                                op0=ALU.mult, op1=ALU.add).then_inc(sv, 1)
                v.wait_ge(sa, P3_SA(qt) + 4 + 2 * ts + 2)
                v.tensor_scalar(scr2[:, :], spt, st1[:, o8 + 1:o8 + 2],
                                st1[:, o8 + 6:o8 + 7],
                                op0=ALU.subtract, op1=ALU.mult).then_inc(sv, 1)
                v.wait_ge(sv, CB + 4)
                v.tensor_tensor(scr2[:, :], scr2[:, :], gam_sb[:, :],
                                op=ALU.mult).then_inc(sv, 1)
                v.wait_ge(sv, CB + 5)
                v.tensor_tensor(scr2[:, :], scr2[:, :], bet_sb[:, :],
                                op=ALU.add).then_inc(sv, 1)
                v.wait_ge(sv, CB + 6)
                v.wait_ge(lx, LX_EGG(qt, ts))
                v.tensor_tensor(scr2[:, :], scr2[:, :], gst[:, sl * D:(sl + 1) * D],
                                op=ALU.mult).then_inc(sv, 1)
                v.wait_ge(sv, CB + 7)
                v.wait_ge(lx, LX_EXG(qt, ts))
                v.tensor_tensor(spre[:, ts * D:(ts + 1) * D], scr2[:, :],
                                xst[:, sl * D:(sl + 1) * D],
                                op=ALU.add).then_inc(sv, 1)

            for qt in range(NQT if ph >= 6 else 0):
                bq = P3_SV(qt)
                evb = QT_EV(qt)
                for p in range(NPAIR):
                    v.wait_ge(s_av, 256 * qt + 16 * p + 16)
                    v.tensor_copy(attn[:, p * 512:(p + 1) * 512],
                                  ps_two[:, (p % 2) * 512:(p % 2) * 512 + 512]).then_inc(s_ev, 1)
                    if qt > 0 and p % 4 == 3:
                        chain(v, qt - 1, p // 4)
                v.wait_ge(sm, P3_SM(qt) + 1)
                with nc.allow_low_precision(reason="1/denom bf16: uniform per-(h,q) scale"):
                    v.reciprocal(rpk[0:32, :], ps_sm[0:32, :]).then_inc(sv, 1)
                for p in range(NPAIR):
                    sl2 = (p % 2) * 512
                    v.wait_ge(sm, P3_SM(qt) + 2 + p)
                    v.tensor_tensor(attn[:, p * 512:(p + 1) * 512],
                                    attn[:, p * 512:(p + 1) * 512],
                                    ps_big[:, sl2:sl2 + 512],
                                    op=ALU.mult).then_inc(sv, 1)
                for ts in range(4):
                    v.wait_ge(sm, P3_SM(qt) + 17 + 2 * ts + 1)
                    if qt > 0:
                        v.wait_ge(lx, LX_EOUT(qt - 1, ts))
                        v.wait_ge(sv, P3_SV(qt - 1) + 25 + 8 * ts + 8)
                        v.wait_ge(sa, P3_SA(qt - 1) + ts + 1)
                    for half in range(2):
                        if half == 1:
                            v.wait_ge(sm, P3_SM(qt) + 17 + 2 * ts + 2)
                        v.tensor_tensor(spre[:, ts * D + half * 512:ts * D + (half + 1) * 512],
                                        ps_big[:, 1024 + half * 512:1024 + (half + 1) * 512],
                                        bout_sb[:, half * 512:(half + 1) * 512],
                                        op=ALU.add).then_inc(sv, 1)
            if ph >= 6:
                for ts in range(4):
                    chain(v, NQT - 1, ts)

        @block.sync
        def _(s):
            for qt in range(NQT if ph >= 6 else 0):
                for ts in range(4):
                    tok0 = qt * 512 + ts * 128
                    sl = ts % 2
                    pts = 4 * qt + ts - 2
                    if pts >= 0:
                        s.wait_ge(sv, P3_SV(pts // 4) + 25 + 8 * (pts % 4) + 8)
                    s.dma_start(out=xst[:, sl * D:(sl + 1) * D],
                                in_=x_d[tok0:tok0 + 128, :]).then_inc(lx, 16)
                    s.dma_start(out=gst[:, sl * D:(sl + 1) * D],
                                in_=g_d[tok0:tok0 + 128, :]).then_inc(lx, 16)
                for ts in range(4):
                    s.wait_ge(sv, P3_SV(qt) + 25 + 8 * ts + 8)
                    s.dma_start(out=out_d[qt * 512 + ts * 128:qt * 512 + (ts + 1) * 128, :],
                                in_=spre[:, ts * D:(ts + 1) * D]).then_inc(lx, 16)
            if ph >= 6:
                s.wait_ge(lx, LX_TOT)
            else:
                tgt = {1: ("sv", P1_SV), 2: ("ld", LD_NB), 3: ("sv", SV_2B),
                       4: ("lx", LX2), 5: ("lx", LX3), 5.5: ("lx", LX4)}[ph]
                s.wait_ge({"sv": sv, "ld": ld, "lx": lx}[tgt[0]], tgt[1])
                s.dma_start(out=out_d[0:128, :], in_=xst[:, 0:D]).then_inc(lx, 16)
                s.wait_ge(lx, (LX1 if ph <= 3 else LX2 if ph == 4 else
                               LX3 if ph == 5 else LX4) + 16)
    return nc


def _host_consts():
    ident = np.eye(128, dtype=np.float32)
    e4 = np.zeros((128, 4), np.float32)
    for j in range(4):
        e4[32 * j:32 * (j + 1), j] = 1.0
    g8 = np.zeros((128, 128), np.float32)
    for e in range(8):
        for r in range(4):
            g8[e, 4 * e + r] = 1.0
    sel = np.zeros((128, NPAIR * 32), np.float32)
    for p in range(NPAIR):
        sel[32, p * 32 + 2 * p] = 1.0
        sel[96, p * 32 + 2 * p + 1] = 1.0
    ep = np.zeros((32, NPAIR * 128), np.float32)
    for p in range(NPAIR):
        ep[2 * p, p * 128 + 0:p * 128 + 32] = 1.0
        ep[2 * p + 1, p * 128 + 64:p * 128 + 96] = 1.0
    ones1 = np.ones((1, 128), np.float32)
    return ident, e4, g8, sel, ep, ones1


def kernel(L0_attn_x, L0_attn_w_qkv, L0_attn_w_out, L0_attn_b_out, L0_attn_ln_gamma,
           L0_attn_ln_beta, L0_ffn_W_lin, L0_ffn_b_lin, L0_ffn_gate_w, L0_ffn_left,
           L0_ffn_right):
    x = np.ascontiguousarray(np.asarray(L0_attn_x, dtype=np.float32))
    wqkv = np.ascontiguousarray(np.asarray(L0_attn_w_qkv, dtype=np.float32))
    wout = np.asarray(L0_attn_w_out, dtype=np.float32)
    bout = np.asarray(L0_attn_b_out, dtype=np.float32)
    gam = np.asarray(L0_attn_ln_gamma, dtype=np.float32)
    bet = np.asarray(L0_attn_ln_beta, dtype=np.float32)
    wlin = np.asarray(L0_ffn_W_lin, dtype=np.float32)
    blin = np.asarray(L0_ffn_b_lin, dtype=np.float32)
    gw = np.asarray(L0_ffn_gate_w, dtype=np.float32)
    left = np.asarray(L0_ffn_left, dtype=np.float32)
    right = np.asarray(L0_ffn_right, dtype=np.float32)

    wlinT = np.ascontiguousarray(wlin.T)
    gwT = np.ascontiguousarray(gw.T)
    leftf = np.ascontiguousarray(left.transpose(1, 0, 2).reshape(D, 32))
    rightp = np.zeros((128, D), np.float32)
    rightp[0:32] = right.reshape(32, D)
    wout_h = np.zeros((NPAIR * 128, D), np.float32)
    for p in range(NPAIR):
        wout_h[128 * p + 0:128 * p + 32] = wout[32 * (2 * p):32 * (2 * p) + 32]
        wout_h[128 * p + 64:128 * p + 96] = wout[32 * (2 * p + 1):32 * (2 * p + 1) + 32]
    rep = lambda a: np.ascontiguousarray(np.broadcast_to(a.reshape(1, D), (128, D)))
    ident, e4, g8, sel, ep, ones1 = _host_consts()

    nc = build_nc()
    eps_rep = np.ascontiguousarray(np.broadcast_to(
        np.array([EPS1, EPS2], np.float32).reshape(1, 2), (128, 2)))
    shared = dict(wqkv=wqkv, wlinT=wlinT, wout_h=wout_h, gwT=gwT, leftf=leftf,
                  rightp=rightp, blin_rep=rep(blin), bout_rep=rep(bout),
                  gam_rep=rep(gam), bet_rep=rep(bet), ident=ident, e4=e4,
                  g8pad=g8, sel=sel, epair=ep, ones1=ones1, eps_rep=eps_rep)
    in_maps = [dict(shared, x=np.ascontiguousarray(x[b])) for b in range(NCORES)]
    res = run_bass_kernel_spmd(nc, in_maps, core_ids=list(range(NCORES)))
    out = np.stack([res.results[b]["out"] for b in range(NCORES)], axis=0)
    return out.astype(np.float32)


# version 12
# speedup vs baseline: 1.7991x; 1.0001x over previous
import contextlib
import numpy as np
import ml_dtypes
import concourse.bass as bass
import concourse.mybir as mybir
from concourse.bass_utils import run_bass_kernel_spmd

F32 = mybir.dt.float32
BF16 = mybir.dt.bfloat16
AF = mybir.ActivationFunctionType
ALU = mybir.AluOpType
AX = mybir.AxisListType

NCORES = 8
T, D = 2048, 1024
NH, DH = 32, 32
NT = T // 128
NQT = T // 512
KT = T // 128
NPAIR = 16
SCALE = float(DH) ** -0.5
EPS1, EPS2 = 1e-5, 1e-6
BIG = 1e30


def build_nc(ph=6):
    nc = bass.Bass("TRN2", target_bir_lowering=False)
    dram = lambda name, shape, dtype=F32, kind="ExternalInput": nc.dram_tensor(
        name, shape, dtype, kind=kind).ap()
    x_d = dram("x", [T, D])
    wqkv_d = dram("wqkv", [D, 3 * D])
    wlinT_d = dram("wlinT", [D, D])
    wout_d = dram("wout_h", [NPAIR * 128, D])
    gwT_d = dram("gwT", [D, 8])
    leftf_d = dram("leftf", [D, 32])
    rightp_d = dram("rightp", [128, D])
    blin_d = dram("blin_rep", [128, D])
    bout_d = dram("bout_rep", [128, D])
    gam_d = dram("gam_rep", [128, D])
    bet_d = dram("bet_rep", [128, D])
    ident_d = dram("ident", [128, 128])
    e4_d = dram("e4", [128, 4])
    g8_d = dram("g8pad", [128, 128])
    sel_d = dram("sel", [128, NPAIR * 32])
    ep_d = dram("epair", [32, NPAIR * 128])
    ones1_d = dram("ones1", [1, 128])
    eps_d = dram("eps_rep", [128, 2])
    out_d = dram("out", [T, D], F32, kind="ExternalOutput")
    pre_d = dram("pre_scr", [T, D], BF16, kind="Internal")
    g_d = dram("g_scr", [T, D], BF16, kind="Internal")
    bnd_d = dram("bnd_scr", [4, 64], F32, kind="Internal")
    nb_d = dram("negB_scr", [128, 1], F32, kind="Internal")

    ctx = contextlib.ExitStack()
    with ctx:
        sb = lambda name, shape, dtype: ctx.enter_context(nc.sbuf_tensor(name, shape, dtype))
        psum = lambda name, shape, dtype=F32: ctx.enter_context(nc.psum_tensor(name, shape, dtype))

        q_sb = sb("q_sb", [128, 8 * T], BF16)
        k_sb = sb("k_sb", [128, 8 * T], BF16)
        v_sb = sb("v_sb", [128, KT * NH * 33], BF16)
        ident_sb = sb("ident_sb", [128, 128], BF16)
        sel_sb = sb("sel_sb", [128, NPAIR * 32], BF16)
        ep_sb = sb("ep_sb", [32, NPAIR * 128], BF16)
        e4_sb = sb("e4_sb", [128, 4], BF16)
        g8_sb = sb("g8_sb", [128, 128], BF16)
        st1 = sb("st1", [128, 32], F32)
        xst = sb("xst", [128, 2 * D], F32)
        scr2 = sb("scr2", [128, D], F32)
        xn_tm = sb("xn_tm", [128, 2 * D], BF16)
        prest = xn_tm
        blin_sb = sb("blin_sb", [128, D], F32)
        biasB_sb = sb("biasB_sb", [128, 1], F32)
        eps_sb = sb("eps_sb", [128, 2], F32)
        mqk = scr2[0:1, 256:512]
        ones1_sb = scr2[0:1, 512:640]
        bnd = scr2[0:4, 640:720]

        ps_big = psum("ps_big", [128, 2048])
        ps_two = psum("ps_two", [128, 1024])
        ps_sm = psum("ps_sm", [128, 512])
        ps_tr = psum("ps_tr", [128, 256], BF16)

        sem = lambda name: ctx.enter_context(nc.semaphore(name))
        ld = sem("ld")
        lx = sem("lx")
        sv = sem("sv")
        sa = sem("sa")
        sm = sem("sm")
        s_sc = sem("ssc")
        s_xp = sem("sxp")
        s_av = sem("sav")
        s_ev = sem("sev")

        block = ctx.enter_context(nc.Block())

        G0_DMAS = 8 * 4 + 6
        G0_LD = 16 * G0_DMAS
        LDW = lambda dc: G0_LD + 16 * (dc + 1)
        LD_2A = G0_LD + 16 * 16
        LD_BND1 = LD_2A + 16
        LD_BND = LD_2A + 16 * 3
        LD_NB1 = LD_BND + 16
        LD_NB = LD_BND + 32
        LD_WOUT = lambda i: LD_NB + 16 * (i + 1)
        LD_TOT = LD_NB + 16 * NPAIR

        LX_CONST = 16 * 3
        LX_X1 = lambda t: LX_CONST + 16 * (t + 1)
        LX1 = LX_CONST + 16 * NT
        LX_PST = lambda t: LX1 + 16 * (t + 1)
        LX2 = LX1 + 16 * NT
        _pr = {t: (t if t < 3 else 2 * t - 2) for t in range(NT)}
        _ps = {t: (3 + 2 * t if t <= NT - 4 else NT + t) for t in range(NT)}
        LX_PLD = lambda t: LX2 + 16 * (_pr[t] + 1)
        LX_GST = lambda t: LX2 + 16 * (_ps[t] + 1)
        LX3 = LX2 + 16 * NT
        LX4 = LX2 + 16 * 2 * NT
        LX_EPI = lambda qt: LX4 + 16 * 12 * qt
        LX_EXG = lambda qt, ts: LX_EPI(qt) + 16 * (2 * ts + 1)
        LX_EGG = lambda qt, ts: LX_EPI(qt) + 16 * (2 * ts + 2)
        LX_EOUT = lambda qt, ts: LX_EPI(qt) + 16 * 8 + 16 * (ts + 1)
        LX_TOT = LX4 + 16 * 12 * NQT

        P1_SV, P1_SA, P1_SM = 8 * NT, 3 * NT, 8 * NT
        SV_2A = P1_SV + 12 * 16
        P2A_SVD = lambda dc: P1_SV + 12 * dc
        P2A_SM = lambda dc: P1_SM + 8 * dc
        SM_2A = P1_SM + 8 * 16
        SA_BF = P1_SA + 2
        SV_BF = SV_2A + 4
        P2B_SV = lambda t: SV_BF + 12 * t
        P2B_SM = lambda t: SM_2A + 2 * t
        P2B_SA = lambda t: SA_BF + t
        SV_2B, SM_2B, SA_2B = SV_BF + 12 * NT, SM_2A + 2 * NT, SA_BF + NT
        P2C_SV = lambda qt: SV_2B + 2 * qt
        P2C_SM = lambda qt: SM_2B + 2 * qt
        SV_2C, SM_2C = SV_2B + 2 * NQT, SM_2B + 2 * NQT
        P2D_SV = lambda t: SV_2C + 2 * t
        P2D_SM = lambda t: SM_2C + 2 * t
        SV_2D, SM_2D = SV_2C + 2 * NT, SM_2C + 2 * NT
        SA_SIL = lambda t: SA_2B + t + 1
        SA_3P = SA_2B + NT
        QT_SC = lambda qt: 256 * qt
        QT_AV = lambda qt: 256 * qt
        QT_XP = lambda qt: 256 * qt
        QT_EV = lambda qt: 16 * qt
        P3_SV = lambda qt: SV_2D + 57 * qt
        P3_SM = lambda qt: SM_2D + 25 * qt
        P3_SA = lambda qt: SA_3P + 12 * qt
        SV_TOT = SV_2D + 57 * NQT
        SM_TOT = SM_2D + 25 * NQT
        SA_TOT = SA_3P + 12 * NQT

        @block.gpsimd
        def _(g):
            n = 0
            for dst, src in [(ident_sb[:, :], ident_d), (e4_sb[:, :], e4_d),
                             (g8_sb[:, :], g8_d), (sel_sb[:, :], sel_d),
                             (ep_sb[0:32, :], ep_d)]:
                g.dma_start(out=dst, in_=src).then_inc(ld, 16)
                n += 1
            g.memset(v_sb[:, :], 1.0)
            assert n == 5

        @block.sync
        def _(s):
            for dst, src in [(blin_sb[:, :], blin_d), (ones1_sb[:, :], ones1_d),
                             (eps_sb[:, :], eps_d)]:
                s.dma_start(out=dst, in_=src).then_inc(lx, 16)
            for t in range(NT):
                if t >= 2:
                    s.wait_ge(sv, 8 * (t - 2) + 4)
                s.dma_start(out=xst[:, (t % 2) * D:(t % 2 + 1) * D],
                            in_=x_d[t * 128:(t + 1) * 128, :]).then_inc(lx, 16)

        ctx2 = ctx.enter_context(contextlib.ExitStack())
        sb2 = lambda name, shape, dtype: ctx2.enter_context(nc.sbuf_tensor(name, shape, dtype))
        xn_sb = sb2("xn_sb", [128, 8 * T], BF16)
        wv_sb = sb2("wv_sb", [128, 8 * D], BF16)
        wlin_sb = sb2("wlin_sb", [128, 8 * D], BF16)
        gwT_sb = sb2("gwT_sb", [128, 64], BF16)
        leftf_sb = sb2("leftf_sb", [128, 256], BF16)
        rightp_sb = sb2("rightp_sb", [128, D], BF16)
        wq_st = sb2("wq_st", [128, 2 * 1024], BF16)
        sqs = sb2("sqs", [128, 512], BF16)
        gates_bf = sb2("gates_bf", [128, NT * 8], BF16)
        gpad = sb2("gpad", [128, 128], BF16)
        gfm_sb = sb2("gfm_sb", [128, NT * 128], BF16)
        t_tmp = sb2("t_tmp", [32, 512], BF16)
        t1pad = sb2("t1pad", [128, T], BF16)
        scr = sb2("scr", [128, D], BF16)

        @block.gpsimd
        def _(g):
            for c in range(8):
                g.dma_start(out=wv_sb[:, c * D:(c + 1) * D],
                            in_=wqkv_d[c * 128:(c + 1) * 128, 2 * D:3 * D]).then_inc(ld, 16)
                g.dma_start(out=wlin_sb[:, c * D:(c + 1) * D],
                            in_=wlinT_d[c * 128:(c + 1) * 128, :]).then_inc(ld, 16)
                g.dma_start(out=gwT_sb[:, c * 8:(c + 1) * 8],
                            in_=gwT_d[c * 128:(c + 1) * 128, :]).then_inc(ld, 16)
                g.dma_start(out=leftf_sb[:, c * 32:(c + 1) * 32],
                            in_=leftf_d[c * 128:(c + 1) * 128, :]).then_inc(ld, 16)
            g.dma_start(out=rightp_sb[:, :], in_=rightp_d).then_inc(ld, 16)
            g.memset(t1pad[:, :], 0.0)
            g.memset(gpad[:, :], 0.0)
            for dc in range(16 if ph >= 2 else 0):
                if dc >= 2:
                    g.wait_ge(sm, P2A_SM(dc - 1) + 4)
                g.dma_start(
                    out=wq_st[:, (dc % 2) * 1024:(dc % 2 + 1) * 1024].rearrange(
                        "p (c m) -> p c m", c=8),
                    in_=wqkv_d[:, dc * 128:(dc + 1) * 128].rearrange(
                        "(c p) m -> p c m", p=128)).then_inc(ld, 16)

        @block.vector
        def _(v):
            for t in range(NT):
                b = 8 * t
                xt = xst[:, (t % 2) * D:(t % 2 + 1) * D]
                v.wait_ge(lx, LX_X1(t))
                if t >= 2:
                    v.wait_ge(sm, 8 * (t - 2) + 8)
                v.tensor_scalar(xn_tm[:, (t % 2) * D:(t % 2 + 1) * D], xt, 1.0 / D,
                                0.0, op0=ALU.mult, op1=ALU.add,
                                accum_out=st1[:, 1:2]).then_inc(sv, 1)
                v.wait_ge(sa, 3 * t + 1)
                v.wait_ge(sv, b + 1)
                v.scalar_tensor_tensor(st1[:, 3:4], st1[:, 1:2], -1.0, st1[:, 1:2],
                                       op0=ALU.mult, op1=ALU.mult).then_inc(sv, 1)
                v.wait_ge(sv, b + 2)
                if t >= 1:
                    v.wait_ge(sa, 3 * (t - 1) + 2)
                v.tensor_scalar(st1[:, 5:6], st1[:, 2:3], 1.0 / D, st1[:, 3:4],
                                op0=ALU.mult, op1=ALU.add).then_inc(sv, 1)
                v.wait_ge(sa, 3 * t + 3)
                v.tensor_scalar(xn_tm[:, (t % 2) * D:(t % 2 + 1) * D], xt,
                                st1[:, 1:2], st1[:, 6:7],
                                op0=ALU.subtract, op1=ALU.mult).then_inc(sv, 1)
                xnv = xn_sb[:, :].rearrange("p (c tok) -> p c tok", c=8)
                for j in range(4):
                    v.wait_ge(sm, 8 * t + 2 * (j + 1))
                    v.tensor_copy(xnv[:, 2 * j:2 * j + 2, t * 128:(t + 1) * 128],
                                  ps_tr[:, 0:256].rearrange("p (c tok) -> p c tok", c=2)
                                  ).then_inc(sv, 1)

        @block.scalar
        def _(sc):
            for t in range(NT):
                xt = xst[:, (t % 2) * D:(t % 2 + 1) * D]
                sc.wait_ge(lx, LX_X1(t))
                if t >= 1:
                    sc.wait_ge(sv, 8 * (t - 1) + 3)
                sc.activation(scr[:, :], xt, AF.Square,
                              accum_out=st1[:, 2:3]).then_inc(sa, 1)
                sc.wait_ge(sv, 8 * t + 3)
                sc.activation(st1[:, 7:8], st1[:, 5:6], AF.Ln,
                              bias=eps_sb[:, 0:1], scale=1.0).then_inc(sa, 1)
                sc.wait_ge(sa, 3 * t + 2)
                if t >= 1:
                    sc.wait_ge(sv, 8 * (t - 1) + 4)
                sc.activation(st1[:, 6:7], st1[:, 7:8], AF.Exp,
                              bias=0.0, scale=-0.5).then_inc(sa, 1)

        @block.tensor
        def _(t_):
            t_.wait_ge(ld, 16 * 5)
            for t in range(NT):
                for j in range(4):
                    t_.wait_ge(sv, 8 * t + 4)
                    prev = 4 * t + j - 1
                    if prev >= 0:
                        pt, pj = prev // 4, prev % 4
                        t_.wait_ge(sv, 8 * pt + 4 + pj + 1)
                    for h in range(2):
                        c = 2 * j + h
                        t_.transpose(ps_tr[:, h * 128:(h + 1) * 128],
                                     xn_tm[:, (t % 2) * D + c * 128:(t % 2) * D + (c + 1) * 128],
                                     ident_sb[:, :]).then_inc(sm, 1)

        @block.tensor
        def _(t_):
            for dc in range(16 if ph >= 2 else 0):
                t_.wait_ge(ld, LDW(dc))
                for qt in range(NQT):
                    idx = 4 * dc + qt
                    if idx >= 4:
                        pdc, pqt = (idx - 4) // 4, (idx - 4) % 4
                        t_.wait_ge(sv, P2A_SVD(pdc) + pqt + 1)
                    ins = None
                    for c in range(8):
                        ins = t_.matmul(
                            ps_big[:, qt * 512:(qt + 1) * 512],
                            wq_st[:, (dc % 2) * 1024 + c * 128:(dc % 2) * 1024 + (c + 1) * 128],
                            xn_sb[:, c * T + qt * 512:c * T + (qt + 1) * 512],
                            start=(c == 0), stop=(c == 7), skip_group_check=True)
                    ins.then_inc(sm, 1)
                for qt in range(NQT):
                    t_.wait_ge(sv, P2A_SVD(dc) + 5 + 2 * qt)
                    t_.matmul(ps_sm[0:4, 0:512], e4_sb[:, :], sqs[:, :],
                              start=True, stop=True,
                              skip_group_check=True).then_inc(sm, 1)

        @block.vector
        def _(v):
            for dc in range(16 if ph >= 2 else 0):
                dst = q_sb if dc < 8 else k_sb
                cc = dc % 8
                base = P2A_SVD(dc)
                for qt in range(NQT):
                    v.wait_ge(sm, P2A_SM(dc) + qt + 1)
                    v.tensor_copy(dst[:, cc * T + qt * 512:cc * T + (qt + 1) * 512],
                                  ps_big[:, qt * 512:(qt + 1) * 512]).then_inc(sv, 1)
                for qt in range(NQT):
                    v.wait_ge(sv, base + 4)
                    v.wait_ge(sm, P2A_SM(dc) + 4 + qt)
                    src = dst[:, cc * T + qt * 512:cc * T + (qt + 1) * 512]
                    v.tensor_tensor(sqs[:, :], src, src, op=ALU.mult).then_inc(sv, 1)
                    v.wait_ge(sm, P2A_SM(dc) + 4 + qt + 1)
                    v.reduce_max(bnd[0:4, dc * 4 + qt:dc * 4 + qt + 1],
                                 ps_sm[0:4, 0:512], axis=AX.X).then_inc(sv, 1)

        @block.gpsimd
        def _(g):
            if ph < 2:
                return
            g.wait_ge(sv, SV_2A)
            g.dma_start(out=bnd_d, in_=bnd[0:4, 0:64]).then_inc(ld, 16)
            g.wait_ge(ld, LD_BND1)
            g.dma_start(out=mqk[:, 0:128].rearrange("a (p c) -> a p c", p=4),
                        in_=bnd_d[:, 0:32]).then_inc(ld, 16)
            g.dma_start(out=mqk[:, 128:256].rearrange("a (p c) -> a p c", p=4),
                        in_=bnd_d[:, 32:64]).then_inc(ld, 16)

        @block.vector
        def _(v):
            if ph < 2:
                return
            v.wait_ge(ld, LD_BND)
            v.reduce_max(st1[0:1, 8:9], mqk[:, 0:128], axis=AX.X)
            v.reduce_max(st1[0:1, 9:10], mqk[:, 128:256], axis=AX.X).then_inc(sv, 1)
            v.wait_ge(sv, SV_2A + 1)
            v.tensor_tensor(st1[0:1, 10:11], st1[0:1, 8:9], st1[0:1, 9:10],
                            op=ALU.mult).then_inc(sv, 1)
            v.wait_ge(sa, SA_BF)
            v.tensor_scalar(st1[0:1, 12:13], st1[0:1, 11:12], -SCALE * 1.02, None,
                            op0=ALU.mult).then_inc(sv, 1)
            v.wait_ge(sv, SV_2A + 3)
            v.tensor_scalar(mqk[:, 0:128], ones1_sb[0:1, :], st1[0:1, 12:13], None,
                            op0=ALU.mult).then_inc(sv, 1)

        @block.scalar
        def _(sc):
            if ph < 2:
                return
            sc.wait_ge(sv, SV_2A + 2)
            sc.activation(st1[0:1, 13:14], st1[0:1, 10:11], AF.Ln,
                          bias=0.0, scale=1.0).then_inc(sa, 1)
            sc.wait_ge(sa, P1_SA + 1)
            sc.activation(st1[0:1, 11:12], st1[0:1, 13:14], AF.Exp,
                          bias=0.0, scale=0.5).then_inc(sa, 1)

        @block.gpsimd
        def _(g):
            if ph < 2:
                return
            g.wait_ge(sv, SV_BF)
            g.dma_start(out=nb_d[:, :].rearrange("p a -> (p a)").rearrange(
                "(a q) -> a q", a=1), in_=mqk[:, 0:128]).then_inc(ld, 16)
            g.wait_ge(ld, LD_NB1)
            g.dma_start(out=biasB_sb[:, :], in_=nb_d).then_inc(ld, 16)

        @block.tensor
        def _(t_):
            for t in range(NT if ph >= 3 else 0):
                if t == 0:
                    t_.wait_ge(sv, SV_2A)
                if t >= 2:
                    t_.wait_ge(sv, P2B_SV(t - 2) + 1)
                if t >= 1:
                    t_.wait_ge(sv, P2B_SV(t - 1) + 7)
                    t_.wait_ge(sa, P2B_SA(t - 1) + 1)
                vb = (t % 2) * 1024
                ins = None
                for c in range(8):
                    xnc = xn_sb[:, c * T + t * 128:c * T + (t + 1) * 128]
                    t_.matmul(ps_big[:, vb:vb + 512], xnc, wv_sb[:, c * D:c * D + 512],
                              start=(c == 0), stop=(c == 7), skip_group_check=True)
                    t_.matmul(ps_big[:, vb + 512:vb + 1024], xnc, wv_sb[:, c * D + 512:(c + 1) * D],
                              start=(c == 0), stop=(c == 7), skip_group_check=True)
                    ins = t_.matmul(ps_sm[:, 0:8], xnc, gwT_sb[:, c * 8:(c + 1) * 8],
                                    start=(c == 0), stop=(c == 7), skip_group_check=True)
                ins.then_inc(sm, 1)
                t_.wait_ge(sv, P2B_SV(t) + 11)
                if t >= 1:
                    t_.wait_ge(sv, P2B_SV(t - 1) + 12)
                t_.transpose(ps_tr[:, 128:256], gpad[:, :], ident_sb[:, :]).then_inc(sm, 1)

        @block.scalar
        def _(sc):
            for t in range(NT if ph >= 3 else 0):
                sc.wait_ge(sv, P2B_SV(t) + 2)
                sc.activation(scr2[:, 0:8], ps_sm[:, 0:8], AF.Exp, bias=st1[:, 8:9],
                              accum_out=st1[:, 9:10]).then_inc(sa, 1)

        @block.vector
        def _(v):
            for t in range(NT if ph >= 3 else 0):
                b = P2B_SV(t)
                v.wait_ge(sm, P2B_SM(t) + 1)
                vb = (t % 2) * 1024
                v.tensor_copy(
                    v_sb[:, t * NH * 33:(t + 1) * NH * 33].rearrange(
                        "p (h w) -> p h w", h=NH)[:, :, 0:32],
                    ps_big[:, vb:vb + 1024].rearrange("p (h w) -> p h w", h=NH)).then_inc(sv, 1)
                v.reduce_max(st1[:, 8:9], ps_sm[:, 0:8], axis=AX.X,
                             negate=True).then_inc(sv, 1)
                v.wait_ge(sv, b + 2)
                v.tensor_scalar(st1[:, 10:11], st1[:, 8:9], -1.0, None,
                                op0=ALU.mult).then_inc(sv, 1)
                v.wait_ge(sv, b + 3)
                v.tensor_scalar(scr2[:, 8:16], ps_sm[:, 0:8], st1[:, 10:11], None,
                                op0=ALU.is_ge).then_inc(sv, 1)
                v.wait_ge(sv, b + 4)
                v.scalar_tensor_tensor(scr2[:, 16:24], scr2[:, 8:16], -BIG,
                                       ps_sm[:, 0:8], op0=ALU.mult,
                                       op1=ALU.add).then_inc(sv, 1)
                v.wait_ge(sv, b + 5)
                v.reduce_max(st1[:, 11:12], scr2[:, 16:24], axis=AX.X).then_inc(sv, 1)
                v.wait_ge(sv, b + 6)
                v.tensor_scalar(scr2[:, 24:32], ps_sm[:, 0:8], st1[:, 11:12], None,
                                op0=ALU.is_ge).then_inc(sv, 1)
                v.wait_ge(sa, P2B_SA(t) + 1)
                with nc.allow_low_precision(reason="tiny gate sums"):
                    v.reciprocal(st1[:, 12:13], st1[:, 9:10]).then_inc(sv, 1)
                v.wait_ge(sv, b + 8)
                v.scalar_tensor_tensor(gates_bf[:, t * 8:(t + 1) * 8], scr2[:, 0:8],
                                       1.0, scr2[:, 24:32], op0=ALU.mult,
                                       op1=ALU.mult).then_inc(sv, 1)
                v.wait_ge(sv, b + 9)
                v.tensor_scalar(gates_bf[:, t * 8:(t + 1) * 8],
                                gates_bf[:, t * 8:(t + 1) * 8], st1[:, 12:13], None,
                                op0=ALU.mult).then_inc(sv, 1)
                v.wait_ge(sv, b + 10)
                if t >= 1:
                    v.wait_ge(sm, P2B_SM(t - 1) + 2)
                v.tensor_copy(gpad[:, 0:8],
                              gates_bf[:, t * 8:(t + 1) * 8]).then_inc(sv, 1)
                v.wait_ge(sm, P2B_SM(t) + 2)
                v.tensor_copy(gfm_sb[:, t * 128:(t + 1) * 128],
                              ps_tr[:, 128:256]).then_inc(sv, 1)

        @block.tensor
        def _(t_):
            for qt in range(NQT if ph >= 4 else 0):
                if qt == 0:
                    t_.wait_ge(sv, SV_2B)
                    t_.wait_ge(sa, SA_2B)
                else:
                    t_.wait_ge(sv, P2C_SV(qt - 1) + 1)
                ins = None
                for c in range(8):
                    ins = t_.matmul(ps_sm[0:32, :], leftf_sb[:, c * 32:(c + 1) * 32],
                                    xn_sb[:, c * T + qt * 512:c * T + (qt + 1) * 512],
                                    start=(c == 0), stop=(c == 7), skip_group_check=True)
                ins.then_inc(sm, 1)
                if qt >= 1:
                    t_.wait_ge(sv, P2C_SV(qt - 1) + 1)
                t_.matmul(ps_two[:, 0:512], g8_sb[:, :],
                          gfm_sb[:, qt * 512:(qt + 1) * 512],
                          start=True, stop=True, skip_group_check=True).then_inc(sm, 1)

        @block.vector
        def _(v):
            for qt in range(NQT if ph >= 4 else 0):
                b = P2C_SV(qt)
                v.wait_ge(sm, P2C_SM(qt) + 1)
                v.tensor_copy(t_tmp[0:32, :], ps_sm[0:32, :]).then_inc(sv, 1)
                v.wait_ge(sv, b + 1)
                v.wait_ge(sm, P2C_SM(qt) + 2)
                v.tensor_tensor(t1pad[0:32, qt * 512:(qt + 1) * 512],
                                ps_two[0:32, 0:512], t_tmp[0:32, :],
                                op=ALU.mult).then_inc(sv, 1)

        @block.tensor
        def _(t_):
            for t in range(NT if ph >= 4 else 0):
                if t <= 1:
                    t_.wait_ge(sv, SV_2B)
                t_.wait_ge(sv, P2C_SV(t // 4) + 2)
                if t >= 2:
                    t_.wait_ge(sv, P2D_SV(t - 2) + 2)
                for half in range(2):
                    hb = (t % 2) * 1024
                    ins = None
                    for c in range(8):
                        ins = t_.matmul(
                            ps_big[:, hb + half * 512:hb + (half + 1) * 512],
                            xn_sb[:, c * T + t * 128:c * T + (t + 1) * 128],
                            wlin_sb[:, c * D + half * 512:c * D + (half + 1) * 512],
                            start=(c == 0), stop=False, skip_group_check=True)
                    ins = t_.matmul(
                        ps_big[:, hb + half * 512:hb + (half + 1) * 512],
                        t1pad[:, t * 128:(t + 1) * 128],
                        rightp_sb[:, half * 512:(half + 1) * 512],
                        start=False, stop=True, skip_group_check=True)
                    ins.then_inc(sm, 1)

        @block.vector
        def _(v):
            for t in range(NT if ph >= 4 else 0):
                b = P2D_SV(t)
                hb = (t % 2) * 1024
                for half in range(2):
                    v.wait_ge(sm, P2D_SM(t) + half + 1)
                    if t >= 2:
                        v.wait_ge(lx, LX_PST(t - 2))
                    v.tensor_tensor(prest[:, (t % 2) * D + half * 512:(t % 2) * D + (half + 1) * 512],
                                    ps_big[:, hb + half * 512:hb + (half + 1) * 512],
                                    blin_sb[:, half * 512:(half + 1) * 512],
                                    op=ALU.add).then_inc(sv, 1)

        @block.sync
        def _(s):
            for t in range(NT if ph >= 4 else 0):
                s.wait_ge(sv, P2D_SV(t) + 2)
                s.dma_start(out=pre_d[t * 128:(t + 1) * 128, :],
                            in_=prest[:, (t % 2) * D:(t % 2 + 1) * D]).then_inc(lx, 16)

        ctx2.close()
        ctx3 = ctx.enter_context(contextlib.ExitStack())
        sb3 = lambda name, shape, dtype: ctx3.enter_context(nc.sbuf_tensor(name, shape, dtype))
        wout_sb = sb3("wout_sb", [128, NPAIR * D], BF16)
        attn = sb3("attn", [128, NPAIR * 512], BF16)
        expS = sb3("expS", [128, 4 * 1024], BF16)
        rpk = sb3("rpk", [32, 512], BF16)
        gst = sb3("gst", [128, 2 * D], BF16)
        spre = sb3("spre", [128, 4 * D], F32)
        bout_sb = sb3("bout_sb", [128, D], F32)
        gam_sb = sb3("gam_sb", [128, D], F32)
        bet_sb = blin_sb

        @block.gpsimd
        def _(g):
            if ph >= 5:
                g.wait_ge(sm, SM_2D)
                g.wait_ge(sv, SV_2D)
            for p in range(NPAIR if ph >= 5 else 0):
                g.dma_start(out=wout_sb[:, p * D:(p + 1) * D],
                            in_=wout_d[p * 128:(p + 1) * 128, :]).then_inc(ld, 16)

        @block.sync
        def _(s):
            for dst, src in [(bout_sb[:, :], bout_d), (gam_sb[:, :], gam_d),
                             (bet_sb[:, :], bet_d)]:
                s.dma_start(out=dst, in_=src).then_inc(lx, 16) if False else None
            if ph >= 5.5:
                order = [("r", 0), ("r", 1), ("r", 2)]
                for k in range(NT - 3):
                    order += [("s", k), ("r", k + 3)]
                order += [("s", NT - 3), ("s", NT - 2), ("s", NT - 1)]
                for kind, t in order:
                    if kind == "r":
                        s.wait_ge(lx, LX_PST(NT - 1))
                        if t >= 2:
                            s.wait_ge(sa, SA_SIL(t - 2))
                        s.dma_start(out=prest[:, (t % 2) * D:(t % 2 + 1) * D],
                                    in_=pre_d[t * 128:(t + 1) * 128, :]).then_inc(lx, 16)
                    else:
                        s.wait_ge(sa, SA_SIL(t))
                        s.dma_start(out=g_d[t * 128:(t + 1) * 128, :],
                                    in_=gst[:, (t % 2) * D:(t % 2 + 1) * D]).then_inc(lx, 16)
            elif ph == 5:
                for t in range(NT):
                    s.wait_ge(lx, LX_PST(NT - 1))
                    s.dma_start(out=prest[:, (t % 2) * D:(t % 2 + 1) * D],
                                in_=pre_d[t * 128:(t + 1) * 128, :]).then_inc(lx, 16)

        @block.gpsimd
        def _(g):
            for dst, src in ([(bout_sb[:, :], bout_d), (gam_sb[:, :], gam_d),
                             (bet_sb[:, :], bet_d)] if ph >= 5 else []):
                g.dma_start(out=dst, in_=src).then_inc(ld, 16)
        LD_TOT2 = LD_TOT + 16 * 3

        @block.scalar
        def _(sc):
            for t in range(NT if ph >= 5.5 else 0):
                sc.wait_ge(lx, LX_PLD(t))
                if t >= 2:
                    sc.wait_ge(lx, LX_GST(t - 2))
                sc.activation(gst[:, (t % 2) * D:(t % 2 + 1) * D],
                              prest[:, (t % 2) * D:(t % 2 + 1) * D],
                              AF.Silu).then_inc(sa, 1)

        @block.tensor
        def _(t_):
            if ph >= 6:
                t_.wait_ge(ld, LD_TOT2)
            for qt in range(NQT if ph >= 6 else 0):
                scb, avb, xpb, evb = QT_SC(qt), QT_AV(qt), QT_XP(qt), QT_EV(qt)
                if qt > 0:
                    t_.wait_ge(sv, P3_SV(qt - 1) + 17)
                NI = NPAIR * KT

                def emit_scores(i):
                    p, kt = i // KT, i % KT
                    cp, jp = p // 2, p % 2
                    sA, sB = 64 * jp, 64 * jp + 32
                    base = 256 * qt + 16 * p
                    par = kt % 2
                    if qt > 0 and p == 0 and kt == 1:
                        t_.wait_ge(sv, P3_SV(qt - 1) + 25)
                    t_.wait_ge(s_xp, max(0, base + kt - 1))
                    t_.matmul(ps_big[:, par * 1024:par * 1024 + 512],
                              k_sb[sA:sA + 32, cp * T + kt * 128:cp * T + (kt + 1) * 128],
                              q_sb[sA:sA + 32, cp * T + qt * 512:cp * T + (qt + 1) * 512],
                              start=True, stop=True, tile_position=(sA, 0),
                              skip_group_check=True)
                    t_.matmul(ps_big[:, par * 1024 + 512:par * 1024 + 1024],
                              k_sb[sB:sB + 32, cp * T + kt * 128:cp * T + (kt + 1) * 128],
                              q_sb[sB:sB + 32, cp * T + qt * 512:cp * T + (qt + 1) * 512],
                              start=True, stop=True, tile_position=(sB, 0),
                              skip_group_check=True).then_inc(s_sc, 1)

                def emit_av(j):
                    p, kt = j // KT, j % KT
                    cp, jp = p // 2, p % 2
                    hA, hB = 4 * cp + 2 * jp, 4 * cp + 2 * jp + 1
                    base = 256 * qt + 16 * p
                    slot = (kt % 4) * 1024
                    ub = (p % 2) * 512
                    t_.wait_ge(s_xp, base + kt + 1)
                    if p >= 2 and kt == 0:
                        t_.wait_ge(s_ev, evb + p - 1)
                    if qt > 0 and p < 2 and kt == 0:
                        t_.wait_ge(s_ev, evb)
                    t_.matmul(ps_two[0:33, ub:ub + 512],
                              v_sb[:, kt * NH * 33 + hA * 33:kt * NH * 33 + hA * 33 + 33],
                              expS[:, slot:slot + 512],
                              start=(kt == 0), stop=(kt == KT - 1),
                              tile_position=(0, 0), skip_group_check=True)
                    t_.matmul(ps_two[64:97, ub:ub + 512],
                              v_sb[:, kt * NH * 33 + hB * 33:kt * NH * 33 + hB * 33 + 33],
                              expS[:, slot + 512:slot + 1024],
                              start=(kt == 0), stop=(kt == KT - 1),
                              tile_position=(0, 64),
                              skip_group_check=True).then_inc(s_av, 1)

                for g in range(NI // 2 + 1):
                    if 2 * g < NI:
                        emit_scores(2 * g)
                        emit_scores(2 * g + 1)
                    if g >= 1:
                        emit_av(2 * g - 2)
                        emit_av(2 * g - 1)
                for p in range(NPAIR):
                    t_.wait_ge(s_ev, evb + p + 1)
                    ins = t_.matmul(ps_sm[0:32, :], sel_sb[:, p * 32:(p + 1) * 32],
                                    attn[:, p * 512:(p + 1) * 512],
                                    start=(p == 0), stop=(p == NPAIR - 1),
                                    skip_group_check=True)
                ins.then_inc(sm, 1)
                for p in range(NPAIR):
                    t_.wait_ge(sv, max(P3_SV(qt) + 1, P3_SV(qt) + p))
                    t_.matmul(ps_big[:, (p % 2) * 512:(p % 2) * 512 + 512],
                              ep_sb[0:32, p * 128:(p + 1) * 128],
                              rpk[0:32, :], start=True, stop=True,
                              skip_group_check=True).then_inc(sm, 1)
                for ts in range(4):
                    for half in range(2):
                        if ts >= 1 or half >= 1:
                            pidx = 2 * ts + half - 2
                            if pidx >= 0:
                                t_.wait_ge(sv, P3_SV(qt) + 17 + pidx + 1)
                        ins = None
                        for p in range(NPAIR):
                            t_.wait_ge(sv, P3_SV(qt) + p + 2)
                            ins = t_.matmul(
                                ps_big[:, 1024 + half * 512:1024 + (half + 1) * 512],
                                attn[:, p * 512 + ts * 128:p * 512 + (ts + 1) * 128],
                                wout_sb[:, p * D + half * 512:p * D + (half + 1) * 512],
                                start=(p == 0), stop=(p == NPAIR - 1),
                                skip_group_check=True)
                        ins.then_inc(sm, 1)

        @block.scalar
        def _(sc):
            if ph >= 6:
                sc.wait_ge(ld, LD_NB)

            def sq_act(sc, qt, ts):
                o8 = ts * 8
                sc.wait_ge(sv, P3_SV(qt) + 17 + 2 * ts + 2)
                sc.activation(expS[:, 0:1024], spre[:, ts * D:(ts + 1) * D],
                              AF.Square, accum_out=st1[:, o8 + 2:o8 + 3]).then_inc(sa, 1)

            def lnr_acts(sc, qt, ts):
                o8 = ts * 8
                CB = P3_SV(qt) + 25 + 8 * ts
                sc.wait_ge(sv, CB + 3)
                sc.activation(st1[:, o8 + 7:o8 + 8], st1[:, o8 + 5:o8 + 6], AF.Ln,
                              bias=eps_sb[:, 1:2], scale=1.0).then_inc(sa, 1)
                sc.wait_ge(sa, P3_SA(qt) + 4 + 2 * ts + 1)
                sc.activation(st1[:, o8 + 6:o8 + 7], st1[:, o8 + 7:o8 + 8], AF.Exp,
                              bias=0.0, scale=-0.5).then_inc(sa, 1)

            for qt in range(NQT if ph >= 6 else 0):
                for p in range(NPAIR):
                    base = 256 * qt + 16 * p
                    for kt in range(KT):
                        par = kt % 2
                        slot = (kt % 4) * 1024
                        sc.wait_ge(s_sc, base + kt + 1)
                        sc.wait_ge(s_av, max(0, base + kt - 3))
                        sc.activation(expS[:, slot:slot + 1024],
                                      ps_big[:, par * 1024:par * 1024 + 1024],
                                      AF.Exp, bias=biasB_sb[:, :],
                                      scale=SCALE).then_inc(s_xp, 1)
                        if qt > 0 and p % 4 == 0 and p > 0 and kt == 4:
                            lnr_acts(sc, qt - 1, p // 4 - 1)
                if qt > 0:
                    lnr_acts(sc, qt - 1, 3)
                for ts in range(4):
                    sq_act(sc, qt, ts)
            if ph >= 6:
                for ts in range(4):
                    lnr_acts(sc, NQT - 1, ts)

        @block.vector
        def _(v):
            def chain(v, qt, ts):
                CB = P3_SV(qt) + 25 + 8 * ts
                sl = ts % 2
                o8 = ts * 8
                spt = spre[:, ts * D:(ts + 1) * D]
                v.wait_ge(sa, P3_SA(qt) + ts + 1)
                v.tensor_scalar(scr2[:, :], spt, 1.0 / D, 0.0, op0=ALU.mult,
                                op1=ALU.add, accum_out=st1[:, o8 + 1:o8 + 2]).then_inc(sv, 1)
                v.wait_ge(sv, CB + 1)
                v.scalar_tensor_tensor(st1[:, o8 + 3:o8 + 4], st1[:, o8 + 1:o8 + 2],
                                       -1.0, st1[:, o8 + 1:o8 + 2],
                                       op0=ALU.mult, op1=ALU.mult).then_inc(sv, 1)
                v.wait_ge(sv, CB + 2)
                v.tensor_scalar(st1[:, o8 + 5:o8 + 6], st1[:, o8 + 2:o8 + 3], 1.0 / D,
                                st1[:, o8 + 3:o8 + 4],
                                op0=ALU.mult, op1=ALU.add).then_inc(sv, 1)
                v.wait_ge(sa, P3_SA(qt) + 4 + 2 * ts + 2)
                v.tensor_scalar(scr2[:, :], spt, st1[:, o8 + 1:o8 + 2],
                                st1[:, o8 + 6:o8 + 7],
                                op0=ALU.subtract, op1=ALU.mult).then_inc(sv, 1)
                v.wait_ge(sv, CB + 4)
                v.tensor_tensor(scr2[:, :], scr2[:, :], gam_sb[:, :],
                                op=ALU.mult).then_inc(sv, 1)
                v.wait_ge(sv, CB + 5)
                v.tensor_tensor(scr2[:, :], scr2[:, :], bet_sb[:, :],
                                op=ALU.add).then_inc(sv, 1)
                v.wait_ge(sv, CB + 6)
                v.wait_ge(lx, LX_EGG(qt, ts))
                v.tensor_tensor(scr2[:, :], scr2[:, :], gst[:, sl * D:(sl + 1) * D],
                                op=ALU.mult).then_inc(sv, 1)
                v.wait_ge(sv, CB + 7)
                v.wait_ge(lx, LX_EXG(qt, ts))
                v.tensor_tensor(spre[:, ts * D:(ts + 1) * D], scr2[:, :],
                                xst[:, sl * D:(sl + 1) * D],
                                op=ALU.add).then_inc(sv, 1)

            for qt in range(NQT if ph >= 6 else 0):
                bq = P3_SV(qt)
                evb = QT_EV(qt)
                for p in range(NPAIR):
                    v.wait_ge(s_av, 256 * qt + 16 * p + 16)
                    v.tensor_copy(attn[:, p * 512:(p + 1) * 512],
                                  ps_two[:, (p % 2) * 512:(p % 2) * 512 + 512]).then_inc(s_ev, 1)
                    if qt > 0 and p % 4 == 3:
                        chain(v, qt - 1, p // 4)
                v.wait_ge(sm, P3_SM(qt) + 1)
                with nc.allow_low_precision(reason="1/denom bf16: uniform per-(h,q) scale"):
                    v.reciprocal(rpk[0:32, :], ps_sm[0:32, :]).then_inc(sv, 1)
                for p in range(NPAIR):
                    sl2 = (p % 2) * 512
                    v.wait_ge(sm, P3_SM(qt) + 2 + p)
                    v.tensor_tensor(attn[:, p * 512:(p + 1) * 512],
                                    attn[:, p * 512:(p + 1) * 512],
                                    ps_big[:, sl2:sl2 + 512],
                                    op=ALU.mult).then_inc(sv, 1)
                for ts in range(4):
                    v.wait_ge(sm, P3_SM(qt) + 17 + 2 * ts + 1)
                    if qt > 0:
                        v.wait_ge(lx, LX_EOUT(qt - 1, ts))
                        v.wait_ge(sv, P3_SV(qt - 1) + 25 + 8 * ts + 8)
                        v.wait_ge(sa, P3_SA(qt - 1) + ts + 1)
                    for half in range(2):
                        if half == 1:
                            v.wait_ge(sm, P3_SM(qt) + 17 + 2 * ts + 2)
                        v.tensor_tensor(spre[:, ts * D + half * 512:ts * D + (half + 1) * 512],
                                        ps_big[:, 1024 + half * 512:1024 + (half + 1) * 512],
                                        bout_sb[:, half * 512:(half + 1) * 512],
                                        op=ALU.add).then_inc(sv, 1)
            if ph >= 6:
                for ts in range(4):
                    chain(v, NQT - 1, ts)

        @block.sync
        def _(s):
            for qt in range(NQT if ph >= 6 else 0):
                for ts in range(4):
                    tok0 = qt * 512 + ts * 128
                    sl = ts % 2
                    pts = 4 * qt + ts - 2
                    if pts >= 0:
                        s.wait_ge(sv, P3_SV(pts // 4) + 25 + 8 * (pts % 4) + 8)
                    s.dma_start(out=xst[:, sl * D:(sl + 1) * D],
                                in_=x_d[tok0:tok0 + 128, :]).then_inc(lx, 16)
                    s.dma_start(out=gst[:, sl * D:(sl + 1) * D],
                                in_=g_d[tok0:tok0 + 128, :]).then_inc(lx, 16)
                for ts in range(4):
                    s.wait_ge(sv, P3_SV(qt) + 25 + 8 * ts + 8)
                    s.dma_start(out=out_d[qt * 512 + ts * 128:qt * 512 + (ts + 1) * 128, :],
                                in_=spre[:, ts * D:(ts + 1) * D]).then_inc(lx, 16)
            if ph >= 6:
                s.wait_ge(lx, LX_TOT)
            else:
                tgt = {1: ("sv", P1_SV), 2: ("ld", LD_NB), 3: ("sv", SV_2B),
                       4: ("lx", LX2), 5: ("lx", LX3), 5.5: ("lx", LX4)}[ph]
                s.wait_ge({"sv": sv, "ld": ld, "lx": lx}[tgt[0]], tgt[1])
                s.dma_start(out=out_d[0:128, :], in_=xst[:, 0:D]).then_inc(lx, 16)
                s.wait_ge(lx, (LX1 if ph <= 3 else LX2 if ph == 4 else
                               LX3 if ph == 5 else LX4) + 16)
    return nc


def _host_consts():
    ident = np.eye(128, dtype=np.float32)
    e4 = np.zeros((128, 4), np.float32)
    for j in range(4):
        e4[32 * j:32 * (j + 1), j] = 1.0
    g8 = np.zeros((128, 128), np.float32)
    for e in range(8):
        for r in range(4):
            g8[e, 4 * e + r] = 1.0
    sel = np.zeros((128, NPAIR * 32), np.float32)
    for p in range(NPAIR):
        sel[32, p * 32 + 2 * p] = 1.0
        sel[96, p * 32 + 2 * p + 1] = 1.0
    ep = np.zeros((32, NPAIR * 128), np.float32)
    for p in range(NPAIR):
        ep[2 * p, p * 128 + 0:p * 128 + 32] = 1.0
        ep[2 * p + 1, p * 128 + 64:p * 128 + 96] = 1.0
    ones1 = np.ones((1, 128), np.float32)
    return ident, e4, g8, sel, ep, ones1


def kernel(L0_attn_x, L0_attn_w_qkv, L0_attn_w_out, L0_attn_b_out, L0_attn_ln_gamma,
           L0_attn_ln_beta, L0_ffn_W_lin, L0_ffn_b_lin, L0_ffn_gate_w, L0_ffn_left,
           L0_ffn_right):
    x = np.ascontiguousarray(np.asarray(L0_attn_x, dtype=np.float32))
    wqkv = np.ascontiguousarray(np.asarray(L0_attn_w_qkv, dtype=np.float32))
    wout = np.asarray(L0_attn_w_out, dtype=np.float32)
    bout = np.asarray(L0_attn_b_out, dtype=np.float32)
    gam = np.asarray(L0_attn_ln_gamma, dtype=np.float32)
    bet = np.asarray(L0_attn_ln_beta, dtype=np.float32)
    wlin = np.asarray(L0_ffn_W_lin, dtype=np.float32)
    blin = np.asarray(L0_ffn_b_lin, dtype=np.float32)
    gw = np.asarray(L0_ffn_gate_w, dtype=np.float32)
    left = np.asarray(L0_ffn_left, dtype=np.float32)
    right = np.asarray(L0_ffn_right, dtype=np.float32)

    wlinT = np.ascontiguousarray(wlin.T)
    gwT = np.ascontiguousarray(gw.T)
    leftf = np.ascontiguousarray(left.transpose(1, 0, 2).reshape(D, 32))
    rightp = np.zeros((128, D), np.float32)
    rightp[0:32] = right.reshape(32, D)
    wout_h = np.zeros((NPAIR * 128, D), np.float32)
    for p in range(NPAIR):
        wout_h[128 * p + 0:128 * p + 32] = wout[32 * (2 * p):32 * (2 * p) + 32]
        wout_h[128 * p + 64:128 * p + 96] = wout[32 * (2 * p + 1):32 * (2 * p + 1) + 32]
    rep = lambda a: np.ascontiguousarray(np.broadcast_to(a.reshape(1, D), (128, D)))
    ident, e4, g8, sel, ep, ones1 = _host_consts()

    nc = build_nc()
    eps_rep = np.ascontiguousarray(np.broadcast_to(
        np.array([EPS1, EPS2], np.float32).reshape(1, 2), (128, 2)))
    shared = dict(wqkv=wqkv, wlinT=wlinT, wout_h=wout_h, gwT=gwT, leftf=leftf,
                  rightp=rightp, blin_rep=rep(blin), bout_rep=rep(bout),
                  gam_rep=rep(gam), bet_rep=rep(bet), ident=ident, e4=e4,
                  g8pad=g8, sel=sel, epair=ep, ones1=ones1, eps_rep=eps_rep)
    in_maps = [dict(shared, x=np.ascontiguousarray(x[b])) for b in range(NCORES)]
    res = run_bass_kernel_spmd(nc, in_maps, core_ids=list(range(NCORES)))
    out = np.stack([res.results[b]["out"] for b in range(NCORES)], axis=0)
    return out.astype(np.float32)


# version 13
# speedup vs baseline: 1.7997x; 1.0003x over previous
import contextlib
import numpy as np
import ml_dtypes
import concourse.bass as bass
import concourse.mybir as mybir
from concourse.bass_utils import run_bass_kernel_spmd

F32 = mybir.dt.float32
BF16 = mybir.dt.bfloat16
AF = mybir.ActivationFunctionType
ALU = mybir.AluOpType
AX = mybir.AxisListType

NCORES = 8
T, D = 2048, 1024
NH, DH = 32, 32
NT = T // 128
NQT = T // 512
KT = T // 128
NPAIR = 16
SCALE = float(DH) ** -0.5
EPS1, EPS2 = 1e-5, 1e-6
BIG = 1e30


def build_nc(ph=6):
    nc = bass.Bass("TRN2", target_bir_lowering=False)
    dram = lambda name, shape, dtype=F32, kind="ExternalInput": nc.dram_tensor(
        name, shape, dtype, kind=kind).ap()
    x_d = dram("x", [T, D])
    wqkv_d = dram("wqkv", [D, 3 * D])
    wlinT_d = dram("wlinT", [D, D])
    wout_d = dram("wout_h", [NPAIR * 128, D])
    gwT_d = dram("gwT", [D, 8])
    leftf_d = dram("leftf", [D, 32])
    rightp_d = dram("rightp", [128, D])
    blin_d = dram("blin_rep", [128, D])
    bout_d = dram("bout_rep", [128, D])
    gam_d = dram("gam_rep", [128, D])
    bet_d = dram("bet_rep", [128, D])
    ident_d = dram("ident", [128, 128])
    e4_d = dram("e4", [128, 4])
    g8_d = dram("g8pad", [128, 128])
    sel_d = dram("sel", [128, NPAIR * 32])
    ep_d = dram("epair", [32, NPAIR * 128])
    ones1_d = dram("ones1", [1, 128])
    eps_d = dram("eps_rep", [128, 2])
    out_d = dram("out", [T, D], F32, kind="ExternalOutput")
    pre_d = dram("pre_scr", [T, D], BF16, kind="Internal")
    g_d = dram("g_scr", [T, D], BF16, kind="Internal")
    bnd_d = dram("bnd_scr", [4, 64], F32, kind="Internal")
    nb_d = dram("negB_scr", [128, 1], F32, kind="Internal")

    ctx = contextlib.ExitStack()
    with ctx:
        sb = lambda name, shape, dtype: ctx.enter_context(nc.sbuf_tensor(name, shape, dtype))
        psum = lambda name, shape, dtype=F32: ctx.enter_context(nc.psum_tensor(name, shape, dtype))

        q_sb = sb("q_sb", [128, 8 * T], BF16)
        k_sb = sb("k_sb", [128, 8 * T], BF16)
        v_sb = sb("v_sb", [128, KT * NH * 33], BF16)
        ident_sb = sb("ident_sb", [128, 128], BF16)
        sel_sb = sb("sel_sb", [128, NPAIR * 32], BF16)
        ep_sb = sb("ep_sb", [32, NPAIR * 128], BF16)
        e4_sb = sb("e4_sb", [128, 4], BF16)
        g8_sb = sb("g8_sb", [128, 128], BF16)
        st1 = sb("st1", [128, 32], F32)
        xst = sb("xst", [128, 2 * D], F32)
        scr2 = sb("scr2", [128, D], F32)
        xn_tm = sb("xn_tm", [128, 2 * D], BF16)
        prest = xn_tm
        blin_sb = sb("blin_sb", [128, D], F32)
        biasB_sb = sb("biasB_sb", [128, 1], F32)
        eps_sb = sb("eps_sb", [128, 2], F32)
        mqk = scr2[0:1, 256:512]
        ones1_sb = scr2[0:1, 512:640]
        bnd = scr2[0:4, 640:720]

        ps_big = psum("ps_big", [128, 2048])
        ps_two = psum("ps_two", [128, 1024])
        ps_sm = psum("ps_sm", [128, 512])
        ps_tr = psum("ps_tr", [128, 256], BF16)

        sem = lambda name: ctx.enter_context(nc.semaphore(name))
        ld = sem("ld")
        lx = sem("lx")
        sv = sem("sv")
        sa = sem("sa")
        sm = sem("sm")
        s_sc = sem("ssc")
        s_xp = sem("sxp")
        s_av = sem("sav")
        s_ev = sem("sev")

        block = ctx.enter_context(nc.Block())

        G0_DMAS = 8 * 4 + 6
        G0_LD = 16 * G0_DMAS
        LDW = lambda dc: G0_LD + 16 * (dc + 1)
        LD_2A = G0_LD + 16 * 16
        LD_BND1 = LD_2A + 16
        LD_BND = LD_2A + 16 * 3
        LD_NB1 = LD_BND + 16
        LD_NB = LD_BND + 32
        LD_WOUT = lambda i: LD_NB + 16 * (i + 1)
        LD_TOT = LD_NB + 16 * NPAIR

        LX_CONST = 16 * 3
        LX_X1 = lambda t: LX_CONST + 16 * (t + 1)
        LX1 = LX_CONST + 16 * NT
        LX_PST = lambda t: LX1 + 16 * (t + 1)
        LX2 = LX1 + 16 * NT
        _pr = {t: (t if t < 3 else 2 * t - 2) for t in range(NT)}
        _ps = {t: (3 + 2 * t if t <= NT - 4 else NT + t) for t in range(NT)}
        LX_PLD = lambda t: LX2 + 16 * (_pr[t] + 1)
        LX_GST = lambda t: LX2 + 16 * (_ps[t] + 1)
        LX3 = LX2 + 16 * NT
        LX4 = LX2 + 16 * 2 * NT
        LX_EPI = lambda qt: LX4 + 16 * 12 * qt
        LX_EXG = lambda qt, ts: LX_EPI(qt) + 16 * (2 * ts + 1)
        LX_EGG = lambda qt, ts: LX_EPI(qt) + 16 * (2 * ts + 2)
        LX_EOUT = lambda qt, ts: LX_EPI(qt) + 16 * 8 + 16 * (ts + 1)
        LX_TOT = LX4 + 16 * 12 * NQT

        P1_SV, P1_SA, P1_SM = 8 * NT, 3 * NT, 8 * NT
        SV_2A = P1_SV + 12 * 16
        P2A_SVD = lambda dc: P1_SV + 12 * dc
        P2A_SM = lambda dc: P1_SM + 8 * dc
        SM_2A = P1_SM + 8 * 16
        SA_BF = P1_SA + 2
        SV_BF = SV_2A + 4
        P2B_SV = lambda t: SV_BF + 12 * t
        P2B_SM = lambda t: SM_2A + 2 * t
        P2B_SA = lambda t: SA_BF + t
        SV_2B, SM_2B, SA_2B = SV_BF + 12 * NT, SM_2A + 2 * NT, SA_BF + NT
        P2C_SV = lambda qt: SV_2B + 2 * qt
        P2C_SM = lambda qt: SM_2B + 2 * qt
        SV_2C, SM_2C = SV_2B + 2 * NQT, SM_2B + 2 * NQT
        P2D_SV = lambda t: SV_2C + 2 * t
        P2D_SM = lambda t: SM_2C + 2 * t
        SV_2D, SM_2D = SV_2C + 2 * NT, SM_2C + 2 * NT
        SA_SIL = lambda t: SA_2B + t + 1
        SA_3P = SA_2B + NT
        QT_SC = lambda qt: 256 * qt
        QT_AV = lambda qt: 256 * qt
        QT_XP = lambda qt: 256 * qt
        QT_EV = lambda qt: 16 * qt
        P3_SV = lambda qt: SV_2D + 57 * qt
        P3_SM = lambda qt: SM_2D + 25 * qt
        P3_SA = lambda qt: SA_3P + 12 * qt
        SV_TOT = SV_2D + 57 * NQT
        SM_TOT = SM_2D + 25 * NQT
        SA_TOT = SA_3P + 12 * NQT

        @block.gpsimd
        def _(g):
            n = 0
            for dst, src in [(ident_sb[:, :], ident_d), (e4_sb[:, :], e4_d),
                             (g8_sb[:, :], g8_d), (sel_sb[:, :], sel_d),
                             (ep_sb[0:32, :], ep_d)]:
                g.dma_start(out=dst, in_=src).then_inc(ld, 16)
                n += 1
            g.memset(v_sb[:, :], 1.0)
            assert n == 5

        @block.sync
        def _(s):
            for dst, src in [(blin_sb[:, :], blin_d), (ones1_sb[:, :], ones1_d),
                             (eps_sb[:, :], eps_d)]:
                s.dma_start(out=dst, in_=src).then_inc(lx, 16)
            for t in range(NT):
                if t >= 2:
                    s.wait_ge(sv, 8 * (t - 2) + 4)
                s.dma_start(out=xst[:, (t % 2) * D:(t % 2 + 1) * D],
                            in_=x_d[t * 128:(t + 1) * 128, :]).then_inc(lx, 16)

        ctx2 = ctx.enter_context(contextlib.ExitStack())
        sb2 = lambda name, shape, dtype: ctx2.enter_context(nc.sbuf_tensor(name, shape, dtype))
        xn_sb = sb2("xn_sb", [128, 8 * T], BF16)
        wv_sb = sb2("wv_sb", [128, 8 * D], BF16)
        wlin_sb = sb2("wlin_sb", [128, 8 * D], BF16)
        gwT_sb = sb2("gwT_sb", [128, 64], BF16)
        leftf_sb = sb2("leftf_sb", [128, 256], BF16)
        rightp_sb = sb2("rightp_sb", [128, D], BF16)
        wq_st = sb2("wq_st", [128, 2 * 1024], BF16)
        sqs = sb2("sqs", [128, 512], BF16)
        gates_bf = sb2("gates_bf", [128, NT * 8], BF16)
        gpad = sb2("gpad", [128, 128], BF16)
        gfm_sb = sb2("gfm_sb", [128, NT * 128], BF16)
        t_tmp = sb2("t_tmp", [32, 512], BF16)
        t1pad = sb2("t1pad", [128, T], BF16)
        scr = sb2("scr", [128, D], BF16)

        @block.gpsimd
        def _(g):
            for c in range(8):
                g.dma_start(out=wv_sb[:, c * D:(c + 1) * D],
                            in_=wqkv_d[c * 128:(c + 1) * 128, 2 * D:3 * D]).then_inc(ld, 16)
                g.dma_start(out=wlin_sb[:, c * D:(c + 1) * D],
                            in_=wlinT_d[c * 128:(c + 1) * 128, :]).then_inc(ld, 16)
                g.dma_start(out=gwT_sb[:, c * 8:(c + 1) * 8],
                            in_=gwT_d[c * 128:(c + 1) * 128, :]).then_inc(ld, 16)
                g.dma_start(out=leftf_sb[:, c * 32:(c + 1) * 32],
                            in_=leftf_d[c * 128:(c + 1) * 128, :]).then_inc(ld, 16)
            g.dma_start(out=rightp_sb[:, :], in_=rightp_d).then_inc(ld, 16)
            g.memset(t1pad[:, :], 0.0)
            g.memset(gpad[:, :], 0.0)
            for dc in range(16 if ph >= 2 else 0):
                if dc >= 2:
                    g.wait_ge(sm, P2A_SM(dc - 1) + 4)
                g.dma_start(
                    out=wq_st[:, (dc % 2) * 1024:(dc % 2 + 1) * 1024].rearrange(
                        "p (c m) -> p c m", c=8),
                    in_=wqkv_d[:, dc * 128:(dc + 1) * 128].rearrange(
                        "(c p) m -> p c m", p=128)).then_inc(ld, 16)

        @block.vector
        def _(v):
            for t in range(NT):
                b = 8 * t
                xt = xst[:, (t % 2) * D:(t % 2 + 1) * D]
                v.wait_ge(lx, LX_X1(t))
                if t >= 2:
                    v.wait_ge(sm, 8 * (t - 2) + 8)
                v.tensor_scalar(xn_tm[:, (t % 2) * D:(t % 2 + 1) * D], xt, 1.0 / D,
                                0.0, op0=ALU.mult, op1=ALU.add,
                                accum_out=st1[:, 1:2]).then_inc(sv, 1)
                v.wait_ge(sa, 3 * t + 1)
                v.wait_ge(sv, b + 1)
                v.scalar_tensor_tensor(st1[:, 3:4], st1[:, 1:2], -1.0, st1[:, 1:2],
                                       op0=ALU.mult, op1=ALU.mult).then_inc(sv, 1)
                v.wait_ge(sv, b + 2)
                if t >= 1:
                    v.wait_ge(sa, 3 * (t - 1) + 2)
                v.tensor_scalar(st1[:, 5:6], st1[:, 2:3], 1.0 / D, st1[:, 3:4],
                                op0=ALU.mult, op1=ALU.add).then_inc(sv, 1)
                v.wait_ge(sa, 3 * t + 3)
                v.tensor_scalar(xn_tm[:, (t % 2) * D:(t % 2 + 1) * D], xt,
                                st1[:, 1:2], st1[:, 6:7],
                                op0=ALU.subtract, op1=ALU.mult).then_inc(sv, 1)
                xnv = xn_sb[:, :].rearrange("p (c tok) -> p c tok", c=8)
                for j in range(4):
                    v.wait_ge(sm, 8 * t + 2 * (j + 1))
                    v.tensor_copy(xnv[:, 2 * j:2 * j + 2, t * 128:(t + 1) * 128],
                                  ps_tr[:, 0:256].rearrange("p (c tok) -> p c tok", c=2)
                                  ).then_inc(sv, 1)

        @block.scalar
        def _(sc):
            for t in range(NT):
                xt = xst[:, (t % 2) * D:(t % 2 + 1) * D]
                sc.wait_ge(lx, LX_X1(t))
                if t >= 1:
                    sc.wait_ge(sv, 8 * (t - 1) + 3)
                sc.activation(scr[:, :], xt, AF.Square,
                              accum_out=st1[:, 2:3]).then_inc(sa, 1)
                sc.wait_ge(sv, 8 * t + 3)
                sc.activation(st1[:, 7:8], st1[:, 5:6], AF.Ln,
                              bias=eps_sb[:, 0:1], scale=1.0).then_inc(sa, 1)
                sc.wait_ge(sa, 3 * t + 2)
                if t >= 1:
                    sc.wait_ge(sv, 8 * (t - 1) + 4)
                sc.activation(st1[:, 6:7], st1[:, 7:8], AF.Exp,
                              bias=0.0, scale=-0.5).then_inc(sa, 1)

        @block.tensor
        def _(t_):
            t_.wait_ge(ld, 16 * 5)
            for t in range(NT):
                for j in range(4):
                    t_.wait_ge(sv, 8 * t + 4)
                    prev = 4 * t + j - 1
                    if prev >= 0:
                        pt, pj = prev // 4, prev % 4
                        t_.wait_ge(sv, 8 * pt + 4 + pj + 1)
                    for h in range(2):
                        c = 2 * j + h
                        t_.transpose(ps_tr[:, h * 128:(h + 1) * 128],
                                     xn_tm[:, (t % 2) * D + c * 128:(t % 2) * D + (c + 1) * 128],
                                     ident_sb[:, :]).then_inc(sm, 1)

        @block.tensor
        def _(t_):
            for dc in range(16 if ph >= 2 else 0):
                t_.wait_ge(ld, LDW(dc))
                for qt in range(NQT):
                    idx = 4 * dc + qt
                    if idx >= 4:
                        pdc, pqt = (idx - 4) // 4, (idx - 4) % 4
                        t_.wait_ge(sv, P2A_SVD(pdc) + pqt + 1)
                    ins = None
                    for c in range(8):
                        ins = t_.matmul(
                            ps_big[:, qt * 512:(qt + 1) * 512],
                            wq_st[:, (dc % 2) * 1024 + c * 128:(dc % 2) * 1024 + (c + 1) * 128],
                            xn_sb[:, c * T + qt * 512:c * T + (qt + 1) * 512],
                            start=(c == 0), stop=(c == 7), skip_group_check=True)
                    ins.then_inc(sm, 1)
                for qt in range(NQT):
                    t_.wait_ge(sv, P2A_SVD(dc) + 5 + 2 * qt)
                    t_.matmul(ps_sm[0:4, 0:512], e4_sb[:, :], sqs[:, :],
                              start=True, stop=True,
                              skip_group_check=True).then_inc(sm, 1)

        @block.vector
        def _(v):
            for dc in range(16 if ph >= 2 else 0):
                dst = q_sb if dc < 8 else k_sb
                cc = dc % 8
                base = P2A_SVD(dc)
                for qt in range(NQT):
                    v.wait_ge(sm, P2A_SM(dc) + qt + 1)
                    v.tensor_copy(dst[:, cc * T + qt * 512:cc * T + (qt + 1) * 512],
                                  ps_big[:, qt * 512:(qt + 1) * 512]).then_inc(sv, 1)
                for qt in range(NQT):
                    v.wait_ge(sv, base + 4)
                    v.wait_ge(sm, P2A_SM(dc) + 4 + qt)
                    src = dst[:, cc * T + qt * 512:cc * T + (qt + 1) * 512]
                    v.tensor_tensor(sqs[:, :], src, src, op=ALU.mult).then_inc(sv, 1)
                    v.wait_ge(sm, P2A_SM(dc) + 4 + qt + 1)
                    v.reduce_max(bnd[0:4, dc * 4 + qt:dc * 4 + qt + 1],
                                 ps_sm[0:4, 0:512], axis=AX.X).then_inc(sv, 1)

        @block.gpsimd
        def _(g):
            if ph < 2:
                return
            g.wait_ge(sv, SV_2A)
            g.dma_start(out=bnd_d, in_=bnd[0:4, 0:64]).then_inc(ld, 16)
            g.wait_ge(ld, LD_BND1)
            g.dma_start(out=mqk[:, 0:128].rearrange("a (p c) -> a p c", p=4),
                        in_=bnd_d[:, 0:32]).then_inc(ld, 16)
            g.dma_start(out=mqk[:, 128:256].rearrange("a (p c) -> a p c", p=4),
                        in_=bnd_d[:, 32:64]).then_inc(ld, 16)

        @block.vector
        def _(v):
            if ph < 2:
                return
            v.wait_ge(ld, LD_BND)
            v.reduce_max(st1[0:1, 8:9], mqk[:, 0:128], axis=AX.X)
            v.reduce_max(st1[0:1, 9:10], mqk[:, 128:256], axis=AX.X).then_inc(sv, 1)
            v.wait_ge(sv, SV_2A + 1)
            v.tensor_tensor(st1[0:1, 10:11], st1[0:1, 8:9], st1[0:1, 9:10],
                            op=ALU.mult).then_inc(sv, 1)
            v.wait_ge(sa, SA_BF)
            v.tensor_scalar(st1[0:1, 12:13], st1[0:1, 11:12], -SCALE * 1.02, None,
                            op0=ALU.mult).then_inc(sv, 1)
            v.wait_ge(sv, SV_2A + 3)
            v.tensor_scalar(mqk[:, 0:128], ones1_sb[0:1, :], st1[0:1, 12:13], None,
                            op0=ALU.mult).then_inc(sv, 1)

        @block.scalar
        def _(sc):
            if ph < 2:
                return
            sc.wait_ge(sv, SV_2A + 2)
            sc.activation(st1[0:1, 13:14], st1[0:1, 10:11], AF.Ln,
                          bias=0.0, scale=1.0).then_inc(sa, 1)
            sc.wait_ge(sa, P1_SA + 1)
            sc.activation(st1[0:1, 11:12], st1[0:1, 13:14], AF.Exp,
                          bias=0.0, scale=0.5).then_inc(sa, 1)

        @block.gpsimd
        def _(g):
            if ph < 2:
                return
            g.wait_ge(sv, SV_BF)
            g.dma_start(out=nb_d[:, :].rearrange("p a -> (p a)").rearrange(
                "(a q) -> a q", a=1), in_=mqk[:, 0:128]).then_inc(ld, 16)
            g.wait_ge(ld, LD_NB1)
            g.dma_start(out=biasB_sb[:, :], in_=nb_d).then_inc(ld, 16)

        @block.tensor
        def _(t_):
            for t in range(NT if ph >= 3 else 0):
                if t == 0:
                    t_.wait_ge(sv, SV_2A)
                if t >= 2:
                    t_.wait_ge(sv, P2B_SV(t - 2) + 1)
                if t >= 1:
                    t_.wait_ge(sv, P2B_SV(t - 1) + 7)
                    t_.wait_ge(sa, P2B_SA(t - 1) + 1)
                vb = (t % 2) * 1024
                ins = None
                for c in range(8):
                    xnc = xn_sb[:, c * T + t * 128:c * T + (t + 1) * 128]
                    t_.matmul(ps_big[:, vb:vb + 512], xnc, wv_sb[:, c * D:c * D + 512],
                              start=(c == 0), stop=(c == 7), skip_group_check=True)
                    t_.matmul(ps_big[:, vb + 512:vb + 1024], xnc, wv_sb[:, c * D + 512:(c + 1) * D],
                              start=(c == 0), stop=(c == 7), skip_group_check=True)
                    ins = t_.matmul(ps_sm[:, 0:8], xnc, gwT_sb[:, c * 8:(c + 1) * 8],
                                    start=(c == 0), stop=(c == 7), skip_group_check=True)
                ins.then_inc(sm, 1)
                t_.wait_ge(sv, P2B_SV(t) + 11)
                if t >= 1:
                    t_.wait_ge(sv, P2B_SV(t - 1) + 12)
                t_.transpose(ps_tr[:, 128:256], gpad[:, :], ident_sb[:, :]).then_inc(sm, 1)

        @block.scalar
        def _(sc):
            for t in range(NT if ph >= 3 else 0):
                sc.wait_ge(sv, P2B_SV(t) + 2)
                sc.activation(scr2[:, 0:8], ps_sm[:, 0:8], AF.Exp, bias=st1[:, 8:9],
                              accum_out=st1[:, 9:10]).then_inc(sa, 1)

        @block.vector
        def _(v):
            for t in range(NT if ph >= 3 else 0):
                b = P2B_SV(t)
                v.wait_ge(sm, P2B_SM(t) + 1)
                vb = (t % 2) * 1024
                v.tensor_copy(
                    v_sb[:, t * NH * 33:(t + 1) * NH * 33].rearrange(
                        "p (h w) -> p h w", h=NH)[:, :, 0:32],
                    ps_big[:, vb:vb + 1024].rearrange("p (h w) -> p h w", h=NH)).then_inc(sv, 1)
                v.reduce_max(st1[:, 8:9], ps_sm[:, 0:8], axis=AX.X,
                             negate=True).then_inc(sv, 1)
                v.wait_ge(sv, b + 2)
                v.tensor_scalar(st1[:, 10:11], st1[:, 8:9], -1.0, None,
                                op0=ALU.mult).then_inc(sv, 1)
                v.wait_ge(sv, b + 3)
                v.tensor_scalar(scr2[:, 8:16], ps_sm[:, 0:8], st1[:, 10:11], None,
                                op0=ALU.is_ge).then_inc(sv, 1)
                v.wait_ge(sv, b + 4)
                v.scalar_tensor_tensor(scr2[:, 16:24], scr2[:, 8:16], -BIG,
                                       ps_sm[:, 0:8], op0=ALU.mult,
                                       op1=ALU.add).then_inc(sv, 1)
                v.wait_ge(sv, b + 5)
                v.reduce_max(st1[:, 11:12], scr2[:, 16:24], axis=AX.X).then_inc(sv, 1)
                v.wait_ge(sv, b + 6)
                v.tensor_scalar(scr2[:, 24:32], ps_sm[:, 0:8], st1[:, 11:12], None,
                                op0=ALU.is_ge).then_inc(sv, 1)
                v.wait_ge(sa, P2B_SA(t) + 1)
                with nc.allow_low_precision(reason="tiny gate sums"):
                    v.reciprocal(st1[:, 12:13], st1[:, 9:10]).then_inc(sv, 1)
                v.wait_ge(sv, b + 8)
                v.scalar_tensor_tensor(gates_bf[:, t * 8:(t + 1) * 8], scr2[:, 0:8],
                                       1.0, scr2[:, 24:32], op0=ALU.mult,
                                       op1=ALU.mult).then_inc(sv, 1)
                v.wait_ge(sv, b + 9)
                v.tensor_scalar(gates_bf[:, t * 8:(t + 1) * 8],
                                gates_bf[:, t * 8:(t + 1) * 8], st1[:, 12:13], None,
                                op0=ALU.mult).then_inc(sv, 1)
                v.wait_ge(sv, b + 10)
                if t >= 1:
                    v.wait_ge(sm, P2B_SM(t - 1) + 2)
                v.tensor_copy(gpad[:, 0:8],
                              gates_bf[:, t * 8:(t + 1) * 8]).then_inc(sv, 1)
                v.wait_ge(sm, P2B_SM(t) + 2)
                v.tensor_copy(gfm_sb[:, t * 128:(t + 1) * 128],
                              ps_tr[:, 128:256]).then_inc(sv, 1)

        @block.tensor
        def _(t_):
            for qt in range(NQT if ph >= 4 else 0):
                if qt == 0:
                    t_.wait_ge(sv, SV_2B)
                    t_.wait_ge(sa, SA_2B)
                else:
                    t_.wait_ge(sv, P2C_SV(qt - 1) + 1)
                ins = None
                for c in range(8):
                    ins = t_.matmul(ps_sm[0:32, :], leftf_sb[:, c * 32:(c + 1) * 32],
                                    xn_sb[:, c * T + qt * 512:c * T + (qt + 1) * 512],
                                    start=(c == 0), stop=(c == 7), skip_group_check=True)
                ins.then_inc(sm, 1)
                if qt >= 1:
                    t_.wait_ge(sv, P2C_SV(qt - 1) + 1)
                t_.matmul(ps_two[:, 0:512], g8_sb[:, :],
                          gfm_sb[:, qt * 512:(qt + 1) * 512],
                          start=True, stop=True, skip_group_check=True).then_inc(sm, 1)

        @block.vector
        def _(v):
            for qt in range(NQT if ph >= 4 else 0):
                b = P2C_SV(qt)
                v.wait_ge(sm, P2C_SM(qt) + 1)
                v.tensor_copy(t_tmp[0:32, :], ps_sm[0:32, :]).then_inc(sv, 1)
                v.wait_ge(sv, b + 1)
                v.wait_ge(sm, P2C_SM(qt) + 2)
                v.tensor_tensor(t1pad[0:32, qt * 512:(qt + 1) * 512],
                                ps_two[0:32, 0:512], t_tmp[0:32, :],
                                op=ALU.mult).then_inc(sv, 1)

        @block.tensor
        def _(t_):
            for t in range(NT if ph >= 4 else 0):
                if t <= 1:
                    t_.wait_ge(sv, SV_2B)
                t_.wait_ge(sv, P2C_SV(t // 4) + 2)
                if t >= 2:
                    t_.wait_ge(sv, P2D_SV(t - 2) + 2)
                for half in range(2):
                    hb = (t % 2) * 1024
                    ins = None
                    for c in range(8):
                        ins = t_.matmul(
                            ps_big[:, hb + half * 512:hb + (half + 1) * 512],
                            xn_sb[:, c * T + t * 128:c * T + (t + 1) * 128],
                            wlin_sb[:, c * D + half * 512:c * D + (half + 1) * 512],
                            start=(c == 0), stop=False, skip_group_check=True)
                    ins = t_.matmul(
                        ps_big[:, hb + half * 512:hb + (half + 1) * 512],
                        t1pad[:, t * 128:(t + 1) * 128],
                        rightp_sb[:, half * 512:(half + 1) * 512],
                        start=False, stop=True, skip_group_check=True)
                    ins.then_inc(sm, 1)

        @block.vector
        def _(v):
            for t in range(NT if ph >= 4 else 0):
                b = P2D_SV(t)
                hb = (t % 2) * 1024
                for half in range(2):
                    v.wait_ge(sm, P2D_SM(t) + half + 1)
                    if t >= 2:
                        v.wait_ge(lx, LX_PST(t - 2))
                    v.tensor_tensor(prest[:, (t % 2) * D + half * 512:(t % 2) * D + (half + 1) * 512],
                                    ps_big[:, hb + half * 512:hb + (half + 1) * 512],
                                    blin_sb[:, half * 512:(half + 1) * 512],
                                    op=ALU.add).then_inc(sv, 1)

        @block.sync
        def _(s):
            for t in range(NT if ph >= 4 else 0):
                s.wait_ge(sv, P2D_SV(t) + 2)
                s.dma_start(out=pre_d[t * 128:(t + 1) * 128, :],
                            in_=prest[:, (t % 2) * D:(t % 2 + 1) * D]).then_inc(lx, 16)

        ctx2.close()
        ctx3 = ctx.enter_context(contextlib.ExitStack())
        sb3 = lambda name, shape, dtype: ctx3.enter_context(nc.sbuf_tensor(name, shape, dtype))
        wout_sb = sb3("wout_sb", [128, NPAIR * D], BF16)
        attn = sb3("attn", [128, NPAIR * 512], BF16)
        expS = sb3("expS", [128, 4 * 1024], BF16)
        rpk = sb3("rpk", [32, 512], BF16)
        gst = sb3("gst", [128, 2 * D], BF16)
        spre = sb3("spre", [128, 4 * D], F32)
        bout_sb = sb3("bout_sb", [128, D], F32)
        gam_sb = sb3("gam_sb", [128, D], F32)
        bet_sb = blin_sb

        @block.gpsimd
        def _(g):
            if ph >= 5:
                g.wait_ge(sm, SM_2D)
                g.wait_ge(sv, SV_2D)
            for p in range(NPAIR if ph >= 5 else 0):
                g.dma_start(out=wout_sb[:, p * D:(p + 1) * D],
                            in_=wout_d[p * 128:(p + 1) * 128, :]).then_inc(ld, 16)

        @block.sync
        def _(s):
            for dst, src in [(bout_sb[:, :], bout_d), (gam_sb[:, :], gam_d),
                             (bet_sb[:, :], bet_d)]:
                s.dma_start(out=dst, in_=src).then_inc(lx, 16) if False else None
            if ph >= 5.5:
                order = [("r", 0), ("r", 1), ("r", 2)]
                for k in range(NT - 3):
                    order += [("s", k), ("r", k + 3)]
                order += [("s", NT - 3), ("s", NT - 2), ("s", NT - 1)]
                for kind, t in order:
                    if kind == "r":
                        s.wait_ge(lx, LX_PST(NT - 1))
                        if t >= 2:
                            s.wait_ge(sa, SA_SIL(t - 2))
                        s.dma_start(out=prest[:, (t % 2) * D:(t % 2 + 1) * D],
                                    in_=pre_d[t * 128:(t + 1) * 128, :]).then_inc(lx, 16)
                    else:
                        s.wait_ge(sa, SA_SIL(t))
                        s.dma_start(out=g_d[t * 128:(t + 1) * 128, :],
                                    in_=gst[:, (t % 2) * D:(t % 2 + 1) * D]).then_inc(lx, 16)
            elif ph == 5:
                for t in range(NT):
                    s.wait_ge(lx, LX_PST(NT - 1))
                    s.dma_start(out=prest[:, (t % 2) * D:(t % 2 + 1) * D],
                                in_=pre_d[t * 128:(t + 1) * 128, :]).then_inc(lx, 16)

        @block.gpsimd
        def _(g):
            for dst, src in ([(bout_sb[:, :], bout_d), (gam_sb[:, :], gam_d),
                             (bet_sb[:, :], bet_d)] if ph >= 5 else []):
                g.dma_start(out=dst, in_=src).then_inc(ld, 16)
        LD_TOT2 = LD_TOT + 16 * 3

        @block.scalar
        def _(sc):
            for t in range(NT if ph >= 5.5 else 0):
                sc.wait_ge(lx, LX_PLD(t))
                if t >= 2:
                    sc.wait_ge(lx, LX_GST(t - 2))
                sc.activation(gst[:, (t % 2) * D:(t % 2 + 1) * D],
                              prest[:, (t % 2) * D:(t % 2 + 1) * D],
                              AF.Silu).then_inc(sa, 1)

        @block.tensor
        def _(t_):
            if ph >= 6:
                t_.wait_ge(ld, LD_TOT2)
            for qt in range(NQT if ph >= 6 else 0):
                scb, avb, xpb, evb = QT_SC(qt), QT_AV(qt), QT_XP(qt), QT_EV(qt)
                if qt > 0:
                    t_.wait_ge(sv, P3_SV(qt - 1) + 25)
                NI = NPAIR * KT

                def emit_scores(i):
                    p, kt = i // KT, i % KT
                    cp, jp = p // 2, p % 2
                    sA, sB = 64 * jp, 64 * jp + 32
                    base = 256 * qt + 16 * p
                    par = kt % 2
                    if qt > 0 and p == 0 and kt == 1:
                        t_.wait_ge(sv, P3_SV(qt - 1) + 25)
                    t_.wait_ge(s_xp, max(0, base + kt - 1))
                    t_.matmul(ps_big[:, par * 1024:par * 1024 + 512],
                              k_sb[sA:sA + 32, cp * T + kt * 128:cp * T + (kt + 1) * 128],
                              q_sb[sA:sA + 32, cp * T + qt * 512:cp * T + (qt + 1) * 512],
                              start=True, stop=True, tile_position=(sA, 0),
                              skip_group_check=True)
                    t_.matmul(ps_big[:, par * 1024 + 512:par * 1024 + 1024],
                              k_sb[sB:sB + 32, cp * T + kt * 128:cp * T + (kt + 1) * 128],
                              q_sb[sB:sB + 32, cp * T + qt * 512:cp * T + (qt + 1) * 512],
                              start=True, stop=True, tile_position=(sB, 0),
                              skip_group_check=True).then_inc(s_sc, 1)

                def emit_av(j):
                    p, kt = j // KT, j % KT
                    cp, jp = p // 2, p % 2
                    hA, hB = 4 * cp + 2 * jp, 4 * cp + 2 * jp + 1
                    base = 256 * qt + 16 * p
                    slot = (kt % 4) * 1024
                    ub = (p % 2) * 512
                    t_.wait_ge(s_xp, base + kt + 1)
                    if p >= 2 and kt == 0:
                        t_.wait_ge(s_ev, evb + p - 1)
                    if qt > 0 and p < 2 and kt == 0:
                        t_.wait_ge(s_ev, evb)
                    t_.matmul(ps_two[0:33, ub:ub + 512],
                              v_sb[:, kt * NH * 33 + hA * 33:kt * NH * 33 + hA * 33 + 33],
                              expS[:, slot:slot + 512],
                              start=(kt == 0), stop=(kt == KT - 1),
                              tile_position=(0, 0), skip_group_check=True)
                    t_.matmul(ps_two[64:97, ub:ub + 512],
                              v_sb[:, kt * NH * 33 + hB * 33:kt * NH * 33 + hB * 33 + 33],
                              expS[:, slot + 512:slot + 1024],
                              start=(kt == 0), stop=(kt == KT - 1),
                              tile_position=(0, 64),
                              skip_group_check=True).then_inc(s_av, 1)

                for g in range(NI // 2 + 1):
                    if 2 * g < NI:
                        emit_scores(2 * g)
                        emit_scores(2 * g + 1)
                    if g >= 1:
                        emit_av(2 * g - 2)
                        emit_av(2 * g - 1)
                for p in range(NPAIR):
                    t_.wait_ge(s_ev, evb + p + 1)
                    ins = t_.matmul(ps_sm[0:32, :], sel_sb[:, p * 32:(p + 1) * 32],
                                    attn[:, p * 512:(p + 1) * 512],
                                    start=(p == 0), stop=(p == NPAIR - 1),
                                    skip_group_check=True)
                ins.then_inc(sm, 1)
                for p in range(NPAIR):
                    t_.wait_ge(sv, max(P3_SV(qt) + 1, P3_SV(qt) + p))
                    t_.matmul(ps_big[:, (p % 2) * 512:(p % 2) * 512 + 512],
                              ep_sb[0:32, p * 128:(p + 1) * 128],
                              rpk[0:32, :], start=True, stop=True,
                              skip_group_check=True).then_inc(sm, 1)
                for ts in range(4):
                    for half in range(2):
                        if ts >= 1 or half >= 1:
                            pidx = 2 * ts + half - 2
                            if pidx >= 0:
                                t_.wait_ge(sv, P3_SV(qt) + 17 + pidx + 1)
                        ins = None
                        for p in range(NPAIR):
                            t_.wait_ge(sv, P3_SV(qt) + p + 2)
                            ins = t_.matmul(
                                ps_big[:, 1024 + half * 512:1024 + (half + 1) * 512],
                                attn[:, p * 512 + ts * 128:p * 512 + (ts + 1) * 128],
                                wout_sb[:, p * D + half * 512:p * D + (half + 1) * 512],
                                start=(p == 0), stop=(p == NPAIR - 1),
                                skip_group_check=True)
                        ins.then_inc(sm, 1)

        @block.scalar
        def _(sc):
            if ph >= 6:
                sc.wait_ge(ld, LD_NB)

            def sq_act(sc, qt, ts):
                o8 = ts * 8
                sc.wait_ge(sv, P3_SV(qt) + 17 + 2 * ts + 2)
                sc.activation(expS[:, 0:1024], spre[:, ts * D:(ts + 1) * D],
                              AF.Square, accum_out=st1[:, o8 + 2:o8 + 3]).then_inc(sa, 1)

            def lnr_acts(sc, qt, ts):
                o8 = ts * 8
                CB = P3_SV(qt) + 25 + 8 * ts
                sc.wait_ge(sv, CB + 3)
                sc.activation(st1[:, o8 + 7:o8 + 8], st1[:, o8 + 5:o8 + 6], AF.Ln,
                              bias=eps_sb[:, 1:2], scale=1.0).then_inc(sa, 1)
                sc.wait_ge(sa, P3_SA(qt) + 4 + 2 * ts + 1)
                sc.activation(st1[:, o8 + 6:o8 + 7], st1[:, o8 + 7:o8 + 8], AF.Exp,
                              bias=0.0, scale=-0.5).then_inc(sa, 1)

            for qt in range(NQT if ph >= 6 else 0):
                for p in range(NPAIR):
                    base = 256 * qt + 16 * p
                    for kt in range(KT):
                        par = kt % 2
                        slot = (kt % 4) * 1024
                        sc.wait_ge(s_sc, base + kt + 1)
                        sc.wait_ge(s_av, max(0, base + kt - 3))
                        sc.activation(expS[:, slot:slot + 1024],
                                      ps_big[:, par * 1024:par * 1024 + 1024],
                                      AF.Exp, bias=biasB_sb[:, :],
                                      scale=SCALE).then_inc(s_xp, 1)
                        if qt > 0 and p % 4 == 0 and p > 0 and kt == 4:
                            lnr_acts(sc, qt - 1, p // 4 - 1)
                if qt > 0:
                    lnr_acts(sc, qt - 1, 3)
                for ts in range(4):
                    sq_act(sc, qt, ts)
            if ph >= 6:
                for ts in range(4):
                    lnr_acts(sc, NQT - 1, ts)

        @block.vector
        def _(v):
            def chain(v, qt, ts):
                CB = P3_SV(qt) + 25 + 8 * ts
                sl = ts % 2
                o8 = ts * 8
                spt = spre[:, ts * D:(ts + 1) * D]
                v.wait_ge(sa, P3_SA(qt) + ts + 1)
                v.tensor_scalar(scr2[:, :], spt, 1.0 / D, 0.0, op0=ALU.mult,
                                op1=ALU.add, accum_out=st1[:, o8 + 1:o8 + 2]).then_inc(sv, 1)
                v.wait_ge(sv, CB + 1)
                v.scalar_tensor_tensor(st1[:, o8 + 3:o8 + 4], st1[:, o8 + 1:o8 + 2],
                                       -1.0, st1[:, o8 + 1:o8 + 2],
                                       op0=ALU.mult, op1=ALU.mult).then_inc(sv, 1)
                v.wait_ge(sv, CB + 2)
                v.tensor_scalar(st1[:, o8 + 5:o8 + 6], st1[:, o8 + 2:o8 + 3], 1.0 / D,
                                st1[:, o8 + 3:o8 + 4],
                                op0=ALU.mult, op1=ALU.add).then_inc(sv, 1)
                v.wait_ge(sa, P3_SA(qt) + 4 + 2 * ts + 2)
                v.tensor_scalar(scr2[:, :], spt, st1[:, o8 + 1:o8 + 2],
                                st1[:, o8 + 6:o8 + 7],
                                op0=ALU.subtract, op1=ALU.mult).then_inc(sv, 1)
                v.wait_ge(sv, CB + 4)
                v.tensor_tensor(scr2[:, :], scr2[:, :], gam_sb[:, :],
                                op=ALU.mult).then_inc(sv, 1)
                v.wait_ge(sv, CB + 5)
                v.tensor_tensor(scr2[:, :], scr2[:, :], bet_sb[:, :],
                                op=ALU.add).then_inc(sv, 1)
                v.wait_ge(sv, CB + 6)
                v.wait_ge(lx, LX_EGG(qt, ts))
                v.tensor_tensor(scr2[:, :], scr2[:, :], gst[:, sl * D:(sl + 1) * D],
                                op=ALU.mult).then_inc(sv, 1)
                v.wait_ge(sv, CB + 7)
                v.wait_ge(lx, LX_EXG(qt, ts))
                v.tensor_tensor(spre[:, ts * D:(ts + 1) * D], scr2[:, :],
                                xst[:, sl * D:(sl + 1) * D],
                                op=ALU.add).then_inc(sv, 1)

            for qt in range(NQT if ph >= 6 else 0):
                bq = P3_SV(qt)
                evb = QT_EV(qt)
                for p in range(NPAIR):
                    v.wait_ge(s_av, 256 * qt + 16 * p + 16)
                    v.tensor_copy(attn[:, p * 512:(p + 1) * 512],
                                  ps_two[:, (p % 2) * 512:(p % 2) * 512 + 512]).then_inc(s_ev, 1)
                    if qt > 0 and p % 4 == 3:
                        chain(v, qt - 1, p // 4)
                v.wait_ge(sm, P3_SM(qt) + 1)
                with nc.allow_low_precision(reason="1/denom bf16: uniform per-(h,q) scale"):
                    v.reciprocal(rpk[0:32, :], ps_sm[0:32, :]).then_inc(sv, 1)
                for p in range(NPAIR):
                    sl2 = (p % 2) * 512
                    v.wait_ge(sm, P3_SM(qt) + 2 + p)
                    v.tensor_tensor(attn[:, p * 512:(p + 1) * 512],
                                    attn[:, p * 512:(p + 1) * 512],
                                    ps_big[:, sl2:sl2 + 512],
                                    op=ALU.mult).then_inc(sv, 1)
                for ts in range(4):
                    v.wait_ge(sm, P3_SM(qt) + 17 + 2 * ts + 1)
                    if qt > 0:
                        v.wait_ge(lx, LX_EOUT(qt - 1, ts))
                        v.wait_ge(sv, P3_SV(qt - 1) + 25 + 8 * ts + 8)
                        v.wait_ge(sa, P3_SA(qt - 1) + ts + 1)
                    for half in range(2):
                        if half == 1:
                            v.wait_ge(sm, P3_SM(qt) + 17 + 2 * ts + 2)
                        v.tensor_tensor(spre[:, ts * D + half * 512:ts * D + (half + 1) * 512],
                                        ps_big[:, 1024 + half * 512:1024 + (half + 1) * 512],
                                        bout_sb[:, half * 512:(half + 1) * 512],
                                        op=ALU.add).then_inc(sv, 1)
            if ph >= 6:
                for ts in range(4):
                    chain(v, NQT - 1, ts)

        @block.sync
        def _(s):
            for qt in range(NQT if ph >= 6 else 0):
                for ts in range(4):
                    tok0 = qt * 512 + ts * 128
                    sl = ts % 2
                    pts = 4 * qt + ts - 2
                    if pts >= 0:
                        s.wait_ge(sv, P3_SV(pts // 4) + 25 + 8 * (pts % 4) + 8)
                    s.dma_start(out=xst[:, sl * D:(sl + 1) * D],
                                in_=x_d[tok0:tok0 + 128, :]).then_inc(lx, 16)
                    s.dma_start(out=gst[:, sl * D:(sl + 1) * D],
                                in_=g_d[tok0:tok0 + 128, :]).then_inc(lx, 16)
                for ts in range(4):
                    s.wait_ge(sv, P3_SV(qt) + 25 + 8 * ts + 8)
                    s.dma_start(out=out_d[qt * 512 + ts * 128:qt * 512 + (ts + 1) * 128, :],
                                in_=spre[:, ts * D:(ts + 1) * D]).then_inc(lx, 16)
            if ph >= 6:
                s.wait_ge(lx, LX_TOT)
            else:
                tgt = {1: ("sv", P1_SV), 2: ("ld", LD_NB), 3: ("sv", SV_2B),
                       4: ("lx", LX2), 5: ("lx", LX3), 5.5: ("lx", LX4)}[ph]
                s.wait_ge({"sv": sv, "ld": ld, "lx": lx}[tgt[0]], tgt[1])
                s.dma_start(out=out_d[0:128, :], in_=xst[:, 0:D]).then_inc(lx, 16)
                s.wait_ge(lx, (LX1 if ph <= 3 else LX2 if ph == 4 else
                               LX3 if ph == 5 else LX4) + 16)
    return nc


def _host_consts():
    ident = np.eye(128, dtype=np.float32)
    e4 = np.zeros((128, 4), np.float32)
    for j in range(4):
        e4[32 * j:32 * (j + 1), j] = 1.0
    g8 = np.zeros((128, 128), np.float32)
    for e in range(8):
        for r in range(4):
            g8[e, 4 * e + r] = 1.0
    sel = np.zeros((128, NPAIR * 32), np.float32)
    for p in range(NPAIR):
        sel[32, p * 32 + 2 * p] = 1.0
        sel[96, p * 32 + 2 * p + 1] = 1.0
    ep = np.zeros((32, NPAIR * 128), np.float32)
    for p in range(NPAIR):
        ep[2 * p, p * 128 + 0:p * 128 + 32] = 1.0
        ep[2 * p + 1, p * 128 + 64:p * 128 + 96] = 1.0
    ones1 = np.ones((1, 128), np.float32)
    return ident, e4, g8, sel, ep, ones1


def kernel(L0_attn_x, L0_attn_w_qkv, L0_attn_w_out, L0_attn_b_out, L0_attn_ln_gamma,
           L0_attn_ln_beta, L0_ffn_W_lin, L0_ffn_b_lin, L0_ffn_gate_w, L0_ffn_left,
           L0_ffn_right):
    x = np.ascontiguousarray(np.asarray(L0_attn_x, dtype=np.float32))
    wqkv = np.ascontiguousarray(np.asarray(L0_attn_w_qkv, dtype=np.float32))
    wout = np.asarray(L0_attn_w_out, dtype=np.float32)
    bout = np.asarray(L0_attn_b_out, dtype=np.float32)
    gam = np.asarray(L0_attn_ln_gamma, dtype=np.float32)
    bet = np.asarray(L0_attn_ln_beta, dtype=np.float32)
    wlin = np.asarray(L0_ffn_W_lin, dtype=np.float32)
    blin = np.asarray(L0_ffn_b_lin, dtype=np.float32)
    gw = np.asarray(L0_ffn_gate_w, dtype=np.float32)
    left = np.asarray(L0_ffn_left, dtype=np.float32)
    right = np.asarray(L0_ffn_right, dtype=np.float32)

    wlinT = np.ascontiguousarray(wlin.T)
    gwT = np.ascontiguousarray(gw.T)
    leftf = np.ascontiguousarray(left.transpose(1, 0, 2).reshape(D, 32))
    rightp = np.zeros((128, D), np.float32)
    rightp[0:32] = right.reshape(32, D)
    wout_h = np.zeros((NPAIR * 128, D), np.float32)
    for p in range(NPAIR):
        wout_h[128 * p + 0:128 * p + 32] = wout[32 * (2 * p):32 * (2 * p) + 32]
        wout_h[128 * p + 64:128 * p + 96] = wout[32 * (2 * p + 1):32 * (2 * p + 1) + 32]
    rep = lambda a: np.ascontiguousarray(np.broadcast_to(a.reshape(1, D), (128, D)))
    ident, e4, g8, sel, ep, ones1 = _host_consts()

    nc = build_nc()
    eps_rep = np.ascontiguousarray(np.broadcast_to(
        np.array([EPS1, EPS2], np.float32).reshape(1, 2), (128, 2)))
    shared = dict(wqkv=wqkv, wlinT=wlinT, wout_h=wout_h, gwT=gwT, leftf=leftf,
                  rightp=rightp, blin_rep=rep(blin), bout_rep=rep(bout),
                  gam_rep=rep(gam), bet_rep=rep(bet), ident=ident, e4=e4,
                  g8pad=g8, sel=sel, epair=ep, ones1=ones1, eps_rep=eps_rep)
    in_maps = [dict(shared, x=np.ascontiguousarray(x[b])) for b in range(NCORES)]
    res = run_bass_kernel_spmd(nc, in_maps, core_ids=list(range(NCORES)))
    out = np.stack([res.results[b]["out"] for b in range(NCORES)], axis=0)
    return out.astype(np.float32)
